# version 41
# baseline (speedup 1.0000x reference)
import numpy as np
from contextlib import ExitStack

import concourse.bass as bass
import concourse.mybir as mybir
from concourse.bass_utils import run_bass_kernel_spmd

F32 = mybir.dt.float32
BF16 = mybir.dt.bfloat16
AF = mybir.ActivationFunctionType
ALU = mybir.AluOpType
AX = mybir.AxisListType

NEG = -3.0e38


class Rec:

    def __init__(self, cnt=None):
        self.ops = []
        self.cnt = dict(cnt or {})

    def E(self, eng, fn, wait=(), inc=()):
        w = tuple((s, v) for (s, v) in wait if v > 0)
        vals = []
        for s, a in inc:
            self.cnt[s] = self.cnt.get(s, 0) + a
            vals.append((s, a))
        self.ops.append((eng, w, fn, tuple(vals)))

    def v(self, s):
        return self.cnt.get(s, 0)


def replay(nc, rec, sems):
    engs = [e for e in ("sync", "gpsimd", "tensor", "vector", "scalar")
            if any(op[0] == e for op in rec.ops)]
    with nc.Block() as block:
        def mk(ename):
            ops = [op for op in rec.ops if op[0] == ename]

            def emit(eng):
                for _, waits, fn, incs in ops:
                    for (s, v) in waits:
                        eng.wait_ge(sems[s], v)
                    if fn is not None:
                        inst = fn(eng)
                        for (s, a) in incs:
                            inst.then_inc(sems[s], a)
            emit.__name__ = "_" + ename
            return emit

        for e in engs:
            getattr(block, e)(mk(e))


def build_nc(NG=16, NE=8, H=4096, debug=False):
    C, D = 1024, 256
    M = 128 * NG
    KEYS = 512 * NG
    MH = M // 2
    NC8 = C // 128
    NH8 = H // 128
    NHC = H // 512
    H8G = min(8, NH8)
    NHG = NH8 // H8G
    TGW = min(512, MH)
    NTG = MH // TGW
    NTP = MH // 128

    nc = bass.Bass("TRN2", target_bir_lowering=False)

    x = nc.dram_tensor("x", [KEYS, C], F32, kind="ExternalInput")
    wq = nc.dram_tensor("wq", [C, D], F32, kind="ExternalInput")
    wk = nc.dram_tensor("wk", [C, D], F32, kind="ExternalInput")
    wvd = nc.dram_tensor("wvd", [C, D], F32, kind="ExternalInput")
    wvup = nc.dram_tensor("wvup", [D, C], F32, kind="ExternalInput")
    rwT = nc.dram_tensor("rwT", [C, NE], F32, kind="ExternalInput")
    w1 = nc.dram_tensor("w1", [NE, C, H], F32, kind="ExternalInput")
    w2 = nc.dram_tensor("w2", [NE, H, C], F32, kind="ExternalInput")
    ident_in = nc.dram_tensor("ident_in", [128, 128], F32, kind="ExternalInput")
    dmask_in = nc.dram_tensor("dmask_in", [128, 2048], F32, kind="ExternalInput")
    ltri_in = nc.dram_tensor("ltri_in", [128, 128], F32, kind="ExternalInput")
    gw_in = nc.dram_tensor("gw_in", [128, 128], F32, kind="ExternalInput")
    iota_in = nc.dram_tensor("iota_in", [128, 384], F32, kind="ExternalInput")
    tfeat_in = nc.dram_tensor("tfeat_in", [128, 2 * NG], F32, kind="ExternalInput")
    xnt_dram = nc.dram_tensor("xnt_scratch", [128, NG * C], BF16, kind="ExternalOutput")
    out = nc.dram_tensor("out", [M, C], F32, kind="ExternalOutput")
    if debug:
        d_xnT = nc.dram_tensor("d_xnT", [128, NC8 * M], F32, kind="ExternalOutput")
        d_KT = nc.dram_tensor("d_KT", [128, 2 * KEYS], F32, kind="ExternalOutput")
        d_Vd = nc.dram_tensor("d_Vd", [128, (KEYS // 128) * D], F32, kind="ExternalOutput")
        d_QT = nc.dram_tensor("d_QT", [128, 2 * M], F32, kind="ExternalOutput")
        d_P = nc.dram_tensor("d_P", [128, KEYS], F32, kind="ExternalOutput")
        d_comb = nc.dram_tensor("d_comb", [128, NG * NE], F32, kind="ExternalOutput")
        d_r = nc.dram_tensor("d_r", [128, NG], F32, kind="ExternalOutput")
        d_ctxT = nc.dram_tensor("d_ctxT", [128, 2 * M], F32, kind="ExternalOutput")
        d_y = nc.dram_tensor("d_y", [128, NTP * C], F32, kind="ExternalOutput")
        dbg_sb = None

    es0 = ExitStack()
    sems = {}
    for s in ("sxa", "sxb", "sxc", "sxd", "ss", "si", "swa", "swb", "swc", "swd",
              "swe", "swf", "swg", "swh", "sxt",
              "sxr0", "sxr1", "so0", "so1", "pe", "ve", "ac"):
        sems[s] = es0.enter_context(nc.semaphore(s))

    ctxT = es0.enter_context(nc.sbuf_tensor("ctxT", [128, 2 * M], BF16))
    comb = es0.enter_context(nc.sbuf_tensor("comb", [128, NG * NE], F32))
    mask_f32 = es0.enter_context(nc.sbuf_tensor("mask_f32", [128, NG * NE], F32))
    wvup_sb = es0.enter_context(nc.sbuf_tensor("wvup_sb", [128, 2 * C], BF16))
    rinv_all = es0.enter_context(nc.sbuf_tensor("rinv_all", [128, NG], F32))
    r_all = es0.enter_context(nc.sbuf_tensor("r_all", [128, NG], F32))
    ident = es0.enter_context(nc.sbuf_tensor("ident", [128, 128], F32))
    fcol = es0.enter_context(nc.sbuf_tensor("fcol", [128, 64], F32))
    _vfk = [0]

    def VF():
        k = _vfk[0] % 64
        _vfk[0] += 1
        rec.E("vector", (lambda k: lambda e: e.memset(fcol[:, k:k + 1], 0.0))(k))

    rec = Rec()
    E = rec.E

    esA = ExitStack()
    xnT_own = esA.enter_context(nc.sbuf_tensor("xnT_own", [128, NC8 * M], BF16))
    x_st = [esA.enter_context(nc.sbuf_tensor(f"x_st{i}", [128, 2 * C], F32)) for i in range(4)]
    stats = esA.enter_context(nc.sbuf_tensor("stats", [128, 192], F32))
    xn_f32 = [esA.enter_context(nc.sbuf_tensor(f"xn_f32{i}", [128, C], BF16)) for i in range(8)]
    ident_bf = esA.enter_context(nc.sbuf_tensor("ident_bfA", [128, 128], BF16))
    xnT_g = esA.enter_context(nc.sbuf_tensor("xnT_g", [128, NC8 * 512], BF16))
    KT = esA.enter_context(nc.sbuf_tensor("KT", [128, 2 * KEYS], BF16))
    Vd = esA.enter_context(nc.sbuf_tensor("Vd", [128, (KEYS // 128) * D], BF16))
    QT = esA.enter_context(nc.sbuf_tensor("QT", [128, 2 * M], BF16))
    STm_sb = [esA.enter_context(nc.sbuf_tensor(f"STm_sb{i}", [128, 256], F32)) for i in range(2)]
    kt2_sb = esA.enter_context(nc.sbuf_tensor("kt2_sb", [128, 2 * 512], BF16))
    qt2_sb = esA.enter_context(nc.sbuf_tensor("qt2_sb", [128, 2 * 256], BF16))
    ones_bf = esA.enter_context(nc.sbuf_tensor("ones_bf", [128, 1], BF16))
    bnd = esA.enter_context(nc.sbuf_tensor("bnd", [128, 16], F32))
    negm_sb = esA.enter_context(nc.sbuf_tensor("negm_sb", [128, 1], F32))
    mk2_bf = esA.enter_context(nc.sbuf_tensor("mk2_bf", [1, 4], F32))
    nm_bf = esA.enter_context(nc.sbuf_tensor("nm_bf", [1, 2], BF16))
    ones_row = esA.enter_context(nc.sbuf_tensor("ones_row", [1, 128], BF16))
    one_f32 = esA.enter_context(nc.sbuf_tensor("one_f32", [1, 1], F32))
    r_row = esA.enter_context(nc.sbuf_tensor("r_row", [1, 256], F32))
    wq_sb = esA.enter_context(nc.sbuf_tensor("wq_sb", [128, NC8 * D], BF16))
    wk_sb = esA.enter_context(nc.sbuf_tensor("wk_sb", [128, NC8 * D], BF16))
    wvd_sb = esA.enter_context(nc.sbuf_tensor("wvd_sb", [128, NC8 * D], BF16))
    rwT_sb = esA.enter_context(nc.sbuf_tensor("rwT_sb", [128, NC8 * NE], BF16))
    dmask = esA.enter_context(nc.sbuf_tensor("dmask", [128, 2048], F32))
    pt_sb = [esA.enter_context(nc.sbuf_tensor(f"pt_sb{i}", [128, 256], BF16)) for i in range(4)]

    lg_sb = esA.enter_context(nc.sbuf_tensor("lg_sb", [128, 4 * NE + 8], F32))
    pool = [esA.enter_context(nc.psum_tensor(f"pool{i}", [128, 512], F32)) for i in range(8)]

    E("gpsimd", lambda g: g.dma_start(out=ident[:], in_=ident_in[:]), inc=[("si", 16)])
    E("gpsimd", lambda g: g.dma_start(
        out=wq_sb.rearrange("p (c d) -> p c d", c=NC8),
        in_=wq.rearrange("(c p) d -> p c d", p=128)), inc=[("ss", 16)])
    E("gpsimd", lambda g: g.dma_start(
        out=wk_sb.rearrange("p (c d) -> p c d", c=NC8),
        in_=wk.rearrange("(c p) d -> p c d", p=128)), inc=[("ss", 16)])
    E("gpsimd", lambda g: g.dma_start(
        out=wvd_sb.rearrange("p (c d) -> p c d", c=NC8),
        in_=wvd.rearrange("(c p) d -> p c d", p=128)), inc=[("ss", 16)])
    E("gpsimd", lambda g: g.dma_start(
        out=wvup_sb.rearrange("p (h c) -> p h c", h=2),
        in_=wvup.rearrange("(h p) c -> p h c", p=128)), inc=[("ss", 16)])
    E("gpsimd", lambda g: g.dma_start(
        out=rwT_sb.rearrange("p (c n) -> p c n", c=NC8),
        in_=rwT.rearrange("(c p) n -> p c n", p=128)), inc=[("ss", 16)])
    E("gpsimd", lambda g: g.dma_start(out=dmask[:], in_=dmask_in[:]), inc=[("ss", 16)])
    SW_SMALL = ("ss", rec.v("ss"))

    z = ("ve", 0)
    pool_free = [z] * 8
    ptb_free = [("pe", 0)] * 4
    xn_free = [("pe", 0)] * 8
    tr_rr = [0]

    NU = KEYS // 256
    x_loaded = [None] * NU
    x_consumed = [None] * NU

    def load_x_unit(u):
        waits = []
        if u >= 4:
            waits.append(("ve", x_consumed[u - 4]))
        sem = ("sxa", "sxb", "sxc", "sxd")[u % 4]
        E("sync", (lambda u: lambda e: e.dma_start(
            out=x_st[u % 4].rearrange("p (t e) -> p t e", t=2),
            in_=x[256 * u:256 * (u + 1), :].rearrange("(t p) e -> p t e", p=128)))(u),
            wait=waits, inc=[(sem, 16)])
        x_loaded[u] = (sem, rec.v(sem))

    for _u in range(min(4, NU)):
        load_x_unit(_u)

    E("vector", lambda e: e.tensor_copy(ident_bf[:], ident[:]), wait=[("si", 16)],
      inc=[("ve", 1)])
    E("vector", lambda e: e.memset(ones_bf[:], 1.0), inc=[("ve", 1)])
    E("vector", lambda e: e.memset(ones_row[:], 1.0), inc=[("ve", 1)])
    E("vector", lambda e: e.memset(one_f32[:], 1.0), inc=[("ve", 1)])
    E("vector", lambda e: e.memset(mk2_bf[:, 0:1], 0.0), inc=[("ve", 1)])
    ve_ones = rec.v("ve")

    prev_ve_xn = [0, 0]
    prev_ac_rstd = [0, 0]

    ac_exp_prev = 0
    pe_q2_prev = 0
    pe_k2_prev = 0
    nm_prev = 0
    rrow_prev = [0]
    ac_bnd_prev = [0]
    mq2_state = {"ve": 0}
    mk2_prev = [0]
    rinv_ve = []
    ln_state = {}
    xn_state = {}
    mk2_state = {"ve": ve_ones}

    def emit_lnstats(g):
        p2 = g % 2
        base = 96 * p2
        gst = stats[:, base:base + 96]
        xts = []
        for ts in range(4):
            ptg = 4 * g + ts
            u, loc = ptg // 2, ptg % 2
            x_t = x_st[u % 4][:, loc * C:(loc + 1) * C]
            xts.append((u, x_t))
            E("vector", (lambda x_t, gst, ts: lambda e: e.bn_stats(
                gst[:, 16 * ts:16 * ts + 6], x_t[:, 0:512]))(x_t, gst, ts),
              wait=[x_loaded[u]], inc=[("ve", 1)])
            E("vector", (lambda x_t, gst, ts: lambda e: e.bn_stats(
                gst[:, 16 * ts + 6:16 * ts + 12], x_t[:, 512:1024]))(x_t, gst, ts),
              inc=[("ve", 1)])
        ve_stats = rec.v("ve")
        for ts in range(4):
            out_ap = bass.AP(stats, base + 64 + ts, [[192, 128], [4, 2]])
            E("vector", (lambda gst, ts, out_ap: lambda e: e.bn_aggr(
                out_ap, gst[:, 16 * ts:16 * ts + 12]))(gst, ts, out_ap),
              wait=([("ve", ve_stats), ("ac", prev_ac_rstd[p2])] if ts == 0 else []),
              inc=[("ve", 1)])
        ve_agg = rec.v("ve")
        ln_state[g] = (gst, xts, ve_agg)

    def emit_lnfin(g):
        p2 = g % 2
        gst, xts, ve_agg = ln_state.pop(g)
        VF()
        E("vector", (lambda gst: lambda e: e.tensor_scalar(
            gst[:, 68:72], gst[:, 68:72], 1e-5, None, ALU.add))(gst),
          wait=[("ve", ve_agg)], inc=[("ve", 1)])
        ve_c1 = rec.v("ve")
        E("scalar", (lambda gst: lambda e: e.activation(
            gst[:, 72:76], gst[:, 68:72], AF.Ln))(gst),
          wait=[("ve", ve_c1), ("ve", prev_ve_xn[p2])], inc=[("ac", 1)])
        ac_ln = rec.v("ac")
        E("scalar", (lambda gst: lambda e: e.activation(
            gst[:, 76:80], gst[:, 72:76], AF.Exp, scale=-0.5))(gst),
          wait=[("ac", ac_ln)], inc=[("ac", 1)])
        ac_rstd = rec.v("ac")
        prev_ac_rstd[p2] = ac_rstd
        ve_xns = []
        for ts in range(4):
            u, x_t = xts[ts]
            ti = 4 * (g % 2) + ts
            w_ts = [("ac", ac_rstd), ("ve", ve_agg), xn_free[ti]]
            if ts == 0:
                w_ts.append(("sxt", 16 * g))
            E("vector", (lambda x_t, gst, ts, ti: lambda e: e.tensor_scalar(
                xn_f32[ti][:], x_t, gst[:, 64 + ts:65 + ts], gst[:, 76 + ts:77 + ts],
                ALU.subtract, ALU.mult))(x_t, gst, ts, ti),
              wait=w_ts, inc=[("ve", 1)])
            ve_xn = rec.v("ve")
            ve_xns.append(ve_xn)
            prev_ve_xn[p2] = ve_xn
            if ts == 0:
                E("gpsimd", (lambda g, ti: lambda e: e.dma_start(
                    out=xnt_dram[:, C * g:C * (g + 1)], in_=xn_f32[ti][:]))(g, ti),
                  wait=[("ve", ve_xn)], inc=[("sxt", 16)])
            if ts % 2 == 1:
                x_consumed[u] = ve_xn
                if u + 4 < NU:
                    load_x_unit(u + 4)
        xn_state[g] = ve_xns

    tr_fillers = []
    xn_edges = {}
    tr_bank = [0]

    def queue_xn(g):
        ve_xns = xn_state.pop(g)
        items = []
        for ts in range(4):
            for c in range(NC8):
                items.append((g, ts, c, ve_xns[ts]))
        tr_fillers.extend(items)

    def _emit_tr(g, ts, c, ve_xn):
        k = 6 + (tr_bank[0] % 2)
        tr_bank[0] += 1
        w = [("ve", ve_xn), pool_free[k]]
        if g == 0 and ts == 0 and c == 0:
            w.append(SW_SMALL)
            w.append(("si", 16))
        E("tensor", (lambda ts, c, k, ti: lambda e: e.matmul(
            pool[k][:, 0:128], xn_f32[ti][:, c * 128:(c + 1) * 128],
            ident_bf[:], start=True, stop=True,
            skip_group_check=True))(ts, c, k, 4 * (g % 2) + ts),
          wait=w, inc=[("pe", 1)])
        pe_tr = rec.v("pe")
        dst = (lambda c, ts: lambda: xnT_g[:, 512 * c + 128 * ts:512 * c + 128 * (ts + 1)])(c, ts)
        if k % 2 == 0:
            E("vector", (lambda k, dst: lambda e: e.tensor_copy(dst(), pool[k][:, 0:128]))(k, dst),
              wait=[("pe", pe_tr)], inc=[("ve", 1)])
            pool_free[k] = ("ve", rec.v("ve"))
        else:
            E("scalar", (lambda k, dst: lambda e: e.copy(dst(), pool[k][:, 0:128]))(k, dst),
              wait=[("pe", pe_tr)], inc=[("ac", 1)])
            pool_free[k] = ("ac", rec.v("ac"))
        if ts == 3 and c == NC8 - 1:
            edges = [("ve", rec.v("ve")), ("ac", rec.v("ac"))]
            xn_edges[g] = edges
            for t_ in range(4):
                xn_free[4 * (g % 2) + t_] = ("pe", pe_tr)

    def drain_tr(n):
        while n > 0 and tr_fillers:
            _emit_tr(*tr_fillers.pop(0))
            n -= 1

    def emit_xn(g):
        while g not in xn_edges:
            assert tr_fillers
            _emit_tr(*tr_fillers.pop(0))
        return xn_edges[g]

    def emit_proj(g, g_done, with_bound=False):
        nonlocal pe_k2_prev
        if with_bound and g >= 2:
            emit_bound(g // 2)
        for c in range(NC8):
            E("vector", (lambda c, g: lambda e: e.tensor_copy(
                xnT_own[:, M * c + 128 * g:M * c + 128 * (g + 1)],
                xnT_g[:, 512 * c:512 * c + 128]))(c, g),
              wait=(g_done if c == 0 else []), inc=[("ve", 1)])
        ve_own = rec.v("ve")
        kt_banks = [7, 6]
        for dh in range(2):
            pb = kt_banks[dh]
            for c in range(NC8):
                E("tensor", (lambda dh, c, pb: lambda e: e.matmul(
                    pool[pb][:], wk_sb[:, D * c + 128 * dh:D * c + 128 * dh + 128],
                    xnT_g[:, 512 * c:512 * (c + 1)],
                    start=(c == 0), stop=(c == NC8 - 1)))(dh, c, pb),
                  wait=(g_done + [pool_free[pb]] if c == 0 else []),
                  inc=([("pe", 1)] if c == NC8 - 1 else []))
            pe_kt = rec.v("pe")
            if dh == 0:
                E("vector", (lambda dh, g, pb: lambda e: e.tensor_copy(
                    KT[:, KEYS * dh + 512 * g:KEYS * dh + 512 * (g + 1)], pool[pb][:]))(dh, g, pb),
                  wait=[("pe", pe_kt)], inc=[("ve", 1)])
                pool_free[pb] = ("ve", rec.v("ve"))
            else:
                E("scalar", (lambda dh, g, pb: lambda e: e.copy(
                    KT[:, KEYS * dh + 512 * g:KEYS * dh + 512 * (g + 1)], pool[pb][:]))(dh, g, pb),
                  wait=[("pe", pe_kt)], inc=[("ac", 1)])
                pool_free[pb] = ("ac", rec.v("ac"))
        ve_kt = rec.v("ve")
        ac_kt = rec.v("ac")
        E("scalar", (lambda g: lambda e: e.activation(
            kt2_sb[:, 0:512], KT[:, 512 * g:512 * (g + 1)], AF.Square))(g),
          wait=[("ve", ve_kt), ("pe", pe_k2_prev)], inc=[("ac", 1)])
        E("scalar", (lambda g: lambda e: e.activation(
            kt2_sb[:, 512:1024], KT[:, KEYS + 512 * g:KEYS + 512 * (g + 1)], AF.Square))(g),
          wait=[("ac", ac_kt)], inc=[("ac", 1)])
        ac_kt2 = rec.v("ac")
        for dh in range(2):
            pb = 6 + dh
            for c in range(NC8):
                E("tensor", (lambda dh, c, g, pb: lambda e: e.matmul(
                    pool[pb][:, 0:128], wq_sb[:, D * c + 128 * dh:D * c + 128 * dh + 128],
                    xnT_own[:, M * c + 128 * g:M * c + 128 * (g + 1)],
                    start=(c == 0), stop=(c == NC8 - 1)))(dh, c, g, pb),
                  wait=([("ve", ve_own), pool_free[pb]] if c == 0 else []),
                  inc=([("pe", 1)] if c == NC8 - 1 else []))
            pe_qt = rec.v("pe")
            if dh == 0:
                E("vector", (lambda dh, g, pb: lambda e: e.tensor_copy(
                    QT[:, M * dh + 128 * g:M * dh + 128 * (g + 1)], pool[pb][:, 0:128]))(dh, g, pb),
                  wait=[("pe", pe_qt)], inc=[("ve", 1)])
                pool_free[pb] = ("ve", rec.v("ve"))
            else:
                E("scalar", (lambda dh, g, pb: lambda e: e.copy(
                    QT[:, M * dh + 128 * g:M * dh + 128 * (g + 1)], pool[pb][:, 0:128]))(dh, g, pb),
                  wait=[("pe", pe_qt)], inc=[("ac", 1)])
                pool_free[pb] = ("ac", rec.v("ac"))
        qt_ve, qt_ac = rec.v("ve"), rec.v("ac")
        if with_bound and g < 2:
            emit_bound(g // 2)
        for ts in range(4):
            pb = kt_banks[ts % 2]
            for c in range(NC8):
                E("tensor", (lambda ts, c, pb: lambda e: e.matmul(
                    pool[pb][:, 0:D], xnT_g[:, 512 * c + 128 * ts:512 * c + 128 * (ts + 1)],
                    wvd_sb[:, D * c:D * (c + 1)],
                    start=(c == 0), stop=(c == NC8 - 1)))(ts, c, pb),
                  wait=([pool_free[pb]] if c == 0 else []),
                  inc=([("pe", 1)] if c == NC8 - 1 else []))
            pe_vd = rec.v("pe")
            if ts % 2 == 0:
                E("vector", (lambda g, ts, pb: lambda e: e.tensor_copy(
                    Vd[:, (4 * g + ts) * D:(4 * g + ts + 1) * D], pool[pb][:, 0:D]))(g, ts, pb),
                  wait=[("pe", pe_vd)], inc=[("ve", 1)])
                pool_free[pb] = ("ve", rec.v("ve"))
            else:
                E("scalar", (lambda g, ts, pb: lambda e: e.copy(
                    Vd[:, (4 * g + ts) * D:(4 * g + ts + 1) * D], pool[pb][:, 0:D]))(g, ts, pb),
                  wait=[("pe", pe_vd)], inc=[("ac", 1)])
                pool_free[pb] = ("ac", rec.v("ac"))
        if with_bound and g >= 2:
            emit_q2_update(g // 2, qt_ve, qt_ac)
        for dh in range(2):
            E("tensor", (lambda dh: lambda e: e.matmul(
                pool[7][0:1, 0:512], ones_bf[:], kt2_sb[:, 512 * dh:512 * (dh + 1)],
                start=(dh == 0), stop=(dh == 1)))(dh),
              wait=([pool_free[7], ("ac", ac_kt2), ("ve", ve_ones)] if dh == 0 else []),
              inc=([("pe", 1)] if dh == 1 else []))
        pe_k2 = rec.v("pe")
        pe_k2_prev = pe_k2
        E("vector", lambda e: e.reduce_max(mk2_bf[0:1, 1:2], pool[7][0:1, 0:512], AX.X),
          wait=[("pe", pe_k2), ("ve", mk2_state["ve"])], inc=[("ve", 1)])
        v_k2r = rec.v("ve")
        pool_free[7] = ("ve", v_k2r)
        VF()
        E("vector", lambda e: e.tensor_max(mk2_bf[0:1, 0:1], mk2_bf[0:1, 0:1], mk2_bf[0:1, 1:2]),
          wait=[("ve", v_k2r)], inc=[("ve", 1)])
        mk2_state["ve"] = rec.v("ve")
        proj_state["ve_own"] = ve_own
        if with_bound:
            emit_bound_bcast()
        return (rec.v("ve"), rec.v("ac"))

    def emit_logits(g):
        ve_own = proj_state["ve_own"]
        for c in range(NC8):
            E("tensor", (lambda c, g: lambda e: e.matmul(
                pool[6][:, 0:NE], xnT_own[:, M * c + 128 * g:M * c + 128 * (g + 1)],
                rwT_sb[:, NE * c:NE * (c + 1)],
                start=(c == 0), stop=(c == NC8 - 1)))(c, g),
              wait=([("ve", ve_own), pool_free[6]] if c == 0 else []),
              inc=([("pe", 1)] if c == NC8 - 1 else []))
        pe_lg = rec.v("pe")
        LG = lg_sb[:, 0:NE]
        EE = lg_sb[:, NE:2 * NE]
        EQ1 = lg_sb[:, 2 * NE:3 * NE]
        PM = lg_sb[:, 3 * NE:4 * NE]
        c_m1 = lg_sb[:, 4 * NE + 2:4 * NE + 3]
        c_m2 = lg_sb[:, 4 * NE + 3:4 * NE + 4]
        c_ws = lg_sb[:, 4 * NE + 4:4 * NE + 5]
        c_wi = lg_sb[:, 4 * NE + 5:4 * NE + 6]
        c_nm = lg_sb[:, 4 * NE + 6:4 * NE + 7]
        E("vector", lambda e: e.tensor_copy(LG, pool[6][:, 0:NE]),
          wait=[("pe", pe_lg)], inc=[("ve", 1)])
        v_lgc = rec.v("ve")
        pool_free[6] = ("ve", v_lgc)
        VF()
        E("vector", lambda e: e.reduce_max(c_nm, LG, AX.X, negate=True),
          wait=[("ve", v_lgc)], inc=[("ve", 1)])
        ve_lg = rec.v("ve")
        E("scalar", lambda e: e.activation(EE, LG, AF.Exp, bias=c_nm), wait=[("ve", ve_lg)],
          inc=[("ac", 1)])
        ac_e = rec.v("ac")
        E("vector", lambda e: e.reduce_max(c_m1, EE, AX.X), wait=[("ac", ac_e)], inc=[("ve", 1)])
        v_m1 = rec.v("ve")
        VF()
        E("vector", lambda e: e.tensor_scalar(EQ1, EE, c_m1, None, ALU.is_equal),
          wait=[("ve", v_m1)], inc=[("ve", 1)])
        v_eq1 = rec.v("ve")
        VF()
        E("vector", lambda e: e.scalar_tensor_tensor(PM, EQ1, -1e30, EE, ALU.mult, ALU.add),
          wait=[("ve", v_eq1)], inc=[("ve", 1)])
        v_pm = rec.v("ve")
        VF()
        E("vector", lambda e: e.reduce_max(c_m2, PM, AX.X), wait=[("ve", v_pm)], inc=[("ve", 1)])
        v_m2 = rec.v("ve")
        VF()
        E("vector", lambda e: e.tensor_scalar(PM, PM, c_m2, None, ALU.is_equal),
          wait=[("ve", v_m2)], inc=[("ve", 1)])
        v_pm2 = rec.v("ve")
        E("vector", lambda e: e.tensor_add(c_ws, c_m1, c_m2), inc=[("ve", 1)])
        v_ws = rec.v("ve")
        E("vector", lambda e: e.tensor_add(EQ1, EQ1, PM), wait=[("ve", v_pm2)], inc=[("ve", 1)])
        v_eq = rec.v("ve")
        E("vector", lambda e: e.reciprocal(c_wi, c_ws), wait=[("ve", v_ws)], inc=[("ve", 1)])
        v_wi = rec.v("ve")
        E("vector", lambda e: e.tensor_mul(EE, EE, EQ1), wait=[("ve", v_eq)], inc=[("ve", 1)])
        v_ee = rec.v("ve")
        E("vector", (lambda g: lambda e: e.tensor_copy(
            mask_f32[:, NE * g:NE * (g + 1)], EQ1))(g),
          wait=[("ve", v_eq)], inc=[("ve", 1)])
        VF()
        E("vector", (lambda g: lambda e: e.tensor_scalar(
            comb[:, NE * g:NE * (g + 1)], EE, c_wi, None, ALU.mult))(g),
          wait=[("ve", v_ee), ("ve", v_wi)], inc=[("ve", 1)])

    proj_state = {}

    bound_state = {}

    def emit_bound(p):
        nonlocal pe_q2_prev, nm_prev
        ve_pre = rec.v("ve")
        ac_pre = rec.v("ac")
        if p == 0:
            emit_q2_update(p, ve_pre, ac_pre)
            E("vector", lambda e: e.tensor_copy(bnd[0:1, 0:1], mk2_bf[0:1, 3:4]),
              wait=[("ve", mq2_state["ve"])], inc=[("ve", 1)])
            v_q2m = rec.v("ve")
            VF()
        else:
            E("vector", lambda e: e.tensor_copy(bnd[0:1, 0:1], mk2_bf[0:1, 3:4]),
              wait=[("ve", mq2_state["ve"]), ("ac", ac_bnd_prev[0])], inc=[("ve", 1)])
            v_q2m = rec.v("ve")
            VF()
        E("scalar", lambda e: e.activation(bnd[0:1, 1:2], bnd[0:1, 0:1], AF.Ln),
          wait=[("ve", v_q2m)], inc=[("ac", 1)])
        E("scalar", lambda e: e.activation(bnd[0:1, 2:3], mk2_bf[0:1, 0:1], AF.Ln),
          wait=[("ve", mk2_state["ve"] if p == 0 else mk2_prev[0])], inc=[("ac", 1)])
        ac_ln2 = rec.v("ac")
        E("vector", lambda e: e.tensor_add(bnd[0:1, 3:4], bnd[0:1, 1:2], bnd[0:1, 2:3]),
          wait=[("ac", ac_ln2)], inc=[("ve", 1)])
        v_sum = rec.v("ve")
        E("scalar", lambda e: e.activation(bnd[0:1, 4:5], bnd[0:1, 3:4], AF.Exp, scale=0.5),
          wait=[("ve", v_sum)], inc=[("ac", 1)])
        ac_m = rec.v("ac")
        ac_bnd_prev[0] = ac_m
        scale = -1.01 / 16.0 if p == 0 else -1.30 / 16.0
        E("vector", (lambda scale: lambda e: e.tensor_scalar(
            bnd[0:1, 5:6], bnd[0:1, 4:5], scale, None, ALU.mult))(scale),
          wait=[("ac", ac_m)], inc=[("ve", 1)])
        v_nm = rec.v("ve")
        VF()
        E("vector", lambda e: e.tensor_copy(nm_bf[0:1, 0:1], bnd[0:1, 5:6]),
          wait=[("ve", v_nm), ("ve", nm_prev)], inc=[("ve", 1)])
        v_nmc = rec.v("ve")
        nm_prev = v_nmc
        bound_state["v_nmc"] = v_nmc

    def emit_bound_bcast():
        v_nmc = bound_state["v_nmc"]
        E("tensor", lambda e: e.matmul(
            pool[3][:, 0:1], ones_row[0:1, :], nm_bf[0:1, 0:1],
            start=True, stop=True),
          wait=[pool_free[3], ("ve", v_nmc), ("ve", ve_ones)], inc=[("pe", 1)])
        pe_bc = rec.v("pe")
        E("vector", lambda e: e.tensor_copy(negm_sb[:], pool[3][:, 0:1]),
          wait=[("pe", pe_bc), ("ac", ac_exp_prev)], inc=[("ve", 1)])
        bound_state["ve_negm"] = rec.v("ve")
        pool_free[3] = ("ve", bound_state["ve_negm"])

    def emit_q2_update(p, ve_pre, ac_pre):
        nonlocal pe_q2_prev
        E("scalar", (lambda p: lambda e: e.activation(
            qt2_sb[:, 0:256], QT[:, 256 * p:256 * (p + 1)], AF.Square))(p),
          wait=[("ve", ve_pre), ("ac", ac_pre), ("pe", pe_q2_prev)], inc=[("ac", 1)])
        E("scalar", (lambda p: lambda e: e.activation(
            qt2_sb[:, 256:512], QT[:, M + 256 * p:M + 256 * (p + 1)], AF.Square))(p),
          inc=[("ac", 1)])
        ac_qt2 = rec.v("ac")
        for dh in range(2):
            E("tensor", (lambda dh: lambda e: e.matmul(
                pool[6][0:1, 0:256], ones_bf[:], qt2_sb[:, 256 * dh:256 * (dh + 1)],
                start=(dh == 0), stop=(dh == 1)))(dh),
              wait=([pool_free[6], ("ac", ac_qt2)] if dh == 0 else []),
              inc=([("pe", 1)] if dh == 1 else []))
        pe_q2 = rec.v("pe")
        pe_q2_prev = pe_q2
        E("vector", lambda e: e.reduce_max(mk2_bf[0:1, 2:3], pool[6][0:1, 0:256], AX.X),
          wait=[("pe", pe_q2), ("ve", mq2_state["ve"])], inc=[("ve", 1)])
        v_q = rec.v("ve")
        pool_free[6] = ("ve", v_q)
        VF()
        if p == 0:
            E("vector", lambda e: e.tensor_copy(mk2_bf[0:1, 3:4], mk2_bf[0:1, 2:3]),
              wait=[("ve", v_q)], inc=[("ve", 1)])
        else:
            E("vector", lambda e: e.tensor_max(mk2_bf[0:1, 3:4], mk2_bf[0:1, 3:4],
                                               mk2_bf[0:1, 2:3]),
              wait=[("ve", v_q)], inc=[("ve", 1)])
        mq2_state["ve"] = rec.v("ve")

    ST_BANKS = (0, 1, 5)

    def make_attn_pair(p, pre):
        nkb = 8 * p + 8
        ve_pre, ac_pre = pre
        st_of = {}
        diag_ac = [ac_exp_prev, ac_exp_prev]
        state = {"kb": 0, "acs": {}, "primed": False, "pe_ctx": None, "ve_rr": None}

        def emit_st(kb):
            sb = ST_BANKS[kb % 3]
            w = []
            for dh in range(2):
                if dh == 0:
                    w = [pool_free[sb]]
                    if kb == 0:
                        w += [("ve", ve_pre), ("ac", ac_pre)]
                E("tensor", (lambda p, kb, dh, sb: lambda e: e.matmul(
                    pool[sb][:, 0:256],
                    KT[:, KEYS * dh + 128 * kb:KEYS * dh + 128 * (kb + 1)],
                    QT[:, M * dh + 256 * p:M * dh + 256 * (p + 1)],
                    start=(dh == 0), stop=(dh == 1),
                    skip_group_check=True))(p, kb, dh, sb),
                  wait=(w if dh == 0 else []), inc=([("pe", 1)] if dh == 1 else []))
            st_of[kb] = ("pe", rec.v("pe"))

        def emit_exp(kb):
            sb = ST_BANKS[kb % 3]
            pb = kb % 4
            j = kb - 8 * p
            if j >= 0:
                db = kb % 2
                E("vector", (lambda kb, sb, j, db: lambda e: e.tensor_add(
                    STm_sb[db][:], pool[sb][:, 0:256],
                    dmask[:, 256 * j:256 * (j + 1)]))(kb, sb, j, db),
                  wait=[st_of[kb], SW_SMALL, ("ac", diag_ac[db])], inc=[("ve", 1)])
                pool_free[sb] = ("ve", rec.v("ve"))
                E("scalar", (lambda pb, db: lambda e: e.activation(
                    pt_sb[pb][:], STm_sb[db][:], AF.Exp,
                    bias=negm_sb[:], scale=0.0625))(pb, db),
                  wait=[("ve", rec.v("ve")), ptb_free[pb]], inc=[("ac", 1)])
                diag_ac[db] = rec.v("ac")
            else:
                E("scalar", (lambda kb, sb, pb: lambda e: e.activation(
                    pt_sb[pb][:], pool[sb][:, 0:256], AF.Exp,
                    bias=negm_sb[:], scale=0.0625))(kb, sb, pb),
                  wait=[st_of[kb], ("ve", bound_state["ve_negm"]), ptb_free[pb]], inc=[("ac", 1)])
                pool_free[sb] = ("ac", rec.v("ac"))
            return rec.v("ac")

        def emit_rctx(kb, ac_p):
            pb = kb % 4
            E("tensor", (lambda kb, pb, nkb: lambda e: e.matmul(
                pool[4][0:1, 0:256], ones_bf[:], pt_sb[pb][:],
                start=(kb == 0), stop=(kb == nkb - 1),
                skip_group_check=True))(kb, pb, nkb),
              wait=[("ac", ac_p)] + ([pool_free[4]] if kb == 0 else []),
              inc=[])
            for dh in range(2):
                E("tensor", (lambda kb, dh, pb, nkb: lambda e: e.matmul(
                    pool[2 + dh][:, 0:256], Vd[:, D * kb + 128 * dh:D * kb + 128 * dh + 128],
                    pt_sb[pb][:], start=(kb == 0), stop=(kb == nkb - 1),
                    skip_group_check=True))(kb, dh, pb, nkb),
                  wait=([pool_free[2 + dh]] if kb == 0 else []),
                  inc=([("pe", 1)] if dh == 1 else []))
            ptb_free[pb] = ("pe", rec.v("pe"))

        def finish():
            nonlocal ac_exp_prev
            ac_exp_prev = rec.v("ac")
            pe_ctx = rec.v("pe")
            state["pe_ctx"] = pe_ctx
            E("vector", (lambda p: lambda e: e.tensor_copy(
                ctxT[:, 256 * p:256 * (p + 1)], pool[2][:, 0:256]))(p),
              wait=[("pe", pe_ctx)], inc=[("ve", 1)])
            pool_free[2] = ("ve", rec.v("ve"))
            E("scalar", (lambda p: lambda e: e.copy(
                ctxT[:, M + 256 * p:M + 256 * (p + 1)], pool[3][:, 0:256]))(p),
              wait=[("pe", pe_ctx)], inc=[("ac", 1)])
            pool_free[3] = ("ac", rec.v("ac"))
            E("scalar", lambda e: e.copy(r_row[:], pool[4][0:1, 0:256]),
              wait=[("pe", pe_ctx), ("pe", rrow_prev[0])], inc=[("ac", 1)])
            pool_free[4] = ("ac", rec.v("ac"))
            state["ve_rr"] = ("ac", rec.v("ac"))

        def run(upto):
            if not state["primed"]:
                emit_st(0)
                if nkb > 1:
                    emit_st(1)
                state["acs"][0] = emit_exp(0)
                state["primed"] = True
            upto = min(upto, nkb)
            kb = state["kb"]
            while kb < upto:
                if kb + 2 < nkb:
                    emit_st(kb + 2)
                if kb + 1 < nkb:
                    state["acs"][kb + 1] = emit_exp(kb + 1)
                emit_rctx(kb, state["acs"].pop(kb))
                drain_tr(2)
                kb += 1
            state["kb"] = kb
            if kb == nkb and state["pe_ctx"] is None:
                finish()

        def tail():
            rr_edge = state["ve_rr"]
            for half in range(2):
                E("tensor", (lambda half: lambda e: e.matmul(
                    pool[3][:, 0:1], r_row[0:1, 128 * half:128 * (half + 1)], one_f32[:],
                    start=True, stop=True))(half),
                  wait=[rr_edge, pool_free[3]],
                  inc=[("pe", 1)])
                pe_rt = rec.v("pe")
                rrow_prev[0] = pe_rt
                E("vector", (lambda p, half: lambda e: e.tensor_copy(
                    r_all[:, 2 * p + half:2 * p + half + 1], pool[3][:, 0:1]))(p, half),
                  wait=[("pe", pe_rt)], inc=[("ve", 1)])
                pool_free[3] = ("ve", rec.v("ve"))
                ve_r = rec.v("ve")
                VF()
                E("vector", (lambda p, half: lambda e: e.reciprocal(
                    rinv_all[:, 2 * p + half:2 * p + half + 1],
                    r_all[:, 2 * p + half:2 * p + half + 1]))(p, half),
                  wait=[("ve", ve_r)], inc=[("ve", 1)])
                rinv_ve.append(rec.v("ve"))
        return run, tail

    for _wu in range(40):
        E("tensor", (lambda: lambda e: e.matmul(
            pool[6][:, 0:128], ident[:], ident[:], start=True, stop=True))(),
          wait=([("si", 16)] if _wu == 0 else []), inc=([("pe", 1)] if _wu == 39 else []))
    pool_free[6] = ("pe", rec.v("pe"))

    emit_lnstats(0)
    emit_lnfin(0)
    queue_xn(0)
    pair = None
    pair_tail = None
    for g in range(NG):
        g_done = emit_xn(g)
        if g + 1 < NG:
            emit_lnstats(g + 1)
        if pair_tail is not None:
            pair_tail()
            pair_tail = None
        mk2_prev[0] = mk2_state["ve"]
        pre = emit_proj(g, g_done, with_bound=(g % 2 == 1))
        if g % 2 == 0 and pair is not None:
            if g + 1 < NG:
                emit_lnfin(g + 1)
                queue_xn(g + 1)
            run, tl = pair
            run(10 ** 9)
            pair = None
            pair_tail = tl
        elif g % 2 == 1:
            if g + 1 < NG:
                emit_lnfin(g + 1)
                queue_xn(g + 1)
            run, tl = make_attn_pair(g // 2, pre)
            run((8 * (g // 2) + 8) // 2)
            pair = (run, tl)
        else:
            if g + 1 < NG:
                emit_lnfin(g + 1)
                queue_xn(g + 1)
        emit_logits(g)
    if pair is not None:
        run, tl = pair
        run(10 ** 9)
        tl()

    if debug:
        wA = [("ve", rec.v("ve")), ("ac", rec.v("ac")), ("pe", rec.v("pe"))]
        E("gpsimd", lambda g: g.dma_start(out=d_xnT[:], in_=xnT_own[:]), wait=wA, inc=[("so0", 16)])
        E("gpsimd", lambda g: g.dma_start(out=d_KT[:], in_=KT[:]), inc=[("so0", 16)])
        E("gpsimd", lambda g: g.dma_start(out=d_Vd[:], in_=Vd[:]), inc=[("so0", 16)])
        E("gpsimd", lambda g: g.dma_start(out=d_QT[:], in_=QT[:]), inc=[("so0", 16)])
        E("gpsimd", lambda g: g.dma_start(out=d_comb[:], in_=comb[:]), inc=[("so0", 16)])
        E("gpsimd", lambda g: g.dma_start(out=d_r[:], in_=r_all[:]), inc=[("so0", 16)])
        E("gpsimd", lambda g: g.dma_start(out=d_ctxT[:], in_=ctxT[:]), inc=[("so0", 16)])
        E("gpsimd", None, wait=[("so0", rec.v("so0"))])


    ve_A0 = rec.v("ve")

    ve_A = rec.v("ve")
    ac_A = rec.v("ac")
    pe_A = rec.v("pe")

    replay(nc, rec, sems)
    esA.close()

    rec = Rec(rec.cnt)
    E = rec.E
    barA = [("ve", ve_A), ("ac", ac_A), ("pe", pe_A)]

    NWW, CAPW, SPE, NSEC = 2, 384, 768, 6
    W1CH = 16
    W2CH = 16

    esB = ExitStack()
    xn_sel = esB.enter_context(nc.sbuf_tensor("xn_sel", [128, NC8 * SPE], BF16))
    hT = esB.enter_context(nc.sbuf_tensor("hT", [128, NH8 * SPE], BF16))
    y_half = esB.enter_context(nc.sbuf_tensor("y_half", [128, NG * C], BF16))
    y_selE = [esB.enter_context(nc.sbuf_tensor(f"y_selE{i}", [128, NSEC * C], BF16))
              for i in range(2)]
    w1sb = [esB.enter_context(nc.sbuf_tensor(f"w1sb{i}", [128, NC8 * 256], BF16)) for i in range(3)]
    w2sb = [esB.enter_context(nc.sbuf_tensor(f"w2sb{i}", [128, 2 * 512], BF16)) for i in range(4)]
    w2f32 = [esB.enter_context(nc.sbuf_tensor(f"w2f32{i}", [128, 2 * 512], F32)) for i in range(3)]
    iota_t = esB.enter_context(nc.sbuf_tensor("iota_t", [128, 384], F32))
    rel_sb = esB.enter_context(nc.sbuf_tensor("rel_sb", [128, 128], F32))
    onesB = esB.enter_context(nc.sbuf_tensor("onesB", [128, 1], BF16))
    ones1b = esB.enter_context(nc.sbuf_tensor("ones1b", [1, 128], BF16))
    st_ring = esB.enter_context(nc.sbuf_tensor("st_ring", [128, 6 * 128], BF16))
    tfeat = esB.enter_context(nc.sbuf_tensor("tfeat", [128, 2 * NG], BF16))
    tls_sb = [esB.enter_context(nc.sbuf_tensor(f"tls_sb{i}", [128, 12], F32)) for i in range(2)]
    eq_tl = esB.enter_context(nc.sbuf_tensor("eq_tl", [128, 6 * 128], BF16))
    m_col = esB.enter_context(nc.sbuf_tensor("m_col", [128, 1], F32))
    tls_snap = esB.enter_context(nc.sbuf_tensor("tls_snap", [128, 12], F32))
    yps = [esB.enter_context(nc.psum_tensor(f"ypsB{i}", [128, 512], F32)) for i in range(8)]
    es_g = ExitStack()
    xn_tok = es_g.enter_context(nc.sbuf_tensor("xn_tok", [128, NG * C], BF16))
    S8 = es_g.enter_context(nc.sbuf_tensor("S8", [128, 8 * 384], BF16))
    es_r = ExitStack()
    ltri_sb = es_r.enter_context(nc.sbuf_tensor("ltri_sb", [128, 128], BF16))
    gw_sb = es_r.enter_context(nc.sbuf_tensor("gw_sb", [128, 128], BF16))
    mask_bf = es_r.enter_context(nc.sbuf_tensor("mask_bf", [128, 128], BF16))
    offs_sb = es_r.enter_context(nc.sbuf_tensor("offs_sb", [1, 128], F32))
    tot_bfc = es_r.enter_context(nc.sbuf_tensor("tot_bfc", [128, 1], BF16))
    offs_hi = es_r.enter_context(nc.sbuf_tensor("offs_hi", [1, 128], BF16))
    offs_lo = es_r.enter_context(nc.sbuf_tensor("offs_lo", [1, 128], BF16))
    offs_hf = es_r.enter_context(nc.sbuf_tensor("offs_hf", [1, 128], F32))
    if debug:
        d_rel = nc.dram_tensor("d_rel", [128, 128], F32, kind="ExternalOutput")
        d_yh = nc.dram_tensor("d_yh", [128, NG * C], F32, kind="ExternalOutput")
        d_xsel = nc.dram_tensor("d_xsel", [128, NC8 * SPE], F32, kind="ExternalOutput")

    poolB = [list(barA) for _ in range(8)]

    E("gpsimd", lambda g: g.dma_start(out=ltri_sb[:], in_=ltri_in[:]), inc=[("ss", 16)])
    E("gpsimd", lambda g: g.dma_start(out=gw_sb[:], in_=gw_in[:]), inc=[("ss", 16)])
    E("gpsimd", lambda g: g.dma_start(out=iota_t[:], in_=iota_in[:]), inc=[("ss", 16)])
    E("gpsimd", lambda g: g.dma_start(out=tfeat[:], in_=tfeat_in[:]), inc=[("ss", 16)])
    SS_B = ("ss", rec.v("ss"))
    E("sync", lambda e: e.dma_start(out=xn_tok[:, 0:8 * C], in_=xnt_dram[:, 0:8 * C]),
      wait=[("sxt", 16 * NG)], inc=[("swh", 16)])
    E("sync", lambda e: e.dma_start(out=xn_tok[:, 8 * C:NG * C], in_=xnt_dram[:, 8 * C:NG * C]),
      inc=[("swh", 16)])
    SI_XT = ("swh", rec.v("swh"))

    E("vector", lambda e: e.memset(onesB[:], 1.0), wait=barA, inc=[("ve", 1)])
    E("vector", lambda e: e.memset(ones1b[:], 1.0), inc=[("ve", 1)])
    ve_onesB = rec.v("ve")

    E("vector", lambda e: e.tensor_copy(mask_bf[:], mask_f32[:]), inc=[("ve", 1)])
    ve_mbf = rec.v("ve")
    E("tensor", lambda t: t.matmul(yps[0][:, 0:128], ltri_sb[:], mask_bf[:],
                                   start=True, stop=True),
      wait=poolB[0] + [("ve", ve_mbf), SS_B], inc=[("pe", 1)])
    pe_pp = rec.v("pe")
    E("vector", lambda e: e.tensor_copy(rel_sb[:], yps[0][:, 0:128]),
      wait=[("pe", pe_pp)], inc=[("ve", 1)])
    ve_pp = rec.v("ve")
    poolB[0] = [("ve", ve_pp)]
    E("tensor", lambda t: t.matmul(yps[1][:, 0:1], mask_bf[:], onesB[:],
                                   start=True, stop=True),
      wait=poolB[1] + [("ve", ve_onesB), ("ve", ve_mbf)], inc=[("pe", 1)])
    pe_tot = rec.v("pe")
    E("scalar", lambda s: s.copy(tot_bfc[:], yps[1][:, 0:1]),
      wait=[("pe", pe_tot)], inc=[("ac", 1)])
    ac_tot = rec.v("ac")
    poolB[1] = [("ac", ac_tot)]
    E("tensor", lambda t: t.matmul(yps[2][0:1, 0:128], tot_bfc[:], gw_sb[:],
                                   start=True, stop=True),
      wait=poolB[2] + [("ac", ac_tot)], inc=[("pe", 1)])
    pe_of = rec.v("pe")
    E("vector", lambda e: e.tensor_copy(offs_sb[:], yps[2][0:1, 0:128]),
      wait=[("pe", pe_of)], inc=[("ve", 1)])
    ve_of = rec.v("ve")
    poolB[2] = [("ve", ve_of)]
    E("scalar", lambda s: s.copy(offs_hi[:], offs_sb[:]),
      wait=[("ve", ve_of)], inc=[("ac", 1)])
    ac_oh = rec.v("ac")
    E("scalar", lambda s: s.copy(offs_hf[:], offs_hi[:]), wait=[("ac", ac_oh)], inc=[("ac", 1)])
    ac_ohf = rec.v("ac")
    E("vector", lambda v: v.tensor_tensor(offs_sb[:], offs_sb[:], offs_hf[:], ALU.subtract),
      wait=[("ac", ac_ohf)], inc=[("ve", 1)])
    v_lf = rec.v("ve")
    VF()
    E("vector", lambda v: v.tensor_copy(offs_lo[:], offs_sb[:]), wait=[("ve", v_lf)],
      inc=[("ve", 1)])
    ve_ol = rec.v("ve")
    for part, wsem in ((offs_hi, ("ac", ac_oh)), (offs_lo, ("ve", ve_ol))):
        E("tensor", (lambda part: lambda t: t.matmul(
            yps[3][:, 0:128], ones1b[:], part[:],
            start=(part is offs_hi), stop=(part is offs_lo)))(part),
          wait=(poolB[3] + [wsem, ("ve", ve_onesB)] if part is offs_hi else [wsem]),
          inc=([("pe", 1)] if part is offs_lo else []))
    pe_ob = rec.v("pe")
    VF()
    E("vector", lambda e: e.tensor_tensor(rel_sb[:], rel_sb[:], yps[3][:, 0:128], ALU.add),
      wait=[("pe", pe_ob), ("ve", ve_pp)], inc=[("ve", 1)])
    v_r1 = rec.v("ve")
    VF()
    E("vector", lambda e: e.tensor_tensor(rel_sb[:], rel_sb[:], mask_f32[:], ALU.mult),
      wait=[("ve", v_r1)], inc=[("ve", 1)])
    v_r2 = rec.v("ve")
    VF()
    E("vector", lambda e: e.tensor_scalar(rel_sb[:], rel_sb[:], 1.0, None, ALU.subtract),
      wait=[("ve", v_r2)], inc=[("ve", 1)])
    ve_rel = rec.v("ve")
    poolB[3] = [("ve", rec.v("ve"))]
    if debug:
        E("gpsimd", lambda g: g.dma_start(out=d_rel[:], in_=rel_sb[:]),
          wait=[("ve", ve_rel)], inc=[("so0", 16)])

    es_r.close()

    w1_loaded = {}
    w1_free = {}
    w1_ptr = [0]
    W1SEM = ("swa", "swb", "swc")

    def issue_w1():
        idx = w1_ptr[0]
        if idx >= NE * W1CH:
            return
        w1_ptr[0] += 1
        e_, ch = idx // W1CH, idx % W1CH
        buf = idx % 3
        waits = [w1_free[idx - 3]] if idx >= 3 else []
        E("gpsimd", (lambda e_, ch, buf: lambda g: g.dma_start(
            out=w1sb[buf].rearrange("p (c h) -> p c h", c=NC8),
            in_=w1[e_, :, 256 * ch:256 * (ch + 1)].rearrange("(c p) h -> p c h", p=128)))(e_, ch, buf),
            wait=waits, inc=[(W1SEM[buf], 16)])
        w1_loaded[idx] = (W1SEM[buf], rec.v(W1SEM[buf]))

    w2_loaded = {}
    w2_free = {}
    w2sb_free = {}
    w2_cast = {}
    w2_ptr = [0]
    w2_cast_ptr = [0]
    W2SEM = ("swe", "swf", "swg", "swh")

    def issue_w2():
        idx = w2_ptr[0]
        if idx >= NE * 2 * W2CH:
            return
        w2_ptr[0] += 1
        e_ = idx // (2 * W2CH)
        cc = (idx // W2CH) % 2
        ch = idx % W2CH
        buf = idx % 3
        q = ("sync", "scalar", "sync")[buf]
        sem = ("swe", "swf", "swg")[buf]
        waits = [w2_free[idx - 3]] if idx >= 3 else []
        E(q, (lambda e_, cc, ch, buf: lambda g: g.dma_start(
            out=w2f32[buf].rearrange("p (h c) -> p h c", h=2),
            in_=w2[e_, 256 * ch:256 * (ch + 1), 512 * cc:512 * (cc + 1)]
                .rearrange("(h p) c -> p h c", p=128)))(e_, cc, ch, buf),
            wait=waits, inc=[(sem, 16)])
        w2_loaded[idx] = (sem, rec.v(sem))

    def cast_w2_upto(idx):
        while w2_cast_ptr[0] <= idx:
            k = w2_cast_ptr[0]
            w2_cast_ptr[0] += 1
            fb, bb = k % 3, k % 4
            waits = [w2_loaded[k]]
            if k >= 4:
                waits.append(w2sb_free[k - 4])
            if k % 2 == 0:
                E("scalar", (lambda fb, bb: lambda s: s.copy(w2sb[bb][:], w2f32[fb][:]))(fb, bb),
                  wait=waits, inc=[("ac", 1)])
                w2_cast[k] = ("ac", rec.v("ac"))
            else:
                E("vector", (lambda fb, bb: lambda v: v.tensor_copy(w2sb[bb][:], w2f32[fb][:]))(fb, bb),
                  wait=waits, inc=[("ve", 1)])
                w2_cast[k] = ("ve", rec.v("ve"))
            w2_free[k] = w2_cast[k]

    for _ in range(3):
        issue_w1()
        issue_w2()

    s8_use_pe = [0]
    gather_done_ve = {}

    def emit_s8_builds(e_, w):
        for k in range(8):
            g_ = w * 8 + k
            wts = [("ve", ve_rel)] if (k == 0) else []
            if k == 0 and s8_use_pe[0]:
                wts.append(("pe", s8_use_pe[0]))
            E("vector", (lambda k, g_, e_: lambda v: v.tensor_scalar(
                S8[:, 384 * k:384 * (k + 1)], iota_t[:],
                rel_sb[:, 8 * g_ + e_:8 * g_ + e_ + 1], None, ALU.is_equal))(k, g_, e_),
              wait=wts, inc=[("ve", 1)])
        return rec.v("ve")

    def emit_gather_tile(e_, idx, bank, ve_s8):
        w, c = idx // 8, idx % 8
        for k in range(8):
            wts = []
            if k == 0:
                wts = poolB[bank] + [("ve", ve_s8), SI_XT]
            E("tensor", (lambda w, c, k, bank: lambda t: t.matmul(
                yps[bank][:, 0:384],
                xn_tok[:, (w * 8 + k) * C + 128 * c:(w * 8 + k) * C + 128 * (c + 1)],
                S8[:, 384 * k:384 * (k + 1)],
                start=(k == 0), stop=(k == 7), skip_group_check=True))(w, c, k, bank),
              wait=wts, inc=([("pe", 1)] if k == 7 else []))
        pe_g = rec.v("pe")
        s8_use_pe[0] = pe_g
        E("vector", (lambda w, c, bank: lambda v: v.tensor_copy(
            xn_sel[:, c * SPE + w * CAPW:c * SPE + w * CAPW + 384], yps[bank][:, 0:384]))(w, c, bank),
          wait=[("pe", pe_g)], inc=[("ve", 1)])
        poolB[bank] = [("ve", rec.v("ve"))]

    def emit_gather_tls(e_, w, bank):
        buf = e_ % 2
        for sec3 in range(3):
            for k in range(8):
                wts = []
                if k == 0:
                    wts = list(poolB[bank]) + [SS_B]
                E("tensor", (lambda w, k, sec3, bank: lambda t: t.matmul(
                    yps[bank][:, 0:2],
                    S8[:, 384 * k + 128 * sec3:384 * k + 128 * sec3 + 128],
                    tfeat[:, 2 * (w * 8 + k):2 * (w * 8 + k) + 2],
                    start=(k == 0), stop=(k == 7), skip_group_check=True))(w, k, sec3, bank),
                  wait=wts, inc=([("pe", 1)] if k == 7 else []))
            pe_t = rec.v("pe")
            s8_use_pe[0] = pe_t
            E("vector", (lambda w, sec3, bank, buf: lambda v: v.tensor_copy(
                tls_sb[buf][:, (w * 3 + sec3) * 2:(w * 3 + sec3) * 2 + 2],
                yps[bank][:, 0:2]))(w, sec3, bank, buf),
              wait=[("pe", pe_t)], inc=[("ve", 1)])
            poolB[bank] = [("ve", rec.v("ve"))]

    gelu_chunk = {}
    hb_rr = [0]

    def emit_h_chunk(e_, ch):
        w1idx = e_ * W1CH + ch
        for hp2 in range(2):
            for sg in range(2):
                hb = 6 + (hb_rr[0] % 2)
                hb_rr[0] += 1
                for c in range(NC8):
                    wts = []
                    if c == 0:
                        wts = poolB[hb] + [w1_loaded[w1idx]]
                        if ch == 0 and hp2 == 0 and sg == 0:
                            wts.append(("ve", gather_done_ve[e_]))
                    E("tensor", (lambda ch, hp2, sg, c, hb, buf: lambda t: t.matmul(
                        yps[hb][:, 0:384],
                        w1sb[buf][:, 256 * c + 128 * hp2:256 * c + 128 * (hp2 + 1)],
                        xn_sel[:, c * SPE + sg * 384:c * SPE + (sg + 1) * 384],
                        start=(c == 0), stop=(c == NC8 - 1),
                        skip_group_check=True))(ch, hp2, sg, c, hb, w1idx % 3),
                      wait=wts, inc=([("pe", 1)] if c == NC8 - 1 else []))
                pe_h = rec.v("pe")
                H8 = 2 * ch + hp2
                E("scalar", (lambda H8, sg, hb: lambda s: s.activation(
                    hT[:, H8 * SPE + sg * 384:H8 * SPE + (sg + 1) * 384],
                    yps[hb][:, 0:384], AF.Gelu))(H8, sg, hb),
                  wait=[("pe", pe_h)], inc=[("ac", 1)])
                poolB[hb] = [("ac", rec.v("ac"))]
        w1_free[w1idx] = ("pe", rec.v("pe"))
        issue_w1()
        gelu_chunk[(e_, ch)] = rec.v("ac")

    yevac_ve = {}
    yev_sect = {}

    def emit_y_half(e_, cc, ch, shalf):
        w2idx = e_ * 2 * W2CH + cc * W2CH + ch
        if shalf == 0:
            cast_w2_upto(min(w2idx + 1, NE * 2 * W2CH - 1))
        for s6 in range(3 * shalf, 3 * shalf + 3):
            for h8x in range(2):
                H8 = 2 * ch + h8x
                first = (ch == 0 and h8x == 0)
                last = (ch == W2CH - 1 and h8x == 1)
                wts = []
                if h8x == 0 and s6 == 3 * shalf:
                    wts = [w2_cast[w2idx], ("ac", gelu_chunk[(e_, ch)])]
                if first:
                    wts = wts + poolB[s6]
                E("tensor", (lambda s6, H8, h8x, first, last, buf: lambda t: t.matmul(
                    yps[s6][:],
                    hT[:, H8 * SPE + s6 * 128:H8 * SPE + (s6 + 1) * 128],
                    w2sb[buf][:, h8x * 512:(h8x + 1) * 512],
                    start=first, stop=last,
                    skip_group_check=True))(s6, H8, h8x, first, last, w2idx % 4),
                  wait=wts,
                  inc=([("pe", 1)] if (last or (h8x == 1 and s6 == 3 * shalf + 2)) else []))
        if shalf == 1:
            w2sb_free[w2idx] = ("pe", rec.v("pe"))
            issue_w2()

    def emit_y_evacs(e_, cc):
        pe_y = rec.v("pe")
        buf = e_ % 2
        for s6 in range(NSEC):
            E("vector", (lambda s6, cc, buf: lambda v: v.tensor_copy(
                y_selE[buf][:, s6 * C + cc * 512:s6 * C + cc * 512 + 512], yps[s6][:]))(s6, cc, buf),
              wait=[("pe", pe_y)], inc=[("ve", 1)])
            poolB[s6] = [("ve", rec.v("ve"))]
            yev_sect[(e_, cc, s6)] = rec.v("ve")
        if cc == 1:
            yevac_ve[e_] = rec.v("ve")

    ring_free = [None] * 6
    ring_ve = [0] * 6
    eq_state = {"e": None, "ve": 0}
    mcol_reader = [0]

    def emit_eq_tl(se):
        buf = se % 2
        E("vector", (lambda buf: lambda v: v.tensor_copy(tls_snap[:], tls_sb[buf][:]))(buf),
          wait=[("ve", mcol_reader[0])], inc=[("ve", 1)])
        for s6 in range(NSEC):
            E("vector", (lambda s6, buf: lambda v: v.tensor_scalar(
                eq_tl[:, s6 * 128:(s6 + 1) * 128], iota_t[:, 0:128],
                tls_sb[buf][:, s6 * 2:s6 * 2 + 1], None, ALU.is_equal))(s6, buf),
              inc=[("ve", 1)])
        eq_state["e"] = se
        eq_state["ve"] = rec.v("ve")

    def emit_st_one(se, g_, sloc):
        buf = se % 2
        s6 = (g_ // 8) * 3 + sloc
        E("vector", (lambda g_, s6: lambda v: v.tensor_scalar(
            m_col[:], tls_snap[:, s6 * 2 + 1:s6 * 2 + 2], float(g_), None,
            ALU.is_equal))(g_, s6),
          wait=[("ve", max(mcol_reader[0], eq_state["ve"]))], inc=[("ve", 1)])
        VF()
        rslot = (g_ * 3 + sloc) % 6
        wst = [("ve", rec.v("ve"))]
        if ring_free[rslot] is not None:
            wst.append(ring_free[rslot])
        E("vector", (lambda s6, rslot: lambda v: v.tensor_scalar(
            st_ring[:, rslot * 128:(rslot + 1) * 128], eq_tl[:, s6 * 128:(s6 + 1) * 128],
            m_col[:], None, ALU.mult))(s6, rslot),
          wait=wst, inc=[("ve", 1)])
        ring_ve[rslot] = rec.v("ve")
        mcol_reader[0] = rec.v("ve")

    def emit_partial(se, g_, cc, first_of_se, via_scalar=False):
        w = g_ // 8
        buf = se % 2
        for sloc in range(3):
            rslot = (g_ * 3 + sloc) % 6
            wts = []
            if sloc == 0:
                wts = poolB[7] + [("ve", max(ring_ve[(g_ * 3 + s) % 6] for s in range(3)))]
                if first_of_se:
                    wts.append(("ve", yevac_ve[se]))
            E("tensor", (lambda sloc, rslot, w, cc, buf: lambda t: t.matmul(
                yps[7][:],
                st_ring[:, rslot * 128:(rslot + 1) * 128],
                y_selE[buf][:, (w * 3 + sloc) * C + cc * 512:(w * 3 + sloc) * C + cc * 512 + 512],
                start=(sloc == 0), stop=(sloc == 2),
                skip_group_check=True))(sloc, rslot, w, cc, buf),
              wait=wts, inc=([("pe", 1)] if sloc == 2 else []))
        pe_p = rec.v("pe")
        if cc == 1:
            for sloc in range(3):
                ring_free[(g_ * 3 + sloc) % 6] = ("pe", pe_p)
        cmb = comb[:, NE * g_ + se:NE * g_ + se + 1]
        dst = y_half[:, g_ * C + cc * 512:g_ * C + cc * 512 + 512]
        if se == 0:
            E("vector", (lambda dst, cmb: lambda v: v.tensor_scalar(
                dst, yps[7][:], cmb, None, ALU.mult))(dst, cmb),
              wait=[("pe", pe_p)], inc=[("ve", 1)])
        elif via_scalar:
            pb = ptmp[cc]
            E("scalar", (lambda pb, cmb: lambda s: s.mul(pb[:], yps[7][:], cmb))(pb, cmb),
              wait=[("pe", pe_p)] + ptmp_free[cc], inc=[("ac", 1)])
            ac_pm = rec.v("ac")
            E("vector", (lambda dst, pb: lambda v: v.tensor_tensor(
                dst, pb[:], dst, ALU.add))(dst, pb),
              wait=[("ac", ac_pm)], inc=[("ve", 1)])
            ptmp_free[cc] = [("ve", rec.v("ve"))]
            poolB[7] = [("ac", ac_pm)]
            return rec.v("ve")
        else:
            E("vector", (lambda dst, cmb: lambda v: v.scalar_tensor_tensor(
                dst, yps[7][:], cmb, dst, ALU.mult, ALU.add))(dst, cmb),
              wait=[("pe", pe_p)], inc=[("ve", 1)])
        poolB[7] = [("ve", rec.v("ve"))]
        return rec.v("ve")

    stt_done = {}

    xres_loaded = {}
    xres_free = [[], []]
    out_free = [("so0", rec.v("so0")), ("so1", rec.v("so1"))]
    gtmp_free = [[], []]
    stmp_free = [[], []]

    def load_xres(g_):
        buf = g_ % 2
        sem = "sxr0" if buf == 0 else "sxr1"
        E("sync", (lambda g_, buf: lambda e: e.dma_start(
            out=xres[buf][:], in_=x[512 * g_:512 * g_ + 128, :]))(g_, buf),
            wait=xres_free[buf], inc=[(sem, 16)])
        xres_loaded[g_] = (sem, rec.v(sem))

    def emit_combine(g_):
        gtmp = gtmp2[g_ % 2]
        stmp = stmp2[g_ % 2]
        pe_sv = [0, 0]
        for cc in range(2):
            bank = (g_ % 2) * 2 + cc
            for dh in range(2):
                wts = []
                if dh == 0:
                    wts = poolB[bank]
                E("tensor", (lambda g_, cc, dh, bank: lambda t: t.matmul(
                    yps[bank][:],
                    ctxT[:, M * dh + 128 * g_:M * dh + 128 * (g_ + 1)],
                    wvup_sb[:, C * dh + 512 * cc:C * dh + 512 * (cc + 1)],
                    start=(dh == 0), stop=(dh == 1),
                    skip_group_check=True))(g_, cc, dh, bank),
                  wait=wts, inc=([("pe", 1)] if dh == 1 else []))
            pe_sv[cc] = rec.v("pe")
        for cc in range(2):
            E("scalar", (lambda g_, cc, gtmp: lambda s: s.activation(
                gtmp[:, 512 * cc:512 * (cc + 1)],
                y_half[:, g_ * C + cc * 512:g_ * C + cc * 512 + 512], AF.Silu))(g_, cc, gtmp),
              wait=[("ve", stt_done[g_])] + (gtmp_free[g_ % 2] if cc == 0 else []), inc=[("ac", 1)])
        ac_g = rec.v("ac")
        for cc in range(2):
            bank = (g_ % 2) * 2 + cc
            E("vector", (lambda g_, cc, bank, stmp, gtmp: lambda v: v.scalar_tensor_tensor(
                stmp[:, 512 * cc:512 * (cc + 1)], yps[bank][:],
                rinv_all[:, g_:g_ + 1], gtmp[:, 512 * cc:512 * (cc + 1)],
                ALU.mult, ALU.mult))(g_, cc, bank, stmp, gtmp),
              wait=[("pe", pe_sv[cc]), ("ve", rinv_ve[g_]), ("ac", ac_g)]
                   + (stmp_free[g_ % 2] if cc == 0 else []),
              inc=[("ve", 1)])
            poolB[bank] = [("ve", rec.v("ve"))]
        v_t1 = rec.v("ve")
        VF()
        for cc in range(2):
            E("vector", (lambda g_, cc, stmp, gtmp: lambda v: v.tensor_add(
                gtmp[:, 512 * cc:512 * (cc + 1)], stmp[:, 512 * cc:512 * (cc + 1)],
                xres[g_ % 2][:, 512 * cc:512 * (cc + 1)]))(g_, cc, stmp, gtmp),
              wait=[("ve", v_t1), xres_loaded[g_]] if cc == 0 else [],
              inc=[("ve", 1)])
        stmp_free[g_ % 2] = [("ve", rec.v("ve"))]
        ve_out = rec.v("ve")
        xres_free[g_ % 2] = [("ve", ve_out)]
        if g_ + 2 < NG:
            load_xres(g_ + 2)
        sem = "so0" if g_ % 2 == 0 else "so1"
        E("sync", (lambda g_, gtmp: lambda e: e.dma_start(
            out=out[128 * g_:128 * (g_ + 1), :], in_=gtmp[:]))(g_, gtmp),
          wait=[("ve", ve_out)], inc=[(sem, 16)])
        out_free[g_ % 2] = (sem, rec.v(sem))
        gtmp_free[g_ % 2] = [(sem, rec.v(sem))]

    ve_s8 = emit_s8_builds(0, 0)
    for idx in range(16):
        if idx == 8:
            emit_gather_tls(0, 0, 6)
            ve_s8 = emit_s8_builds(0, 1)
        emit_gather_tile(0, idx, 6 + idx % 2, ve_s8)
    emit_gather_tls(0, 1, 6)
    gather_done_ve[0] = rec.v("ve")

    for e_ in range(NE):
        for ch in range(W1CH):
            emit_h_chunk(e_, ch)
            if ch >= 1:
                emit_y_half(e_, 0, ch - 1, 0)
                emit_y_half(e_, 0, ch - 1, 1)
        emit_y_half(e_, 0, W1CH - 1, 0)
        emit_y_half(e_, 0, W1CH - 1, 1)
        emit_y_evacs(e_, 0)
        se = e_ - 1
        ve_s8 = None
        if se >= 0:
            emit_eq_tl(se)
        for j in range(W2CH):
            if e_ + 1 < NE and j in (0, 8):
                ve_s8 = emit_s8_builds(e_ + 1, j // 8)
            if se >= 0:
                emit_st_one(se, j, 0)
                emit_st_one(se, j, 1)
                emit_st_one(se, j, 2)
            emit_y_half(e_, 1, j, 0)
            if se >= 0 and j >= 1:
                emit_partial(se, j - 1, 0, first_of_se=(j == 1))
            emit_y_half(e_, 1, j, 1)
            if e_ + 1 < NE:
                emit_gather_tile(e_ + 1, j, 6, ve_s8)
                if j in (7, 15):
                    emit_gather_tls(e_ + 1, j // 8, 6)
            if se >= 0 and j >= 1:
                emit_partial(se, j - 1, 1, False)
        if e_ + 1 < NE:
            gather_done_ve[e_ + 1] = rec.v("ve")
        emit_y_evacs(e_, 1)
        if se >= 0:
            emit_partial(se, W2CH - 1, 0, first_of_se=False)
            emit_partial(se, W2CH - 1, 1, False)

    es_g.close()
    es_t = ExitStack()
    gtmp2 = [es_t.enter_context(nc.sbuf_tensor(f"gtmpB{i}", [128, C], F32)) for i in range(2)]
    stmp2 = [es_t.enter_context(nc.sbuf_tensor(f"stmpB{i}", [128, C], F32)) for i in range(2)]
    xres = [es_t.enter_context(nc.sbuf_tensor(f"xresB{i}", [128, C], F32)) for i in range(2)]
    ptmp = [es_t.enter_context(nc.sbuf_tensor(f"ptmpB{i}", [128, 512], BF16)) for i in range(2)]
    ptmp_free = [[], []]
    load_xres(0)
    load_xres(1)
    se = NE - 1
    emit_eq_tl(se)
    for sl in range(3):
        emit_st_one(se, 0, sl)
    for g_ in range(NG):
        if g_ + 1 < NG:
            for sl in range(3):
                emit_st_one(se, g_ + 1, sl)
        emit_partial(se, g_, 0, first_of_se=(g_ == 0), via_scalar=True)
        if g_ >= 1:
            emit_combine(g_ - 1)
        stt_done[g_] = emit_partial(se, g_, 1, False, via_scalar=True)
    emit_combine(NG - 1)

    if debug:
        E("gpsimd", lambda g: g.dma_start(out=d_yh[:], in_=y_half[:]),
          wait=[("ve", rec.v("ve"))], inc=[("so0", 16)])
        E("gpsimd", lambda g: g.dma_start(out=d_xsel[:], in_=xn_sel[:]), inc=[("so0", 16)])
    E("sync", None, wait=[("so0", rec.v("so0")), ("so1", rec.v("so1"))])
    replay(nc, rec, sems)
    es_t.close()
    esB.close()
    es0.close()
    return nc


def _perm_and_mask(cm):
    own = [r for r in range(512) if r % 4 == cm]
    rest = [r for r in range(512) if r % 4 != cm]
    perm = np.array(own + rest, dtype=np.int64)
    p = np.arange(128)[:, None]
    orig = perm[None, :]
    dmask = np.where(orig <= 4 * p + cm, 0.0, NEG).astype(np.float32)
    dm = np.zeros((128, 2048), dtype=np.float32)
    for j in range(4):
        dT = dmask[:, 128 * j:128 * (j + 1)].T
        dm[:, 256 * j:256 * j + 128] = dT
        dm[:, 256 * (4 + j) + 128:256 * (4 + j) + 256] = dT
        dm[:, 256 * (4 + j):256 * (4 + j) + 128] = NEG
    return perm, dm


def kernel(L0_attn_inputs, L0_attn_Wq, L0_attn_Wk, L0_attn_Wvdown, L0_attn_Wvup,
           L0_ffn_router_w, L0_ffn_w1, L0_ffn_w2):
    x = np.asarray(L0_attn_inputs, dtype=np.float32)
    wq = np.ascontiguousarray(np.asarray(L0_attn_Wq, dtype=np.float32))
    wk = np.ascontiguousarray(np.asarray(L0_attn_Wk, dtype=np.float32))
    wvd = np.ascontiguousarray(np.asarray(L0_attn_Wvdown, dtype=np.float32))
    wvup = np.ascontiguousarray(np.asarray(L0_attn_Wvup, dtype=np.float32))
    rwT = np.ascontiguousarray(np.asarray(L0_ffn_router_w, dtype=np.float32).T)
    w1 = np.ascontiguousarray(np.asarray(L0_ffn_w1, dtype=np.float32))
    w2 = np.ascontiguousarray(np.asarray(L0_ffn_w2, dtype=np.float32))
    ident = np.eye(128, dtype=np.float32)
    ltri = np.triu(np.ones((128, 128), dtype=np.float32))
    ii = np.arange(128)
    gi, ei = ii // 8, ii % 8
    gw_m = ((ei[:, None] == ei[None, :]) & ((gi[:, None] // 8) == (gi[None, :] // 8))
            & (gi[:, None] < gi[None, :])).astype(np.float32)
    iota = np.tile(np.arange(384, dtype=np.float32), (128, 1))
    tfeat = np.zeros((128, 32), dtype=np.float32)
    tfeat[:, 0::2] = np.arange(128, dtype=np.float32)[:, None]
    tfeat[:, 1::2] = np.arange(16, dtype=np.float32)[None, :]

    nc = build_nc(NG=16, NE=8, H=4096)
    in_maps = []
    for c in range(8):
        b, cm = c // 4, c % 4
        perm, dmask = _perm_and_mask(cm)
        xb = np.ascontiguousarray(
            x[b].reshape(16, 512, 1024)[:, perm, :].reshape(8192, 1024))
        in_maps.append({"x": xb, "wq": wq, "wk": wk, "wvd": wvd, "wvup": wvup,
                        "rwT": rwT, "w1": w1, "w2": w2,
                        "ident_in": ident, "dmask_in": dmask,
                        "ltri_in": ltri, "gw_in": gw_m, "iota_in": iota,
                        "tfeat_in": tfeat})
    res = run_bass_kernel_spmd(nc, in_maps, core_ids=list(range(8)))
    out = np.empty((2, 8192, 1024), dtype=np.float32)
    for c in range(8):
        b, cm = c // 4, c % 4
        out[b, cm::4, :] = res.results[c]["out"]
    return out


# version 42
# speedup vs baseline: 1.1868x; 1.1868x over previous
import numpy as np
from contextlib import ExitStack

import concourse.bass as bass
import concourse.mybir as mybir
from concourse.bass_utils import run_bass_kernel_spmd

F32 = mybir.dt.float32
BF16 = mybir.dt.bfloat16
AF = mybir.ActivationFunctionType
ALU = mybir.AluOpType
AX = mybir.AxisListType

NEG = -3.0e38


class Rec:

    def __init__(self, cnt=None):
        self.ops = []
        self.cnt = dict(cnt or {})

    def E(self, eng, fn, wait=(), inc=()):
        w = tuple((s, v) for (s, v) in wait if v > 0)
        vals = []
        for s, a in inc:
            self.cnt[s] = self.cnt.get(s, 0) + a
            vals.append((s, a))
        self.ops.append((eng, w, fn, tuple(vals)))

    def v(self, s):
        return self.cnt.get(s, 0)


def replay(nc, rec, sems):
    engs = [e for e in ("sync", "gpsimd", "tensor", "vector", "scalar")
            if any(op[0] == e for op in rec.ops)]
    with nc.Block() as block:
        def mk(ename):
            ops = [op for op in rec.ops if op[0] == ename]

            def emit(eng):
                for _, waits, fn, incs in ops:
                    for (s, v) in waits:
                        eng.wait_ge(sems[s], v)
                    if fn is not None:
                        inst = fn(eng)
                        for (s, a) in incs:
                            inst.then_inc(sems[s], a)
            emit.__name__ = "_" + ename
            return emit

        for e in engs:
            getattr(block, e)(mk(e))


def build_nc(NG=16, NE=8, H=4096, debug=False):
    C, D = 1024, 256
    M = 128 * NG
    KEYS = 512 * NG
    MH = M // 2
    NC8 = C // 128
    NH8 = H // 128
    NHC = H // 512
    H8G = min(8, NH8)
    NHG = NH8 // H8G
    TGW = min(512, MH)
    NTG = MH // TGW
    NTP = MH // 128

    nc = bass.Bass("TRN2", target_bir_lowering=False)

    x = nc.dram_tensor("x", [KEYS, C], F32, kind="ExternalInput")
    wq = nc.dram_tensor("wq", [C, D], F32, kind="ExternalInput")
    wk = nc.dram_tensor("wk", [C, D], F32, kind="ExternalInput")
    wvd = nc.dram_tensor("wvd", [C, D], F32, kind="ExternalInput")
    wvup = nc.dram_tensor("wvup", [D, C], F32, kind="ExternalInput")
    rwT = nc.dram_tensor("rwT", [C, NE], F32, kind="ExternalInput")
    w1 = nc.dram_tensor("w1", [NE, C, H], F32, kind="ExternalInput")
    w2 = nc.dram_tensor("w2", [NE, H, C], F32, kind="ExternalInput")
    ident_in = nc.dram_tensor("ident_in", [128, 128], F32, kind="ExternalInput")
    dmask_in = nc.dram_tensor("dmask_in", [128, 2048], F32, kind="ExternalInput")
    ltri_in = nc.dram_tensor("ltri_in", [128, 128], F32, kind="ExternalInput")
    gw_in = nc.dram_tensor("gw_in", [128, 128], F32, kind="ExternalInput")
    iota_in = nc.dram_tensor("iota_in", [128, 384], F32, kind="ExternalInput")
    tfeat_in = nc.dram_tensor("tfeat_in", [128, 2 * NG], F32, kind="ExternalInput")
    xnt_dram = nc.dram_tensor("xnt_scratch", [128, NG * C], BF16, kind="ExternalOutput")
    out = nc.dram_tensor("out", [M, C], F32, kind="ExternalOutput")
    if debug:
        d_xnT = nc.dram_tensor("d_xnT", [128, NC8 * M], F32, kind="ExternalOutput")
        d_KT = nc.dram_tensor("d_KT", [128, 2 * KEYS], F32, kind="ExternalOutput")
        d_Vd = nc.dram_tensor("d_Vd", [128, (KEYS // 128) * D], F32, kind="ExternalOutput")
        d_QT = nc.dram_tensor("d_QT", [128, 2 * M], F32, kind="ExternalOutput")
        d_P = nc.dram_tensor("d_P", [128, KEYS], F32, kind="ExternalOutput")
        d_comb = nc.dram_tensor("d_comb", [128, NG * NE], F32, kind="ExternalOutput")
        d_r = nc.dram_tensor("d_r", [128, NG], F32, kind="ExternalOutput")
        d_ctxT = nc.dram_tensor("d_ctxT", [128, 2 * M], F32, kind="ExternalOutput")
        d_y = nc.dram_tensor("d_y", [128, NTP * C], F32, kind="ExternalOutput")
        dbg_sb = None

    es0 = ExitStack()
    sems = {}
    for s in ("sxa", "sxb", "sxc", "sxd", "ss", "si", "swa", "swb", "swc", "swd",
              "swe", "swf", "swg", "swh", "sxt",
              "sxr0", "sxr1", "so0", "so1", "pe", "ve", "ac"):
        sems[s] = es0.enter_context(nc.semaphore(s))

    ctxT = es0.enter_context(nc.sbuf_tensor("ctxT", [128, 2 * M], BF16))
    comb = es0.enter_context(nc.sbuf_tensor("comb", [128, NG * NE], F32))
    mask_f32 = es0.enter_context(nc.sbuf_tensor("mask_f32", [128, NG * NE], F32))
    wvup_sb = es0.enter_context(nc.sbuf_tensor("wvup_sb", [128, 2 * C], BF16))
    rinv_all = es0.enter_context(nc.sbuf_tensor("rinv_all", [128, NG], F32))
    r_all = es0.enter_context(nc.sbuf_tensor("r_all", [128, NG], F32))
    ident = es0.enter_context(nc.sbuf_tensor("ident", [128, 128], F32))
    fcol = es0.enter_context(nc.sbuf_tensor("fcol", [128, 64], F32))
    _vfk = [0]

    def VF():
        k = _vfk[0] % 64
        _vfk[0] += 1
        rec.E("vector", (lambda k: lambda e: e.memset(fcol[:, k:k + 1], 0.0))(k))

    rec = Rec()
    E = rec.E

    esA = ExitStack()
    xnT_own = esA.enter_context(nc.sbuf_tensor("xnT_own", [128, NC8 * M], BF16))
    x_st = [esA.enter_context(nc.sbuf_tensor(f"x_st{i}", [128, 2 * C], F32)) for i in range(4)]
    stats = esA.enter_context(nc.sbuf_tensor("stats", [128, 192], F32))
    xn_f32 = [esA.enter_context(nc.sbuf_tensor(f"xn_f32{i}", [128, C], BF16)) for i in range(8)]
    ident_bf = esA.enter_context(nc.sbuf_tensor("ident_bfA", [128, 128], BF16))
    xnT_g = esA.enter_context(nc.sbuf_tensor("xnT_g", [128, NC8 * 512], BF16))
    KT = esA.enter_context(nc.sbuf_tensor("KT", [128, 2 * KEYS], BF16))
    Vd = esA.enter_context(nc.sbuf_tensor("Vd", [128, (KEYS // 128) * D], BF16))
    QT = esA.enter_context(nc.sbuf_tensor("QT", [128, 2 * M], BF16))
    STm_sb = [esA.enter_context(nc.sbuf_tensor(f"STm_sb{i}", [128, 256], F32)) for i in range(2)]
    kt2_sb = esA.enter_context(nc.sbuf_tensor("kt2_sb", [128, 2 * 512], BF16))
    qt2_sb = esA.enter_context(nc.sbuf_tensor("qt2_sb", [128, 2 * 256], BF16))
    ones_bf = esA.enter_context(nc.sbuf_tensor("ones_bf", [128, 1], BF16))
    bnd = esA.enter_context(nc.sbuf_tensor("bnd", [128, 16], F32))
    negm_sb = esA.enter_context(nc.sbuf_tensor("negm_sb", [128, 1], F32))
    mk2_bf = esA.enter_context(nc.sbuf_tensor("mk2_bf", [1, 4], F32))
    nm_bf = esA.enter_context(nc.sbuf_tensor("nm_bf", [1, 2], BF16))
    ones_row = esA.enter_context(nc.sbuf_tensor("ones_row", [1, 128], BF16))
    one_f32 = esA.enter_context(nc.sbuf_tensor("one_f32", [1, 1], F32))
    r_row = esA.enter_context(nc.sbuf_tensor("r_row", [1, 256], F32))
    wq_sb = esA.enter_context(nc.sbuf_tensor("wq_sb", [128, NC8 * D], BF16))
    wk_sb = esA.enter_context(nc.sbuf_tensor("wk_sb", [128, NC8 * D], BF16))
    wvd_sb = esA.enter_context(nc.sbuf_tensor("wvd_sb", [128, NC8 * D], BF16))
    rwT_sb = esA.enter_context(nc.sbuf_tensor("rwT_sb", [128, NC8 * NE], BF16))
    dmask = esA.enter_context(nc.sbuf_tensor("dmask", [128, 2048], F32))
    pt_sb = [esA.enter_context(nc.sbuf_tensor(f"pt_sb{i}", [128, 256], BF16)) for i in range(4)]

    lg_sb = esA.enter_context(nc.sbuf_tensor("lg_sb", [128, 4 * NE + 8], F32))
    pool = [esA.enter_context(nc.psum_tensor(f"pool{i}", [128, 512], F32)) for i in range(8)]

    E("gpsimd", lambda g: g.dma_start(out=ident[:], in_=ident_in[:]), inc=[("si", 16)])
    E("gpsimd", lambda g: g.dma_start(
        out=wq_sb.rearrange("p (c d) -> p c d", c=NC8),
        in_=wq.rearrange("(c p) d -> p c d", p=128)), inc=[("ss", 16)])
    E("gpsimd", lambda g: g.dma_start(
        out=wk_sb.rearrange("p (c d) -> p c d", c=NC8),
        in_=wk.rearrange("(c p) d -> p c d", p=128)), inc=[("ss", 16)])
    E("gpsimd", lambda g: g.dma_start(
        out=wvd_sb.rearrange("p (c d) -> p c d", c=NC8),
        in_=wvd.rearrange("(c p) d -> p c d", p=128)), inc=[("ss", 16)])
    E("gpsimd", lambda g: g.dma_start(
        out=wvup_sb.rearrange("p (h c) -> p h c", h=2),
        in_=wvup.rearrange("(h p) c -> p h c", p=128)), inc=[("ss", 16)])
    E("gpsimd", lambda g: g.dma_start(
        out=rwT_sb.rearrange("p (c n) -> p c n", c=NC8),
        in_=rwT.rearrange("(c p) n -> p c n", p=128)), inc=[("ss", 16)])
    E("gpsimd", lambda g: g.dma_start(out=dmask[:], in_=dmask_in[:]), inc=[("ss", 16)])
    SW_SMALL = ("ss", rec.v("ss"))

    z = ("ve", 0)
    pool_free = [z] * 8
    ptb_free = [("pe", 0)] * 4
    xn_free = [("pe", 0)] * 8
    tr_rr = [0]

    NU = KEYS // 256
    x_loaded = [None] * NU
    x_consumed = [None] * NU

    def load_x_unit(u):
        waits = []
        if u >= 4:
            waits.append(("ve", x_consumed[u - 4]))
        sem = ("sxa", "sxb", "sxc", "sxd")[u % 4]
        q = "sync" if u % 2 == 0 else "scalar"
        E(q, (lambda u: lambda e: e.dma_start(
            out=x_st[u % 4].rearrange("p (t e) -> p t e", t=2),
            in_=x[256 * u:256 * (u + 1), :].rearrange("(t p) e -> p t e", p=128)))(u),
            wait=waits, inc=[(sem, 16)])
        x_loaded[u] = (sem, rec.v(sem))

    for _u in range(min(4, NU)):
        load_x_unit(_u)

    E("vector", lambda e: e.tensor_copy(ident_bf[:], ident[:]), wait=[("si", 16)],
      inc=[("ve", 1)])
    E("vector", lambda e: e.memset(ones_bf[:], 1.0), inc=[("ve", 1)])
    E("vector", lambda e: e.memset(ones_row[:], 1.0), inc=[("ve", 1)])
    E("vector", lambda e: e.memset(one_f32[:], 1.0), inc=[("ve", 1)])
    E("vector", lambda e: e.memset(mk2_bf[:, 0:1], 0.0), inc=[("ve", 1)])
    ve_ones = rec.v("ve")

    prev_ve_xn = [0, 0]
    prev_ac_rstd = [0, 0]

    ac_exp_prev = 0
    pe_q2_prev = 0
    pe_k2_prev = 0
    nm_prev = 0
    rrow_prev = [0]
    ac_bnd_prev = [0]
    mq2_state = {"ve": 0}
    mk2_prev = [0]
    rinv_ve = []
    ln_state = {}
    xn_state = {}
    mk2_state = {"ve": ve_ones}

    def emit_lnstats(g):
        p2 = g % 2
        base = 96 * p2
        gst = stats[:, base:base + 96]
        xts = []
        for ts in range(4):
            ptg = 4 * g + ts
            u, loc = ptg // 2, ptg % 2
            x_t = x_st[u % 4][:, loc * C:(loc + 1) * C]
            xts.append((u, x_t))
            E("vector", (lambda x_t, gst, ts: lambda e: e.bn_stats(
                gst[:, 16 * ts:16 * ts + 6], x_t[:, 0:512]))(x_t, gst, ts),
              wait=[x_loaded[u]], inc=[("ve", 1)])
            E("vector", (lambda x_t, gst, ts: lambda e: e.bn_stats(
                gst[:, 16 * ts + 6:16 * ts + 12], x_t[:, 512:1024]))(x_t, gst, ts),
              inc=[("ve", 1)])
        ve_stats = rec.v("ve")
        for ts in range(4):
            out_ap = bass.AP(stats, base + 64 + ts, [[192, 128], [4, 2]])
            E("vector", (lambda gst, ts, out_ap: lambda e: e.bn_aggr(
                out_ap, gst[:, 16 * ts:16 * ts + 12]))(gst, ts, out_ap),
              wait=([("ve", ve_stats), ("ac", prev_ac_rstd[p2])] if ts == 0 else []),
              inc=[("ve", 1)])
        ve_agg = rec.v("ve")
        ln_state[g] = (gst, xts, ve_agg)

    def emit_lnfin(g):
        p2 = g % 2
        gst, xts, ve_agg = ln_state.pop(g)
        VF()
        E("vector", (lambda gst: lambda e: e.tensor_scalar(
            gst[:, 68:72], gst[:, 68:72], 1e-5, None, ALU.add))(gst),
          wait=[("ve", ve_agg)], inc=[("ve", 1)])
        ve_c1 = rec.v("ve")
        E("scalar", (lambda gst: lambda e: e.activation(
            gst[:, 72:76], gst[:, 68:72], AF.Ln))(gst),
          wait=[("ve", ve_c1), ("ve", prev_ve_xn[p2])], inc=[("ac", 1)])
        ac_ln = rec.v("ac")
        E("scalar", (lambda gst: lambda e: e.activation(
            gst[:, 76:80], gst[:, 72:76], AF.Exp, scale=-0.5))(gst),
          wait=[("ac", ac_ln)], inc=[("ac", 1)])
        ac_rstd = rec.v("ac")
        prev_ac_rstd[p2] = ac_rstd
        ve_xns = []
        for ts in range(4):
            u, x_t = xts[ts]
            ti = 4 * (g % 2) + ts
            w_ts = [("ac", ac_rstd), ("ve", ve_agg), xn_free[ti]]
            if ts == 0:
                w_ts.append(("sxt", 16 * g))
            E("vector", (lambda x_t, gst, ts, ti: lambda e: e.tensor_scalar(
                xn_f32[ti][:], x_t, gst[:, 64 + ts:65 + ts], gst[:, 76 + ts:77 + ts],
                ALU.subtract, ALU.mult))(x_t, gst, ts, ti),
              wait=w_ts, inc=[("ve", 1)])
            ve_xn = rec.v("ve")
            ve_xns.append(ve_xn)
            prev_ve_xn[p2] = ve_xn
            if ts == 0:
                E("gpsimd", (lambda g, ti: lambda e: e.dma_start(
                    out=xnt_dram[:, C * g:C * (g + 1)], in_=xn_f32[ti][:]))(g, ti),
                  wait=[("ve", ve_xn)], inc=[("sxt", 16)])
            if ts % 2 == 1:
                x_consumed[u] = ve_xn
                if u + 4 < NU:
                    load_x_unit(u + 4)
        xn_state[g] = ve_xns

    tr_fillers = []
    xn_edges = {}
    tr_bank = [0]

    def queue_xn(g):
        ve_xns = xn_state.pop(g)
        items = []
        for ts in range(4):
            for c in range(NC8):
                items.append((g, ts, c, ve_xns[ts]))
        tr_fillers.extend(items)

    def _emit_tr(g, ts, c, ve_xn):
        k = 6 + (tr_bank[0] % 2)
        tr_bank[0] += 1
        w = [("ve", ve_xn), pool_free[k]]
        if g == 0 and ts == 0 and c == 0:
            w.append(SW_SMALL)
            w.append(("si", 16))
        E("tensor", (lambda ts, c, k, ti: lambda e: e.matmul(
            pool[k][:, 0:128], xn_f32[ti][:, c * 128:(c + 1) * 128],
            ident_bf[:], start=True, stop=True,
            skip_group_check=True))(ts, c, k, 4 * (g % 2) + ts),
          wait=w, inc=[("pe", 1)])
        pe_tr = rec.v("pe")
        dst = (lambda c, ts: lambda: xnT_g[:, 512 * c + 128 * ts:512 * c + 128 * (ts + 1)])(c, ts)
        if k % 2 == 0:
            E("vector", (lambda k, dst: lambda e: e.tensor_copy(dst(), pool[k][:, 0:128]))(k, dst),
              wait=[("pe", pe_tr)], inc=[("ve", 1)])
            pool_free[k] = ("ve", rec.v("ve"))
        else:
            E("scalar", (lambda k, dst: lambda e: e.copy(dst(), pool[k][:, 0:128]))(k, dst),
              wait=[("pe", pe_tr)], inc=[("ac", 1)])
            pool_free[k] = ("ac", rec.v("ac"))
        if ts == 3 and c == NC8 - 1:
            edges = [("ve", rec.v("ve")), ("ac", rec.v("ac"))]
            xn_edges[g] = edges
            for t_ in range(4):
                xn_free[4 * (g % 2) + t_] = ("pe", pe_tr)

    def drain_tr(n):
        while n > 0 and tr_fillers:
            _emit_tr(*tr_fillers.pop(0))
            n -= 1

    def emit_xn(g):
        while g not in xn_edges:
            assert tr_fillers
            _emit_tr(*tr_fillers.pop(0))
        return xn_edges[g]

    def emit_proj(g, g_done, with_bound=False):
        nonlocal pe_k2_prev
        if with_bound and g >= 2:
            emit_bound(g // 2)
        for c in range(NC8):
            E("vector", (lambda c, g: lambda e: e.tensor_copy(
                xnT_own[:, M * c + 128 * g:M * c + 128 * (g + 1)],
                xnT_g[:, 512 * c:512 * c + 128]))(c, g),
              wait=(g_done if c == 0 else []), inc=[("ve", 1)])
        ve_own = rec.v("ve")
        kt_banks = [7, 6]
        for dh in range(2):
            pb = kt_banks[dh]
            for c in range(NC8):
                E("tensor", (lambda dh, c, pb: lambda e: e.matmul(
                    pool[pb][:], wk_sb[:, D * c + 128 * dh:D * c + 128 * dh + 128],
                    xnT_g[:, 512 * c:512 * (c + 1)],
                    start=(c == 0), stop=(c == NC8 - 1)))(dh, c, pb),
                  wait=(g_done + [pool_free[pb]] if c == 0 else []),
                  inc=([("pe", 1)] if c == NC8 - 1 else []))
            pe_kt = rec.v("pe")
            if dh == 0:
                E("vector", (lambda dh, g, pb: lambda e: e.tensor_copy(
                    KT[:, KEYS * dh + 512 * g:KEYS * dh + 512 * (g + 1)], pool[pb][:]))(dh, g, pb),
                  wait=[("pe", pe_kt)], inc=[("ve", 1)])
                pool_free[pb] = ("ve", rec.v("ve"))
            else:
                E("scalar", (lambda dh, g, pb: lambda e: e.copy(
                    KT[:, KEYS * dh + 512 * g:KEYS * dh + 512 * (g + 1)], pool[pb][:]))(dh, g, pb),
                  wait=[("pe", pe_kt)], inc=[("ac", 1)])
                pool_free[pb] = ("ac", rec.v("ac"))
        ve_kt = rec.v("ve")
        ac_kt = rec.v("ac")
        E("scalar", (lambda g: lambda e: e.activation(
            kt2_sb[:, 0:512], KT[:, 512 * g:512 * (g + 1)], AF.Square))(g),
          wait=[("ve", ve_kt), ("pe", pe_k2_prev)], inc=[("ac", 1)])
        E("scalar", (lambda g: lambda e: e.activation(
            kt2_sb[:, 512:1024], KT[:, KEYS + 512 * g:KEYS + 512 * (g + 1)], AF.Square))(g),
          wait=[("ac", ac_kt)], inc=[("ac", 1)])
        ac_kt2 = rec.v("ac")
        for dh in range(2):
            pb = 6 + dh
            for c in range(NC8):
                E("tensor", (lambda dh, c, g, pb: lambda e: e.matmul(
                    pool[pb][:, 0:128], wq_sb[:, D * c + 128 * dh:D * c + 128 * dh + 128],
                    xnT_own[:, M * c + 128 * g:M * c + 128 * (g + 1)],
                    start=(c == 0), stop=(c == NC8 - 1)))(dh, c, g, pb),
                  wait=([("ve", ve_own), pool_free[pb]] if c == 0 else []),
                  inc=([("pe", 1)] if c == NC8 - 1 else []))
            pe_qt = rec.v("pe")
            if dh == 0:
                E("vector", (lambda dh, g, pb: lambda e: e.tensor_copy(
                    QT[:, M * dh + 128 * g:M * dh + 128 * (g + 1)], pool[pb][:, 0:128]))(dh, g, pb),
                  wait=[("pe", pe_qt)], inc=[("ve", 1)])
                pool_free[pb] = ("ve", rec.v("ve"))
            else:
                E("scalar", (lambda dh, g, pb: lambda e: e.copy(
                    QT[:, M * dh + 128 * g:M * dh + 128 * (g + 1)], pool[pb][:, 0:128]))(dh, g, pb),
                  wait=[("pe", pe_qt)], inc=[("ac", 1)])
                pool_free[pb] = ("ac", rec.v("ac"))
        qt_ve, qt_ac = rec.v("ve"), rec.v("ac")
        if with_bound and g < 2:
            emit_bound(g // 2)
        for ts in range(4):
            pb = kt_banks[ts % 2]
            for c in range(NC8):
                E("tensor", (lambda ts, c, pb: lambda e: e.matmul(
                    pool[pb][:, 0:D], xnT_g[:, 512 * c + 128 * ts:512 * c + 128 * (ts + 1)],
                    wvd_sb[:, D * c:D * (c + 1)],
                    start=(c == 0), stop=(c == NC8 - 1)))(ts, c, pb),
                  wait=([pool_free[pb]] if c == 0 else []),
                  inc=([("pe", 1)] if c == NC8 - 1 else []))
            pe_vd = rec.v("pe")
            if ts % 2 == 0:
                E("vector", (lambda g, ts, pb: lambda e: e.tensor_copy(
                    Vd[:, (4 * g + ts) * D:(4 * g + ts + 1) * D], pool[pb][:, 0:D]))(g, ts, pb),
                  wait=[("pe", pe_vd)], inc=[("ve", 1)])
                pool_free[pb] = ("ve", rec.v("ve"))
            else:
                E("scalar", (lambda g, ts, pb: lambda e: e.copy(
                    Vd[:, (4 * g + ts) * D:(4 * g + ts + 1) * D], pool[pb][:, 0:D]))(g, ts, pb),
                  wait=[("pe", pe_vd)], inc=[("ac", 1)])
                pool_free[pb] = ("ac", rec.v("ac"))
        if with_bound and g >= 2:
            emit_q2_update(g // 2, qt_ve, qt_ac)
        for dh in range(2):
            E("tensor", (lambda dh: lambda e: e.matmul(
                pool[7][0:1, 0:512], ones_bf[:], kt2_sb[:, 512 * dh:512 * (dh + 1)],
                start=(dh == 0), stop=(dh == 1)))(dh),
              wait=([pool_free[7], ("ac", ac_kt2), ("ve", ve_ones)] if dh == 0 else []),
              inc=([("pe", 1)] if dh == 1 else []))
        pe_k2 = rec.v("pe")
        pe_k2_prev = pe_k2
        E("vector", lambda e: e.reduce_max(mk2_bf[0:1, 1:2], pool[7][0:1, 0:512], AX.X),
          wait=[("pe", pe_k2), ("ve", mk2_state["ve"])], inc=[("ve", 1)])
        v_k2r = rec.v("ve")
        pool_free[7] = ("ve", v_k2r)
        VF()
        E("vector", lambda e: e.tensor_max(mk2_bf[0:1, 0:1], mk2_bf[0:1, 0:1], mk2_bf[0:1, 1:2]),
          wait=[("ve", v_k2r)], inc=[("ve", 1)])
        mk2_state["ve"] = rec.v("ve")
        proj_state["ve_own"] = ve_own
        if with_bound:
            emit_bound_bcast()
        return (rec.v("ve"), rec.v("ac"))

    def emit_logits(g):
        ve_own = proj_state["ve_own"]
        for c in range(NC8):
            E("tensor", (lambda c, g: lambda e: e.matmul(
                pool[6][:, 0:NE], xnT_own[:, M * c + 128 * g:M * c + 128 * (g + 1)],
                rwT_sb[:, NE * c:NE * (c + 1)],
                start=(c == 0), stop=(c == NC8 - 1)))(c, g),
              wait=([("ve", ve_own), pool_free[6]] if c == 0 else []),
              inc=([("pe", 1)] if c == NC8 - 1 else []))
        pe_lg = rec.v("pe")
        LG = lg_sb[:, 0:NE]
        EE = lg_sb[:, NE:2 * NE]
        EQ1 = lg_sb[:, 2 * NE:3 * NE]
        PM = lg_sb[:, 3 * NE:4 * NE]
        c_m1 = lg_sb[:, 4 * NE + 2:4 * NE + 3]
        c_m2 = lg_sb[:, 4 * NE + 3:4 * NE + 4]
        c_ws = lg_sb[:, 4 * NE + 4:4 * NE + 5]
        c_wi = lg_sb[:, 4 * NE + 5:4 * NE + 6]
        c_nm = lg_sb[:, 4 * NE + 6:4 * NE + 7]
        E("vector", lambda e: e.tensor_copy(LG, pool[6][:, 0:NE]),
          wait=[("pe", pe_lg)], inc=[("ve", 1)])
        v_lgc = rec.v("ve")
        pool_free[6] = ("ve", v_lgc)
        VF()
        E("vector", lambda e: e.reduce_max(c_nm, LG, AX.X, negate=True),
          wait=[("ve", v_lgc)], inc=[("ve", 1)])
        ve_lg = rec.v("ve")
        E("scalar", lambda e: e.activation(EE, LG, AF.Exp, bias=c_nm), wait=[("ve", ve_lg)],
          inc=[("ac", 1)])
        ac_e = rec.v("ac")
        E("vector", lambda e: e.reduce_max(c_m1, EE, AX.X), wait=[("ac", ac_e)], inc=[("ve", 1)])
        v_m1 = rec.v("ve")
        VF()
        E("vector", lambda e: e.tensor_scalar(EQ1, EE, c_m1, None, ALU.is_equal),
          wait=[("ve", v_m1)], inc=[("ve", 1)])
        v_eq1 = rec.v("ve")
        VF()
        E("vector", lambda e: e.scalar_tensor_tensor(PM, EQ1, -1e30, EE, ALU.mult, ALU.add),
          wait=[("ve", v_eq1)], inc=[("ve", 1)])
        v_pm = rec.v("ve")
        VF()
        E("vector", lambda e: e.reduce_max(c_m2, PM, AX.X), wait=[("ve", v_pm)], inc=[("ve", 1)])
        v_m2 = rec.v("ve")
        VF()
        E("vector", lambda e: e.tensor_scalar(PM, PM, c_m2, None, ALU.is_equal),
          wait=[("ve", v_m2)], inc=[("ve", 1)])
        v_pm2 = rec.v("ve")
        E("vector", lambda e: e.tensor_add(c_ws, c_m1, c_m2), inc=[("ve", 1)])
        v_ws = rec.v("ve")
        E("vector", lambda e: e.tensor_add(EQ1, EQ1, PM), wait=[("ve", v_pm2)], inc=[("ve", 1)])
        v_eq = rec.v("ve")
        E("vector", lambda e: e.reciprocal(c_wi, c_ws), wait=[("ve", v_ws)], inc=[("ve", 1)])
        v_wi = rec.v("ve")
        E("vector", lambda e: e.tensor_mul(EE, EE, EQ1), wait=[("ve", v_eq)], inc=[("ve", 1)])
        v_ee = rec.v("ve")
        E("vector", (lambda g: lambda e: e.tensor_copy(
            mask_f32[:, NE * g:NE * (g + 1)], EQ1))(g),
          wait=[("ve", v_eq)], inc=[("ve", 1)])
        VF()
        E("vector", (lambda g: lambda e: e.tensor_scalar(
            comb[:, NE * g:NE * (g + 1)], EE, c_wi, None, ALU.mult))(g),
          wait=[("ve", v_ee), ("ve", v_wi)], inc=[("ve", 1)])

    proj_state = {}

    bound_state = {}

    def emit_bound(p):
        nonlocal pe_q2_prev, nm_prev
        ve_pre = rec.v("ve")
        ac_pre = rec.v("ac")
        if p == 0:
            emit_q2_update(p, ve_pre, ac_pre)
            E("vector", lambda e: e.tensor_copy(bnd[0:1, 0:1], mk2_bf[0:1, 3:4]),
              wait=[("ve", mq2_state["ve"])], inc=[("ve", 1)])
            v_q2m = rec.v("ve")
            VF()
        else:
            E("vector", lambda e: e.tensor_copy(bnd[0:1, 0:1], mk2_bf[0:1, 3:4]),
              wait=[("ve", mq2_state["ve"]), ("ac", ac_bnd_prev[0])], inc=[("ve", 1)])
            v_q2m = rec.v("ve")
            VF()
        E("scalar", lambda e: e.activation(bnd[0:1, 1:2], bnd[0:1, 0:1], AF.Ln),
          wait=[("ve", v_q2m)], inc=[("ac", 1)])
        E("scalar", lambda e: e.activation(bnd[0:1, 2:3], mk2_bf[0:1, 0:1], AF.Ln),
          wait=[("ve", mk2_state["ve"] if p == 0 else mk2_prev[0])], inc=[("ac", 1)])
        ac_ln2 = rec.v("ac")
        E("vector", lambda e: e.tensor_add(bnd[0:1, 3:4], bnd[0:1, 1:2], bnd[0:1, 2:3]),
          wait=[("ac", ac_ln2)], inc=[("ve", 1)])
        v_sum = rec.v("ve")
        E("scalar", lambda e: e.activation(bnd[0:1, 4:5], bnd[0:1, 3:4], AF.Exp, scale=0.5),
          wait=[("ve", v_sum)], inc=[("ac", 1)])
        ac_m = rec.v("ac")
        ac_bnd_prev[0] = ac_m
        scale = -1.01 / 16.0 if p == 0 else -1.30 / 16.0
        E("vector", (lambda scale: lambda e: e.tensor_scalar(
            bnd[0:1, 5:6], bnd[0:1, 4:5], scale, None, ALU.mult))(scale),
          wait=[("ac", ac_m)], inc=[("ve", 1)])
        v_nm = rec.v("ve")
        VF()
        E("vector", lambda e: e.tensor_copy(nm_bf[0:1, 0:1], bnd[0:1, 5:6]),
          wait=[("ve", v_nm), ("ve", nm_prev)], inc=[("ve", 1)])
        v_nmc = rec.v("ve")
        nm_prev = v_nmc
        bound_state["v_nmc"] = v_nmc

    def emit_bound_bcast():
        v_nmc = bound_state["v_nmc"]
        E("tensor", lambda e: e.matmul(
            pool[3][:, 0:1], ones_row[0:1, :], nm_bf[0:1, 0:1],
            start=True, stop=True),
          wait=[pool_free[3], ("ve", v_nmc), ("ve", ve_ones)], inc=[("pe", 1)])
        pe_bc = rec.v("pe")
        E("vector", lambda e: e.tensor_copy(negm_sb[:], pool[3][:, 0:1]),
          wait=[("pe", pe_bc), ("ac", ac_exp_prev)], inc=[("ve", 1)])
        bound_state["ve_negm"] = rec.v("ve")
        pool_free[3] = ("ve", bound_state["ve_negm"])

    def emit_q2_update(p, ve_pre, ac_pre):
        nonlocal pe_q2_prev
        E("scalar", (lambda p: lambda e: e.activation(
            qt2_sb[:, 0:256], QT[:, 256 * p:256 * (p + 1)], AF.Square))(p),
          wait=[("ve", ve_pre), ("ac", ac_pre), ("pe", pe_q2_prev)], inc=[("ac", 1)])
        E("scalar", (lambda p: lambda e: e.activation(
            qt2_sb[:, 256:512], QT[:, M + 256 * p:M + 256 * (p + 1)], AF.Square))(p),
          inc=[("ac", 1)])
        ac_qt2 = rec.v("ac")
        for dh in range(2):
            E("tensor", (lambda dh: lambda e: e.matmul(
                pool[6][0:1, 0:256], ones_bf[:], qt2_sb[:, 256 * dh:256 * (dh + 1)],
                start=(dh == 0), stop=(dh == 1)))(dh),
              wait=([pool_free[6], ("ac", ac_qt2)] if dh == 0 else []),
              inc=([("pe", 1)] if dh == 1 else []))
        pe_q2 = rec.v("pe")
        pe_q2_prev = pe_q2
        E("vector", lambda e: e.reduce_max(mk2_bf[0:1, 2:3], pool[6][0:1, 0:256], AX.X),
          wait=[("pe", pe_q2), ("ve", mq2_state["ve"])], inc=[("ve", 1)])
        v_q = rec.v("ve")
        pool_free[6] = ("ve", v_q)
        VF()
        if p == 0:
            E("vector", lambda e: e.tensor_copy(mk2_bf[0:1, 3:4], mk2_bf[0:1, 2:3]),
              wait=[("ve", v_q)], inc=[("ve", 1)])
        else:
            E("vector", lambda e: e.tensor_max(mk2_bf[0:1, 3:4], mk2_bf[0:1, 3:4],
                                               mk2_bf[0:1, 2:3]),
              wait=[("ve", v_q)], inc=[("ve", 1)])
        mq2_state["ve"] = rec.v("ve")

    ST_BANKS = (0, 1, 5)

    def make_attn_pair(p, pre):
        nkb = 8 * p + 8
        ve_pre, ac_pre = pre
        st_of = {}
        diag_ac = [ac_exp_prev, ac_exp_prev]
        state = {"kb": 0, "acs": {}, "primed": False, "pe_ctx": None, "ve_rr": None}

        def emit_st(kb):
            sb = ST_BANKS[kb % 3]
            w = []
            for dh in range(2):
                if dh == 0:
                    w = [pool_free[sb]]
                    if kb == 0:
                        w += [("ve", ve_pre), ("ac", ac_pre)]
                E("tensor", (lambda p, kb, dh, sb: lambda e: e.matmul(
                    pool[sb][:, 0:256],
                    KT[:, KEYS * dh + 128 * kb:KEYS * dh + 128 * (kb + 1)],
                    QT[:, M * dh + 256 * p:M * dh + 256 * (p + 1)],
                    start=(dh == 0), stop=(dh == 1),
                    skip_group_check=True))(p, kb, dh, sb),
                  wait=(w if dh == 0 else []), inc=([("pe", 1)] if dh == 1 else []))
            st_of[kb] = ("pe", rec.v("pe"))

        def emit_exp(kb):
            sb = ST_BANKS[kb % 3]
            pb = kb % 4
            j = kb - 8 * p
            if j >= 0:
                db = kb % 2
                E("vector", (lambda kb, sb, j, db: lambda e: e.tensor_add(
                    STm_sb[db][:], pool[sb][:, 0:256],
                    dmask[:, 256 * j:256 * (j + 1)]))(kb, sb, j, db),
                  wait=[st_of[kb], SW_SMALL, ("ac", diag_ac[db])], inc=[("ve", 1)])
                pool_free[sb] = ("ve", rec.v("ve"))
                E("scalar", (lambda pb, db: lambda e: e.activation(
                    pt_sb[pb][:], STm_sb[db][:], AF.Exp,
                    bias=negm_sb[:], scale=0.0625))(pb, db),
                  wait=[("ve", rec.v("ve")), ptb_free[pb]], inc=[("ac", 1)])
                diag_ac[db] = rec.v("ac")
            else:
                E("scalar", (lambda kb, sb, pb: lambda e: e.activation(
                    pt_sb[pb][:], pool[sb][:, 0:256], AF.Exp,
                    bias=negm_sb[:], scale=0.0625))(kb, sb, pb),
                  wait=[st_of[kb], ("ve", bound_state["ve_negm"]), ptb_free[pb]], inc=[("ac", 1)])
                pool_free[sb] = ("ac", rec.v("ac"))
            return rec.v("ac")

        def emit_rctx(kb, ac_p):
            pb = kb % 4
            E("tensor", (lambda kb, pb, nkb: lambda e: e.matmul(
                pool[4][0:1, 0:256], ones_bf[:], pt_sb[pb][:],
                start=(kb == 0), stop=(kb == nkb - 1),
                skip_group_check=True))(kb, pb, nkb),
              wait=[("ac", ac_p)] + ([pool_free[4]] if kb == 0 else []),
              inc=[])
            for dh in range(2):
                E("tensor", (lambda kb, dh, pb, nkb: lambda e: e.matmul(
                    pool[2 + dh][:, 0:256], Vd[:, D * kb + 128 * dh:D * kb + 128 * dh + 128],
                    pt_sb[pb][:], start=(kb == 0), stop=(kb == nkb - 1),
                    skip_group_check=True))(kb, dh, pb, nkb),
                  wait=([pool_free[2 + dh]] if kb == 0 else []),
                  inc=([("pe", 1)] if dh == 1 else []))
            ptb_free[pb] = ("pe", rec.v("pe"))

        def finish():
            nonlocal ac_exp_prev
            ac_exp_prev = rec.v("ac")
            pe_ctx = rec.v("pe")
            state["pe_ctx"] = pe_ctx
            E("vector", (lambda p: lambda e: e.tensor_copy(
                ctxT[:, 256 * p:256 * (p + 1)], pool[2][:, 0:256]))(p),
              wait=[("pe", pe_ctx)], inc=[("ve", 1)])
            pool_free[2] = ("ve", rec.v("ve"))
            E("scalar", (lambda p: lambda e: e.copy(
                ctxT[:, M + 256 * p:M + 256 * (p + 1)], pool[3][:, 0:256]))(p),
              wait=[("pe", pe_ctx)], inc=[("ac", 1)])
            pool_free[3] = ("ac", rec.v("ac"))
            E("scalar", lambda e: e.copy(r_row[:], pool[4][0:1, 0:256]),
              wait=[("pe", pe_ctx), ("pe", rrow_prev[0])], inc=[("ac", 1)])
            pool_free[4] = ("ac", rec.v("ac"))
            state["ve_rr"] = ("ac", rec.v("ac"))

        def run(upto):
            if not state["primed"]:
                emit_st(0)
                if nkb > 1:
                    emit_st(1)
                state["acs"][0] = emit_exp(0)
                state["primed"] = True
            upto = min(upto, nkb)
            kb = state["kb"]
            while kb < upto:
                if kb + 2 < nkb:
                    emit_st(kb + 2)
                if kb + 1 < nkb:
                    state["acs"][kb + 1] = emit_exp(kb + 1)
                emit_rctx(kb, state["acs"].pop(kb))
                drain_tr(2)
                kb += 1
            state["kb"] = kb
            if kb == nkb and state["pe_ctx"] is None:
                finish()

        def tail():
            rr_edge = state["ve_rr"]
            for half in range(2):
                E("tensor", (lambda half: lambda e: e.matmul(
                    pool[3][:, 0:1], r_row[0:1, 128 * half:128 * (half + 1)], one_f32[:],
                    start=True, stop=True))(half),
                  wait=[rr_edge, pool_free[3]],
                  inc=[("pe", 1)])
                pe_rt = rec.v("pe")
                rrow_prev[0] = pe_rt
                E("vector", (lambda p, half: lambda e: e.tensor_copy(
                    r_all[:, 2 * p + half:2 * p + half + 1], pool[3][:, 0:1]))(p, half),
                  wait=[("pe", pe_rt)], inc=[("ve", 1)])
                pool_free[3] = ("ve", rec.v("ve"))
                ve_r = rec.v("ve")
                VF()
                E("vector", (lambda p, half: lambda e: e.reciprocal(
                    rinv_all[:, 2 * p + half:2 * p + half + 1],
                    r_all[:, 2 * p + half:2 * p + half + 1]))(p, half),
                  wait=[("ve", ve_r)], inc=[("ve", 1)])
                rinv_ve.append(rec.v("ve"))
        return run, tail

    for _wu in range(40):
        E("tensor", (lambda: lambda e: e.matmul(
            pool[6][:, 0:128], ident[:], ident[:], start=True, stop=True))(),
          wait=([("si", 16)] if _wu == 0 else []), inc=([("pe", 1)] if _wu == 39 else []))
    pool_free[6] = ("pe", rec.v("pe"))

    emit_lnstats(0)
    emit_lnfin(0)
    queue_xn(0)
    pair = None
    pair_tail = None
    for g in range(NG):
        g_done = emit_xn(g)
        if g + 1 < NG:
            emit_lnstats(g + 1)
        if pair_tail is not None:
            pair_tail()
            pair_tail = None
        mk2_prev[0] = mk2_state["ve"]
        pre = emit_proj(g, g_done, with_bound=(g % 2 == 1))
        if g % 2 == 0 and pair is not None:
            if g + 1 < NG:
                emit_lnfin(g + 1)
                queue_xn(g + 1)
            run, tl = pair
            run(10 ** 9)
            pair = None
            pair_tail = tl
        elif g % 2 == 1:
            if g + 1 < NG:
                emit_lnfin(g + 1)
                queue_xn(g + 1)
            run, tl = make_attn_pair(g // 2, pre)
            run((8 * (g // 2) + 8) // 2)
            pair = (run, tl)
        else:
            if g + 1 < NG:
                emit_lnfin(g + 1)
                queue_xn(g + 1)
        emit_logits(g)
    if pair is not None:
        run, tl = pair
        run(10 ** 9)
        tl()

    if debug:
        wA = [("ve", rec.v("ve")), ("ac", rec.v("ac")), ("pe", rec.v("pe"))]
        E("gpsimd", lambda g: g.dma_start(out=d_xnT[:], in_=xnT_own[:]), wait=wA, inc=[("so0", 16)])
        E("gpsimd", lambda g: g.dma_start(out=d_KT[:], in_=KT[:]), inc=[("so0", 16)])
        E("gpsimd", lambda g: g.dma_start(out=d_Vd[:], in_=Vd[:]), inc=[("so0", 16)])
        E("gpsimd", lambda g: g.dma_start(out=d_QT[:], in_=QT[:]), inc=[("so0", 16)])
        E("gpsimd", lambda g: g.dma_start(out=d_comb[:], in_=comb[:]), inc=[("so0", 16)])
        E("gpsimd", lambda g: g.dma_start(out=d_r[:], in_=r_all[:]), inc=[("so0", 16)])
        E("gpsimd", lambda g: g.dma_start(out=d_ctxT[:], in_=ctxT[:]), inc=[("so0", 16)])
        E("gpsimd", None, wait=[("so0", rec.v("so0"))])


    ve_A0 = rec.v("ve")

    ve_A = rec.v("ve")
    ac_A = rec.v("ac")
    pe_A = rec.v("pe")

    replay(nc, rec, sems)
    esA.close()

    rec = Rec(rec.cnt)
    E = rec.E
    barA = [("ve", ve_A), ("ac", ac_A), ("pe", pe_A)]

    NWW, CAPW, SPE, NSEC = 2, 384, 768, 6
    W1CH = 16
    W2CH = 16

    esB = ExitStack()
    xn_sel = esB.enter_context(nc.sbuf_tensor("xn_sel", [128, NC8 * SPE], BF16))
    hT = esB.enter_context(nc.sbuf_tensor("hT", [128, NH8 * SPE], BF16))
    y_half = esB.enter_context(nc.sbuf_tensor("y_half", [128, NG * C], BF16))
    y_selE = [esB.enter_context(nc.sbuf_tensor(f"y_selE{i}", [128, NSEC * C], BF16))
              for i in range(2)]
    w1sb = [esB.enter_context(nc.sbuf_tensor(f"w1sb{i}", [128, NC8 * 256], BF16)) for i in range(3)]
    w2sb = [esB.enter_context(nc.sbuf_tensor(f"w2sb{i}", [128, 2 * 512], BF16)) for i in range(4)]
    w2f32 = [esB.enter_context(nc.sbuf_tensor(f"w2f32{i}", [128, 2 * 512], F32)) for i in range(3)]
    iota_t = esB.enter_context(nc.sbuf_tensor("iota_t", [128, 384], F32))
    rel_sb = esB.enter_context(nc.sbuf_tensor("rel_sb", [128, 128], F32))
    onesB = esB.enter_context(nc.sbuf_tensor("onesB", [128, 1], BF16))
    ones1b = esB.enter_context(nc.sbuf_tensor("ones1b", [1, 128], BF16))
    st_ring = esB.enter_context(nc.sbuf_tensor("st_ring", [128, 6 * 128], BF16))
    tfeat = esB.enter_context(nc.sbuf_tensor("tfeat", [128, 2 * NG], BF16))
    tls_sb = [esB.enter_context(nc.sbuf_tensor(f"tls_sb{i}", [128, 12], F32)) for i in range(2)]
    eq_tl = esB.enter_context(nc.sbuf_tensor("eq_tl", [128, 6 * 128], BF16))
    m_col = esB.enter_context(nc.sbuf_tensor("m_col", [128, 1], F32))
    tls_snap = esB.enter_context(nc.sbuf_tensor("tls_snap", [128, 12], F32))
    yps = [esB.enter_context(nc.psum_tensor(f"ypsB{i}", [128, 512], F32)) for i in range(8)]
    es_g = ExitStack()
    xn_tok = es_g.enter_context(nc.sbuf_tensor("xn_tok", [128, NG * C], BF16))
    S8 = es_g.enter_context(nc.sbuf_tensor("S8", [128, 8 * 384], BF16))
    es_r = ExitStack()
    ltri_sb = es_r.enter_context(nc.sbuf_tensor("ltri_sb", [128, 128], BF16))
    gw_sb = es_r.enter_context(nc.sbuf_tensor("gw_sb", [128, 128], BF16))
    mask_bf = es_r.enter_context(nc.sbuf_tensor("mask_bf", [128, 128], BF16))
    offs_sb = es_r.enter_context(nc.sbuf_tensor("offs_sb", [1, 128], F32))
    tot_bfc = es_r.enter_context(nc.sbuf_tensor("tot_bfc", [128, 1], BF16))
    offs_hi = es_r.enter_context(nc.sbuf_tensor("offs_hi", [1, 128], BF16))
    offs_lo = es_r.enter_context(nc.sbuf_tensor("offs_lo", [1, 128], BF16))
    offs_hf = es_r.enter_context(nc.sbuf_tensor("offs_hf", [1, 128], F32))
    if debug:
        d_rel = nc.dram_tensor("d_rel", [128, 128], F32, kind="ExternalOutput")
        d_yh = nc.dram_tensor("d_yh", [128, NG * C], F32, kind="ExternalOutput")
        d_xsel = nc.dram_tensor("d_xsel", [128, NC8 * SPE], F32, kind="ExternalOutput")

    poolB = [list(barA) for _ in range(8)]

    E("gpsimd", lambda g: g.dma_start(out=ltri_sb[:], in_=ltri_in[:]), inc=[("ss", 16)])
    E("gpsimd", lambda g: g.dma_start(out=gw_sb[:], in_=gw_in[:]), inc=[("ss", 16)])
    E("gpsimd", lambda g: g.dma_start(out=iota_t[:], in_=iota_in[:]), inc=[("ss", 16)])
    E("gpsimd", lambda g: g.dma_start(out=tfeat[:], in_=tfeat_in[:]), inc=[("ss", 16)])
    SS_B = ("ss", rec.v("ss"))
    E("sync", lambda e: e.dma_start(out=xn_tok[:, 0:8 * C], in_=xnt_dram[:, 0:8 * C]),
      wait=[("sxt", 16 * NG)], inc=[("swh", 16)])
    E("sync", lambda e: e.dma_start(out=xn_tok[:, 8 * C:NG * C], in_=xnt_dram[:, 8 * C:NG * C]),
      inc=[("swh", 16)])
    SI_XT = ("swh", rec.v("swh"))

    E("vector", lambda e: e.memset(onesB[:], 1.0), wait=barA, inc=[("ve", 1)])
    E("vector", lambda e: e.memset(ones1b[:], 1.0), inc=[("ve", 1)])
    ve_onesB = rec.v("ve")

    E("vector", lambda e: e.tensor_copy(mask_bf[:], mask_f32[:]), inc=[("ve", 1)])
    ve_mbf = rec.v("ve")
    E("tensor", lambda t: t.matmul(yps[0][:, 0:128], ltri_sb[:], mask_bf[:],
                                   start=True, stop=True),
      wait=poolB[0] + [("ve", ve_mbf), SS_B], inc=[("pe", 1)])
    pe_pp = rec.v("pe")
    E("vector", lambda e: e.tensor_copy(rel_sb[:], yps[0][:, 0:128]),
      wait=[("pe", pe_pp)], inc=[("ve", 1)])
    ve_pp = rec.v("ve")
    poolB[0] = [("ve", ve_pp)]
    E("tensor", lambda t: t.matmul(yps[1][:, 0:1], mask_bf[:], onesB[:],
                                   start=True, stop=True),
      wait=poolB[1] + [("ve", ve_onesB), ("ve", ve_mbf)], inc=[("pe", 1)])
    pe_tot = rec.v("pe")
    E("scalar", lambda s: s.copy(tot_bfc[:], yps[1][:, 0:1]),
      wait=[("pe", pe_tot)], inc=[("ac", 1)])
    ac_tot = rec.v("ac")
    poolB[1] = [("ac", ac_tot)]
    E("tensor", lambda t: t.matmul(yps[2][0:1, 0:128], tot_bfc[:], gw_sb[:],
                                   start=True, stop=True),
      wait=poolB[2] + [("ac", ac_tot)], inc=[("pe", 1)])
    pe_of = rec.v("pe")
    E("vector", lambda e: e.tensor_copy(offs_sb[:], yps[2][0:1, 0:128]),
      wait=[("pe", pe_of)], inc=[("ve", 1)])
    ve_of = rec.v("ve")
    poolB[2] = [("ve", ve_of)]
    E("scalar", lambda s: s.copy(offs_hi[:], offs_sb[:]),
      wait=[("ve", ve_of)], inc=[("ac", 1)])
    ac_oh = rec.v("ac")
    E("scalar", lambda s: s.copy(offs_hf[:], offs_hi[:]), wait=[("ac", ac_oh)], inc=[("ac", 1)])
    ac_ohf = rec.v("ac")
    E("vector", lambda v: v.tensor_tensor(offs_sb[:], offs_sb[:], offs_hf[:], ALU.subtract),
      wait=[("ac", ac_ohf)], inc=[("ve", 1)])
    v_lf = rec.v("ve")
    VF()
    E("vector", lambda v: v.tensor_copy(offs_lo[:], offs_sb[:]), wait=[("ve", v_lf)],
      inc=[("ve", 1)])
    ve_ol = rec.v("ve")
    for part, wsem in ((offs_hi, ("ac", ac_oh)), (offs_lo, ("ve", ve_ol))):
        E("tensor", (lambda part: lambda t: t.matmul(
            yps[3][:, 0:128], ones1b[:], part[:],
            start=(part is offs_hi), stop=(part is offs_lo)))(part),
          wait=(poolB[3] + [wsem, ("ve", ve_onesB)] if part is offs_hi else [wsem]),
          inc=([("pe", 1)] if part is offs_lo else []))
    pe_ob = rec.v("pe")
    VF()
    E("vector", lambda e: e.tensor_tensor(rel_sb[:], rel_sb[:], yps[3][:, 0:128], ALU.add),
      wait=[("pe", pe_ob), ("ve", ve_pp)], inc=[("ve", 1)])
    v_r1 = rec.v("ve")
    VF()
    E("vector", lambda e: e.tensor_tensor(rel_sb[:], rel_sb[:], mask_f32[:], ALU.mult),
      wait=[("ve", v_r1)], inc=[("ve", 1)])
    v_r2 = rec.v("ve")
    VF()
    E("vector", lambda e: e.tensor_scalar(rel_sb[:], rel_sb[:], 1.0, None, ALU.subtract),
      wait=[("ve", v_r2)], inc=[("ve", 1)])
    ve_rel = rec.v("ve")
    poolB[3] = [("ve", rec.v("ve"))]
    if debug:
        E("gpsimd", lambda g: g.dma_start(out=d_rel[:], in_=rel_sb[:]),
          wait=[("ve", ve_rel)], inc=[("so0", 16)])

    es_r.close()

    w1_loaded = {}
    w1_free = {}
    w1_ptr = [0]
    W1SEM = ("swa", "swb", "swc")

    def issue_w1():
        idx = w1_ptr[0]
        if idx >= NE * W1CH:
            return
        w1_ptr[0] += 1
        e_, ch = idx // W1CH, idx % W1CH
        buf = idx % 3
        waits = [w1_free[idx - 3]] if idx >= 3 else []
        E("gpsimd", (lambda e_, ch, buf: lambda g: g.dma_start(
            out=w1sb[buf].rearrange("p (c h) -> p c h", c=NC8),
            in_=w1[e_, :, 256 * ch:256 * (ch + 1)].rearrange("(c p) h -> p c h", p=128)))(e_, ch, buf),
            wait=waits, inc=[(W1SEM[buf], 16)])
        w1_loaded[idx] = (W1SEM[buf], rec.v(W1SEM[buf]))

    w2_loaded = {}
    w2_free = {}
    w2sb_free = {}
    w2_cast = {}
    w2_ptr = [0]
    w2_cast_ptr = [0]
    W2SEM = ("swe", "swf", "swg", "swh")

    def issue_w2():
        idx = w2_ptr[0]
        if idx >= NE * 2 * W2CH:
            return
        w2_ptr[0] += 1
        e_ = idx // (2 * W2CH)
        cc = (idx // W2CH) % 2
        ch = idx % W2CH
        buf = idx % 3
        q = ("sync", "scalar", "sync")[buf]
        sem = ("swe", "swf", "swg")[buf]
        waits = [w2_free[idx - 3]] if idx >= 3 else []
        E(q, (lambda e_, cc, ch, buf: lambda g: g.dma_start(
            out=w2f32[buf].rearrange("p (h c) -> p h c", h=2),
            in_=w2[e_, 256 * ch:256 * (ch + 1), 512 * cc:512 * (cc + 1)]
                .rearrange("(h p) c -> p h c", p=128)))(e_, cc, ch, buf),
            wait=waits, inc=[(sem, 16)])
        w2_loaded[idx] = (sem, rec.v(sem))

    def cast_w2_upto(idx):
        while w2_cast_ptr[0] <= idx:
            k = w2_cast_ptr[0]
            w2_cast_ptr[0] += 1
            fb, bb = k % 3, k % 4
            waits = [w2_loaded[k]]
            if k >= 4:
                waits.append(w2sb_free[k - 4])
            if k % 2 == 0:
                E("scalar", (lambda fb, bb: lambda s: s.copy(w2sb[bb][:], w2f32[fb][:]))(fb, bb),
                  wait=waits, inc=[("ac", 1)])
                w2_cast[k] = ("ac", rec.v("ac"))
            else:
                E("vector", (lambda fb, bb: lambda v: v.tensor_copy(w2sb[bb][:], w2f32[fb][:]))(fb, bb),
                  wait=waits, inc=[("ve", 1)])
                w2_cast[k] = ("ve", rec.v("ve"))
            w2_free[k] = w2_cast[k]

    for _ in range(3):
        issue_w1()
        issue_w2()

    s8_use_pe = [0]
    gather_done_ve = {}

    def emit_s8_builds(e_, w):
        for k in range(8):
            g_ = w * 8 + k
            wts = [("ve", ve_rel)] if (k == 0) else []
            if k == 0 and s8_use_pe[0]:
                wts.append(("pe", s8_use_pe[0]))
            E("vector", (lambda k, g_, e_: lambda v: v.tensor_scalar(
                S8[:, 384 * k:384 * (k + 1)], iota_t[:],
                rel_sb[:, 8 * g_ + e_:8 * g_ + e_ + 1], None, ALU.is_equal))(k, g_, e_),
              wait=wts, inc=[("ve", 1)])
        return rec.v("ve")

    def emit_gather_tile(e_, idx, bank, ve_s8):
        w, c = idx // 8, idx % 8
        for k in range(8):
            wts = []
            if k == 0:
                wts = poolB[bank] + [("ve", ve_s8), SI_XT]
            E("tensor", (lambda w, c, k, bank: lambda t: t.matmul(
                yps[bank][:, 0:384],
                xn_tok[:, (w * 8 + k) * C + 128 * c:(w * 8 + k) * C + 128 * (c + 1)],
                S8[:, 384 * k:384 * (k + 1)],
                start=(k == 0), stop=(k == 7), skip_group_check=True))(w, c, k, bank),
              wait=wts, inc=([("pe", 1)] if k == 7 else []))
        pe_g = rec.v("pe")
        s8_use_pe[0] = pe_g
        E("vector", (lambda w, c, bank: lambda v: v.tensor_copy(
            xn_sel[:, c * SPE + w * CAPW:c * SPE + w * CAPW + 384], yps[bank][:, 0:384]))(w, c, bank),
          wait=[("pe", pe_g)], inc=[("ve", 1)])
        poolB[bank] = [("ve", rec.v("ve"))]

    def emit_gather_tls(e_, w, bank):
        buf = e_ % 2
        for sec3 in range(3):
            for k in range(8):
                wts = []
                if k == 0:
                    wts = list(poolB[bank]) + [SS_B]
                E("tensor", (lambda w, k, sec3, bank: lambda t: t.matmul(
                    yps[bank][:, 0:2],
                    S8[:, 384 * k + 128 * sec3:384 * k + 128 * sec3 + 128],
                    tfeat[:, 2 * (w * 8 + k):2 * (w * 8 + k) + 2],
                    start=(k == 0), stop=(k == 7), skip_group_check=True))(w, k, sec3, bank),
                  wait=wts, inc=([("pe", 1)] if k == 7 else []))
            pe_t = rec.v("pe")
            s8_use_pe[0] = pe_t
            E("vector", (lambda w, sec3, bank, buf: lambda v: v.tensor_copy(
                tls_sb[buf][:, (w * 3 + sec3) * 2:(w * 3 + sec3) * 2 + 2],
                yps[bank][:, 0:2]))(w, sec3, bank, buf),
              wait=[("pe", pe_t)], inc=[("ve", 1)])
            poolB[bank] = [("ve", rec.v("ve"))]

    gelu_chunk = {}
    hb_rr = [0]

    def emit_h_chunk(e_, ch):
        w1idx = e_ * W1CH + ch
        for hp2 in range(2):
            for sg in range(2):
                hb = 6 + (hb_rr[0] % 2)
                hb_rr[0] += 1
                for c in range(NC8):
                    wts = []
                    if c == 0:
                        wts = poolB[hb] + [w1_loaded[w1idx]]
                        if ch == 0 and hp2 == 0 and sg == 0:
                            wts.append(("ve", gather_done_ve[e_]))
                    E("tensor", (lambda ch, hp2, sg, c, hb, buf: lambda t: t.matmul(
                        yps[hb][:, 0:384],
                        w1sb[buf][:, 256 * c + 128 * hp2:256 * c + 128 * (hp2 + 1)],
                        xn_sel[:, c * SPE + sg * 384:c * SPE + (sg + 1) * 384],
                        start=(c == 0), stop=(c == NC8 - 1),
                        skip_group_check=True))(ch, hp2, sg, c, hb, w1idx % 3),
                      wait=wts, inc=([("pe", 1)] if c == NC8 - 1 else []))
                pe_h = rec.v("pe")
                H8 = 2 * ch + hp2
                E("scalar", (lambda H8, sg, hb: lambda s: s.activation(
                    hT[:, H8 * SPE + sg * 384:H8 * SPE + (sg + 1) * 384],
                    yps[hb][:, 0:384], AF.Gelu))(H8, sg, hb),
                  wait=[("pe", pe_h)], inc=[("ac", 1)])
                poolB[hb] = [("ac", rec.v("ac"))]
        w1_free[w1idx] = ("pe", rec.v("pe"))
        issue_w1()
        gelu_chunk[(e_, ch)] = rec.v("ac")

    yevac_ve = {}
    yev_sect = {}

    def emit_y_half(e_, cc, ch, shalf):
        w2idx = e_ * 2 * W2CH + cc * W2CH + ch
        if shalf == 0:
            cast_w2_upto(min(w2idx + 1, NE * 2 * W2CH - 1))
        for s6 in range(3 * shalf, 3 * shalf + 3):
            for h8x in range(2):
                H8 = 2 * ch + h8x
                first = (ch == 0 and h8x == 0)
                last = (ch == W2CH - 1 and h8x == 1)
                wts = []
                if h8x == 0 and s6 == 3 * shalf:
                    wts = [w2_cast[w2idx], ("ac", gelu_chunk[(e_, ch)])]
                if first:
                    wts = wts + poolB[s6]
                E("tensor", (lambda s6, H8, h8x, first, last, buf: lambda t: t.matmul(
                    yps[s6][:],
                    hT[:, H8 * SPE + s6 * 128:H8 * SPE + (s6 + 1) * 128],
                    w2sb[buf][:, h8x * 512:(h8x + 1) * 512],
                    start=first, stop=last,
                    skip_group_check=True))(s6, H8, h8x, first, last, w2idx % 4),
                  wait=wts,
                  inc=([("pe", 1)] if (last or (h8x == 1 and s6 == 3 * shalf + 2)) else []))
        if shalf == 1:
            w2sb_free[w2idx] = ("pe", rec.v("pe"))
            issue_w2()

    def emit_y_evacs(e_, cc):
        pe_y = rec.v("pe")
        buf = e_ % 2
        for s6 in range(NSEC):
            E("vector", (lambda s6, cc, buf: lambda v: v.tensor_copy(
                y_selE[buf][:, s6 * C + cc * 512:s6 * C + cc * 512 + 512], yps[s6][:]))(s6, cc, buf),
              wait=[("pe", pe_y)], inc=[("ve", 1)])
            poolB[s6] = [("ve", rec.v("ve"))]
            yev_sect[(e_, cc, s6)] = rec.v("ve")
        if cc == 1:
            yevac_ve[e_] = rec.v("ve")

    ring_free = [None] * 6
    ring_ve = [0] * 6
    eq_state = {"e": None, "ve": 0}
    mcol_reader = [0]

    def emit_eq_tl(se):
        buf = se % 2
        E("vector", (lambda buf: lambda v: v.tensor_copy(tls_snap[:], tls_sb[buf][:]))(buf),
          wait=[("ve", mcol_reader[0])], inc=[("ve", 1)])
        for s6 in range(NSEC):
            E("vector", (lambda s6, buf: lambda v: v.tensor_scalar(
                eq_tl[:, s6 * 128:(s6 + 1) * 128], iota_t[:, 0:128],
                tls_sb[buf][:, s6 * 2:s6 * 2 + 1], None, ALU.is_equal))(s6, buf),
              inc=[("ve", 1)])
        eq_state["e"] = se
        eq_state["ve"] = rec.v("ve")

    def emit_st_one(se, g_, sloc):
        buf = se % 2
        s6 = (g_ // 8) * 3 + sloc
        E("vector", (lambda g_, s6: lambda v: v.tensor_scalar(
            m_col[:], tls_snap[:, s6 * 2 + 1:s6 * 2 + 2], float(g_), None,
            ALU.is_equal))(g_, s6),
          wait=[("ve", max(mcol_reader[0], eq_state["ve"]))], inc=[("ve", 1)])
        VF()
        rslot = (g_ * 3 + sloc) % 6
        wst = [("ve", rec.v("ve"))]
        if ring_free[rslot] is not None:
            wst.append(ring_free[rslot])
        E("vector", (lambda s6, rslot: lambda v: v.tensor_scalar(
            st_ring[:, rslot * 128:(rslot + 1) * 128], eq_tl[:, s6 * 128:(s6 + 1) * 128],
            m_col[:], None, ALU.mult))(s6, rslot),
          wait=wst, inc=[("ve", 1)])
        ring_ve[rslot] = rec.v("ve")
        mcol_reader[0] = rec.v("ve")

    def emit_partial(se, g_, cc, first_of_se, via_scalar=False):
        w = g_ // 8
        buf = se % 2
        for sloc in range(3):
            rslot = (g_ * 3 + sloc) % 6
            wts = []
            if sloc == 0:
                wts = poolB[7] + [("ve", max(ring_ve[(g_ * 3 + s) % 6] for s in range(3)))]
                if first_of_se:
                    wts.append(("ve", yevac_ve[se]))
            E("tensor", (lambda sloc, rslot, w, cc, buf: lambda t: t.matmul(
                yps[7][:],
                st_ring[:, rslot * 128:(rslot + 1) * 128],
                y_selE[buf][:, (w * 3 + sloc) * C + cc * 512:(w * 3 + sloc) * C + cc * 512 + 512],
                start=(sloc == 0), stop=(sloc == 2),
                skip_group_check=True))(sloc, rslot, w, cc, buf),
              wait=wts, inc=([("pe", 1)] if sloc == 2 else []))
        pe_p = rec.v("pe")
        if cc == 1:
            for sloc in range(3):
                ring_free[(g_ * 3 + sloc) % 6] = ("pe", pe_p)
        cmb = comb[:, NE * g_ + se:NE * g_ + se + 1]
        dst = y_half[:, g_ * C + cc * 512:g_ * C + cc * 512 + 512]
        if se == 0:
            E("vector", (lambda dst, cmb: lambda v: v.tensor_scalar(
                dst, yps[7][:], cmb, None, ALU.mult))(dst, cmb),
              wait=[("pe", pe_p)], inc=[("ve", 1)])
        elif via_scalar:
            pb = ptmp[cc]
            E("scalar", (lambda pb, cmb: lambda s: s.mul(pb[:], yps[7][:], cmb))(pb, cmb),
              wait=[("pe", pe_p)] + ptmp_free[cc], inc=[("ac", 1)])
            ac_pm = rec.v("ac")
            E("vector", (lambda dst, pb: lambda v: v.tensor_tensor(
                dst, pb[:], dst, ALU.add))(dst, pb),
              wait=[("ac", ac_pm)], inc=[("ve", 1)])
            ptmp_free[cc] = [("ve", rec.v("ve"))]
            poolB[7] = [("ac", ac_pm)]
            return rec.v("ve")
        else:
            E("vector", (lambda dst, cmb: lambda v: v.scalar_tensor_tensor(
                dst, yps[7][:], cmb, dst, ALU.mult, ALU.add))(dst, cmb),
              wait=[("pe", pe_p)], inc=[("ve", 1)])
        poolB[7] = [("ve", rec.v("ve"))]
        return rec.v("ve")

    stt_done = {}

    xres_loaded = {}
    xres_free = [[], []]
    out_free = [("so0", rec.v("so0")), ("so1", rec.v("so1"))]
    gtmp_free = [[], []]
    stmp_free = [[], []]

    def load_xres(g_):
        buf = g_ % 2
        sem = "sxr0" if buf == 0 else "sxr1"
        E("sync", (lambda g_, buf: lambda e: e.dma_start(
            out=xres[buf][:], in_=x[512 * g_:512 * g_ + 128, :]))(g_, buf),
            wait=xres_free[buf], inc=[(sem, 16)])
        xres_loaded[g_] = (sem, rec.v(sem))

    def emit_combine(g_):
        gtmp = gtmp2[g_ % 2]
        stmp = stmp2[g_ % 2]
        pe_sv = [0, 0]
        for cc in range(2):
            bank = (g_ % 2) * 2 + cc
            for dh in range(2):
                wts = []
                if dh == 0:
                    wts = poolB[bank]
                E("tensor", (lambda g_, cc, dh, bank: lambda t: t.matmul(
                    yps[bank][:],
                    ctxT[:, M * dh + 128 * g_:M * dh + 128 * (g_ + 1)],
                    wvup_sb[:, C * dh + 512 * cc:C * dh + 512 * (cc + 1)],
                    start=(dh == 0), stop=(dh == 1),
                    skip_group_check=True))(g_, cc, dh, bank),
                  wait=wts, inc=([("pe", 1)] if dh == 1 else []))
            pe_sv[cc] = rec.v("pe")
        for cc in range(2):
            E("scalar", (lambda g_, cc, gtmp: lambda s: s.activation(
                gtmp[:, 512 * cc:512 * (cc + 1)],
                y_half[:, g_ * C + cc * 512:g_ * C + cc * 512 + 512], AF.Silu))(g_, cc, gtmp),
              wait=[("ve", stt_done[g_])] + (gtmp_free[g_ % 2] if cc == 0 else []), inc=[("ac", 1)])
        ac_g = rec.v("ac")
        for cc in range(2):
            bank = (g_ % 2) * 2 + cc
            E("vector", (lambda g_, cc, bank, stmp, gtmp: lambda v: v.scalar_tensor_tensor(
                stmp[:, 512 * cc:512 * (cc + 1)], yps[bank][:],
                rinv_all[:, g_:g_ + 1], gtmp[:, 512 * cc:512 * (cc + 1)],
                ALU.mult, ALU.mult))(g_, cc, bank, stmp, gtmp),
              wait=[("pe", pe_sv[cc]), ("ve", rinv_ve[g_]), ("ac", ac_g)]
                   + (stmp_free[g_ % 2] if cc == 0 else []),
              inc=[("ve", 1)])
            poolB[bank] = [("ve", rec.v("ve"))]
        v_t1 = rec.v("ve")
        VF()
        for cc in range(2):
            E("vector", (lambda g_, cc, stmp, gtmp: lambda v: v.tensor_add(
                gtmp[:, 512 * cc:512 * (cc + 1)], stmp[:, 512 * cc:512 * (cc + 1)],
                xres[g_ % 2][:, 512 * cc:512 * (cc + 1)]))(g_, cc, stmp, gtmp),
              wait=[("ve", v_t1), xres_loaded[g_]] if cc == 0 else [],
              inc=[("ve", 1)])
        stmp_free[g_ % 2] = [("ve", rec.v("ve"))]
        ve_out = rec.v("ve")
        xres_free[g_ % 2] = [("ve", ve_out)]
        if g_ + 2 < NG:
            load_xres(g_ + 2)
        sem = "so0" if g_ % 2 == 0 else "so1"
        E("sync", (lambda g_, gtmp: lambda e: e.dma_start(
            out=out[128 * g_:128 * (g_ + 1), :], in_=gtmp[:]))(g_, gtmp),
          wait=[("ve", ve_out)], inc=[(sem, 16)])
        out_free[g_ % 2] = (sem, rec.v(sem))
        gtmp_free[g_ % 2] = [(sem, rec.v(sem))]

    ve_s8 = emit_s8_builds(0, 0)
    for idx in range(16):
        if idx == 8:
            emit_gather_tls(0, 0, 6)
            ve_s8 = emit_s8_builds(0, 1)
        emit_gather_tile(0, idx, 6 + idx % 2, ve_s8)
    emit_gather_tls(0, 1, 6)
    gather_done_ve[0] = rec.v("ve")

    for e_ in range(NE):
        for ch in range(W1CH):
            emit_h_chunk(e_, ch)
            if ch >= 1:
                emit_y_half(e_, 0, ch - 1, 0)
                emit_y_half(e_, 0, ch - 1, 1)
        emit_y_half(e_, 0, W1CH - 1, 0)
        emit_y_half(e_, 0, W1CH - 1, 1)
        emit_y_evacs(e_, 0)
        se = e_ - 1
        ve_s8 = None
        if se >= 0:
            emit_eq_tl(se)
        for j in range(W2CH):
            if e_ + 1 < NE and j in (0, 8):
                ve_s8 = emit_s8_builds(e_ + 1, j // 8)
            if se >= 0:
                emit_st_one(se, j, 0)
                emit_st_one(se, j, 1)
                emit_st_one(se, j, 2)
            emit_y_half(e_, 1, j, 0)
            if se >= 0 and j >= 1:
                emit_partial(se, j - 1, 0, first_of_se=(j == 1))
            emit_y_half(e_, 1, j, 1)
            if e_ + 1 < NE:
                emit_gather_tile(e_ + 1, j, 6, ve_s8)
                if j in (7, 15):
                    emit_gather_tls(e_ + 1, j // 8, 6)
            if se >= 0 and j >= 1:
                emit_partial(se, j - 1, 1, False)
        if e_ + 1 < NE:
            gather_done_ve[e_ + 1] = rec.v("ve")
        emit_y_evacs(e_, 1)
        if se >= 0:
            emit_partial(se, W2CH - 1, 0, first_of_se=False)
            emit_partial(se, W2CH - 1, 1, False)

    es_g.close()
    es_t = ExitStack()
    gtmp2 = [es_t.enter_context(nc.sbuf_tensor(f"gtmpB{i}", [128, C], F32)) for i in range(2)]
    stmp2 = [es_t.enter_context(nc.sbuf_tensor(f"stmpB{i}", [128, C], F32)) for i in range(2)]
    xres = [es_t.enter_context(nc.sbuf_tensor(f"xresB{i}", [128, C], F32)) for i in range(2)]
    ptmp = [es_t.enter_context(nc.sbuf_tensor(f"ptmpB{i}", [128, 512], BF16)) for i in range(2)]
    ptmp_free = [[], []]
    load_xres(0)
    load_xres(1)
    se = NE - 1
    emit_eq_tl(se)
    for sl in range(3):
        emit_st_one(se, 0, sl)
    for g_ in range(NG):
        if g_ + 1 < NG:
            for sl in range(3):
                emit_st_one(se, g_ + 1, sl)
        emit_partial(se, g_, 0, first_of_se=(g_ == 0), via_scalar=True)
        if g_ >= 1:
            emit_combine(g_ - 1)
        stt_done[g_] = emit_partial(se, g_, 1, False, via_scalar=True)
    emit_combine(NG - 1)

    if debug:
        E("gpsimd", lambda g: g.dma_start(out=d_yh[:], in_=y_half[:]),
          wait=[("ve", rec.v("ve"))], inc=[("so0", 16)])
        E("gpsimd", lambda g: g.dma_start(out=d_xsel[:], in_=xn_sel[:]), inc=[("so0", 16)])
    E("sync", None, wait=[("so0", rec.v("so0")), ("so1", rec.v("so1"))])
    replay(nc, rec, sems)
    es_t.close()
    esB.close()
    es0.close()
    return nc


def _perm_and_mask(cm):
    own = [r for r in range(512) if r % 4 == cm]
    rest = [r for r in range(512) if r % 4 != cm]
    perm = np.array(own + rest, dtype=np.int64)
    p = np.arange(128)[:, None]
    orig = perm[None, :]
    dmask = np.where(orig <= 4 * p + cm, 0.0, NEG).astype(np.float32)
    dm = np.zeros((128, 2048), dtype=np.float32)
    for j in range(4):
        dT = dmask[:, 128 * j:128 * (j + 1)].T
        dm[:, 256 * j:256 * j + 128] = dT
        dm[:, 256 * (4 + j) + 128:256 * (4 + j) + 256] = dT
        dm[:, 256 * (4 + j):256 * (4 + j) + 128] = NEG
    return perm, dm


def kernel(L0_attn_inputs, L0_attn_Wq, L0_attn_Wk, L0_attn_Wvdown, L0_attn_Wvup,
           L0_ffn_router_w, L0_ffn_w1, L0_ffn_w2):
    x = np.asarray(L0_attn_inputs, dtype=np.float32)
    wq = np.ascontiguousarray(np.asarray(L0_attn_Wq, dtype=np.float32))
    wk = np.ascontiguousarray(np.asarray(L0_attn_Wk, dtype=np.float32))
    wvd = np.ascontiguousarray(np.asarray(L0_attn_Wvdown, dtype=np.float32))
    wvup = np.ascontiguousarray(np.asarray(L0_attn_Wvup, dtype=np.float32))
    rwT = np.ascontiguousarray(np.asarray(L0_ffn_router_w, dtype=np.float32).T)
    w1 = np.ascontiguousarray(np.asarray(L0_ffn_w1, dtype=np.float32))
    w2 = np.ascontiguousarray(np.asarray(L0_ffn_w2, dtype=np.float32))
    ident = np.eye(128, dtype=np.float32)
    ltri = np.triu(np.ones((128, 128), dtype=np.float32))
    ii = np.arange(128)
    gi, ei = ii // 8, ii % 8
    gw_m = ((ei[:, None] == ei[None, :]) & ((gi[:, None] // 8) == (gi[None, :] // 8))
            & (gi[:, None] < gi[None, :])).astype(np.float32)
    iota = np.tile(np.arange(384, dtype=np.float32), (128, 1))
    tfeat = np.zeros((128, 32), dtype=np.float32)
    tfeat[:, 0::2] = np.arange(128, dtype=np.float32)[:, None]
    tfeat[:, 1::2] = np.arange(16, dtype=np.float32)[None, :]

    nc = build_nc(NG=16, NE=8, H=4096)
    in_maps = []
    for c in range(8):
        b, cm = c // 4, c % 4
        perm, dmask = _perm_and_mask(cm)
        xb = np.ascontiguousarray(
            x[b].reshape(16, 512, 1024)[:, perm, :].reshape(8192, 1024))
        in_maps.append({"x": xb, "wq": wq, "wk": wk, "wvd": wvd, "wvup": wvup,
                        "rwT": rwT, "w1": w1, "w2": w2,
                        "ident_in": ident, "dmask_in": dmask,
                        "ltri_in": ltri, "gw_in": gw_m, "iota_in": iota,
                        "tfeat_in": tfeat})
    res = run_bass_kernel_spmd(nc, in_maps, core_ids=list(range(8)))
    out = np.empty((2, 8192, 1024), dtype=np.float32)
    for c in range(8):
        b, cm = c // 4, c % 4
        out[b, cm::4, :] = res.results[c]["out"]
    return out


# version 43
# speedup vs baseline: 1.1940x; 1.0061x over previous
import numpy as np
from contextlib import ExitStack

import concourse.bass as bass
import concourse.mybir as mybir
from concourse.bass_utils import run_bass_kernel_spmd

F32 = mybir.dt.float32
BF16 = mybir.dt.bfloat16
AF = mybir.ActivationFunctionType
ALU = mybir.AluOpType
AX = mybir.AxisListType

NEG = -3.0e38


class Rec:

    def __init__(self, cnt=None):
        self.ops = []
        self.cnt = dict(cnt or {})

    def E(self, eng, fn, wait=(), inc=()):
        w = tuple((s, v) for (s, v) in wait if v > 0)
        vals = []
        for s, a in inc:
            self.cnt[s] = self.cnt.get(s, 0) + a
            vals.append((s, a))
        self.ops.append((eng, w, fn, tuple(vals)))

    def v(self, s):
        return self.cnt.get(s, 0)


def replay(nc, rec, sems):
    engs = [e for e in ("sync", "gpsimd", "tensor", "vector", "scalar")
            if any(op[0] == e for op in rec.ops)]
    with nc.Block() as block:
        def mk(ename):
            ops = [op for op in rec.ops if op[0] == ename]

            def emit(eng):
                for _, waits, fn, incs in ops:
                    for (s, v) in waits:
                        eng.wait_ge(sems[s], v)
                    if fn is not None:
                        inst = fn(eng)
                        for (s, a) in incs:
                            inst.then_inc(sems[s], a)
            emit.__name__ = "_" + ename
            return emit

        for e in engs:
            getattr(block, e)(mk(e))


def build_nc(NG=16, NE=8, H=4096, debug=False):
    C, D = 1024, 256
    M = 128 * NG
    KEYS = 512 * NG
    MH = M // 2
    NC8 = C // 128
    NH8 = H // 128
    NHC = H // 512
    H8G = min(8, NH8)
    NHG = NH8 // H8G
    TGW = min(512, MH)
    NTG = MH // TGW
    NTP = MH // 128

    nc = bass.Bass("TRN2", target_bir_lowering=False)

    x = nc.dram_tensor("x", [KEYS, C], F32, kind="ExternalInput")
    wq = nc.dram_tensor("wq", [C, D], F32, kind="ExternalInput")
    wk = nc.dram_tensor("wk", [C, D], F32, kind="ExternalInput")
    wvd = nc.dram_tensor("wvd", [C, D], F32, kind="ExternalInput")
    wvup = nc.dram_tensor("wvup", [D, C], F32, kind="ExternalInput")
    rwT = nc.dram_tensor("rwT", [C, NE], F32, kind="ExternalInput")
    w1 = nc.dram_tensor("w1", [NE, C, H], F32, kind="ExternalInput")
    w2 = nc.dram_tensor("w2", [NE, H, C], F32, kind="ExternalInput")
    ident_in = nc.dram_tensor("ident_in", [128, 128], F32, kind="ExternalInput")
    dmask_in = nc.dram_tensor("dmask_in", [128, 2048], F32, kind="ExternalInput")
    ltri_in = nc.dram_tensor("ltri_in", [128, 128], F32, kind="ExternalInput")
    gw_in = nc.dram_tensor("gw_in", [128, 128], F32, kind="ExternalInput")
    iota_in = nc.dram_tensor("iota_in", [128, 384], F32, kind="ExternalInput")
    tfeat_in = nc.dram_tensor("tfeat_in", [128, 2 * NG], F32, kind="ExternalInput")
    xnt_dram = nc.dram_tensor("xnt_scratch", [128, NG * C], BF16, kind="ExternalOutput")
    out = nc.dram_tensor("out", [M, C], F32, kind="ExternalOutput")
    if debug:
        d_xnT = nc.dram_tensor("d_xnT", [128, NC8 * M], F32, kind="ExternalOutput")
        d_KT = nc.dram_tensor("d_KT", [128, 2 * KEYS], F32, kind="ExternalOutput")
        d_Vd = nc.dram_tensor("d_Vd", [128, (KEYS // 128) * D], F32, kind="ExternalOutput")
        d_QT = nc.dram_tensor("d_QT", [128, 2 * M], F32, kind="ExternalOutput")
        d_P = nc.dram_tensor("d_P", [128, KEYS], F32, kind="ExternalOutput")
        d_comb = nc.dram_tensor("d_comb", [128, NG * NE], F32, kind="ExternalOutput")
        d_r = nc.dram_tensor("d_r", [128, NG], F32, kind="ExternalOutput")
        d_ctxT = nc.dram_tensor("d_ctxT", [128, 2 * M], F32, kind="ExternalOutput")
        d_y = nc.dram_tensor("d_y", [128, NTP * C], F32, kind="ExternalOutput")
        dbg_sb = None

    es0 = ExitStack()
    sems = {}
    for s in ("sxa", "sxb", "sxc", "sxd", "ss", "si", "swa", "swb", "swc", "swd",
              "swe", "swf", "swg", "swh", "sxt",
              "sxr0", "sxr1", "so0", "so1", "pe", "ve", "ac"):
        sems[s] = es0.enter_context(nc.semaphore(s))

    ctxT = es0.enter_context(nc.sbuf_tensor("ctxT", [128, 2 * M], BF16))
    comb = es0.enter_context(nc.sbuf_tensor("comb", [128, NG * NE], F32))
    mask_f32 = es0.enter_context(nc.sbuf_tensor("mask_f32", [128, NG * NE], F32))
    wvup_sb = es0.enter_context(nc.sbuf_tensor("wvup_sb", [128, 2 * C], BF16))
    rinv_all = es0.enter_context(nc.sbuf_tensor("rinv_all", [128, NG], F32))
    r_all = es0.enter_context(nc.sbuf_tensor("r_all", [128, NG], F32))
    ident = es0.enter_context(nc.sbuf_tensor("ident", [128, 128], F32))
    fcol = es0.enter_context(nc.sbuf_tensor("fcol", [128, 64], F32))
    _vfk = [0]

    def VF():
        k = _vfk[0] % 64
        _vfk[0] += 1
        rec.E("vector", (lambda k: lambda e: e.memset(fcol[:, k:k + 1], 0.0))(k))

    rec = Rec()
    E = rec.E

    esA = ExitStack()
    xnT_own = esA.enter_context(nc.sbuf_tensor("xnT_own", [128, NC8 * M], BF16))
    x_st = [esA.enter_context(nc.sbuf_tensor(f"x_st{i}", [128, 2 * C], F32)) for i in range(4)]
    stats = esA.enter_context(nc.sbuf_tensor("stats", [128, 288], F32))
    xn_f32 = [esA.enter_context(nc.sbuf_tensor(f"xn_f32{i}", [128, C], BF16)) for i in range(8)]
    ident_bf = esA.enter_context(nc.sbuf_tensor("ident_bfA", [128, 128], BF16))
    xnT_g = esA.enter_context(nc.sbuf_tensor("xnT_g", [128, NC8 * 512], BF16))
    KT = esA.enter_context(nc.sbuf_tensor("KT", [128, 2 * KEYS], BF16))
    Vd = esA.enter_context(nc.sbuf_tensor("Vd", [128, (KEYS // 128) * D], BF16))
    QT = esA.enter_context(nc.sbuf_tensor("QT", [128, 2 * M], BF16))
    STm_sb = [esA.enter_context(nc.sbuf_tensor(f"STm_sb{i}", [128, 256], F32)) for i in range(2)]
    kt2_sb = esA.enter_context(nc.sbuf_tensor("kt2_sb", [128, 2 * 512], BF16))
    qt2_sb = esA.enter_context(nc.sbuf_tensor("qt2_sb", [128, 2 * 256], BF16))
    ones_bf = esA.enter_context(nc.sbuf_tensor("ones_bf", [128, 1], BF16))
    bnd = esA.enter_context(nc.sbuf_tensor("bnd", [128, 16], F32))
    negm_sb = esA.enter_context(nc.sbuf_tensor("negm_sb", [128, 1], F32))
    mk2_bf = esA.enter_context(nc.sbuf_tensor("mk2_bf", [1, 4], F32))
    nm_bf = esA.enter_context(nc.sbuf_tensor("nm_bf", [1, 2], BF16))
    ones_row = esA.enter_context(nc.sbuf_tensor("ones_row", [1, 128], BF16))
    one_f32 = esA.enter_context(nc.sbuf_tensor("one_f32", [1, 1], F32))
    r_row = esA.enter_context(nc.sbuf_tensor("r_row", [1, 256], F32))
    wq_sb = esA.enter_context(nc.sbuf_tensor("wq_sb", [128, NC8 * D], BF16))
    wk_sb = esA.enter_context(nc.sbuf_tensor("wk_sb", [128, NC8 * D], BF16))
    wvd_sb = esA.enter_context(nc.sbuf_tensor("wvd_sb", [128, NC8 * D], BF16))
    rwT_sb = esA.enter_context(nc.sbuf_tensor("rwT_sb", [128, NC8 * NE], BF16))
    dmask = esA.enter_context(nc.sbuf_tensor("dmask", [128, 2048], F32))
    pt_sb = [esA.enter_context(nc.sbuf_tensor(f"pt_sb{i}", [128, 256], BF16)) for i in range(4)]

    lg_sb = esA.enter_context(nc.sbuf_tensor("lg_sb", [128, 4 * NE + 8], F32))
    pool = [esA.enter_context(nc.psum_tensor(f"pool{i}", [128, 512], F32)) for i in range(8)]

    E("gpsimd", lambda g: g.dma_start(out=ident[:], in_=ident_in[:]), inc=[("si", 16)])
    E("gpsimd", lambda g: g.dma_start(
        out=wq_sb.rearrange("p (c d) -> p c d", c=NC8),
        in_=wq.rearrange("(c p) d -> p c d", p=128)), inc=[("ss", 16)])
    E("gpsimd", lambda g: g.dma_start(
        out=wk_sb.rearrange("p (c d) -> p c d", c=NC8),
        in_=wk.rearrange("(c p) d -> p c d", p=128)), inc=[("ss", 16)])
    E("gpsimd", lambda g: g.dma_start(
        out=wvd_sb.rearrange("p (c d) -> p c d", c=NC8),
        in_=wvd.rearrange("(c p) d -> p c d", p=128)), inc=[("ss", 16)])
    E("gpsimd", lambda g: g.dma_start(
        out=wvup_sb.rearrange("p (h c) -> p h c", h=2),
        in_=wvup.rearrange("(h p) c -> p h c", p=128)), inc=[("ss", 16)])
    E("gpsimd", lambda g: g.dma_start(
        out=rwT_sb.rearrange("p (c n) -> p c n", c=NC8),
        in_=rwT.rearrange("(c p) n -> p c n", p=128)), inc=[("ss", 16)])
    E("gpsimd", lambda g: g.dma_start(out=dmask[:], in_=dmask_in[:]), inc=[("ss", 16)])
    SW_SMALL = ("ss", rec.v("ss"))

    z = ("ve", 0)
    pool_free = [z] * 8
    ptb_free = [("pe", 0)] * 4
    xn_free = [("pe", 0)] * 8
    tr_rr = [0]

    NU = KEYS // 256
    x_loaded = [None] * NU
    x_consumed = [None] * NU

    def load_x_unit(u):
        waits = []
        if u >= 4:
            waits.append(("ve", x_consumed[u - 4]))
        sem = ("sxa", "sxb", "sxc", "sxd")[u % 4]
        E("sync", (lambda u: lambda e: e.dma_start(
            out=x_st[u % 4].rearrange("p (t e) -> p t e", t=2),
            in_=x[256 * u:256 * (u + 1), :].rearrange("(t p) e -> p t e", p=128)))(u),
            wait=waits, inc=[(sem, 16)])
        x_loaded[u] = (sem, rec.v(sem))

    for _u in range(min(4, NU)):
        load_x_unit(_u)

    E("vector", lambda e: e.tensor_copy(ident_bf[:], ident[:]), wait=[("si", 16)],
      inc=[("ve", 1)])
    E("vector", lambda e: e.memset(ones_bf[:], 1.0), inc=[("ve", 1)])
    E("vector", lambda e: e.memset(ones_row[:], 1.0), inc=[("ve", 1)])
    E("vector", lambda e: e.memset(one_f32[:], 1.0), inc=[("ve", 1)])
    E("vector", lambda e: e.memset(mk2_bf[:, 0:1], 0.0), inc=[("ve", 1)])
    ve_ones = rec.v("ve")

    prev_ve_xn = [0, 0, 0]
    prev_ac_rstd = [0, 0, 0]

    ac_exp_prev = 0
    pe_q2_prev = 0
    pe_k2_prev = 0
    nm_prev = 0
    rrow_prev = [0]
    ac_bnd_prev = [0]
    mq2_state = {"ve": 0}
    mk2_prev = [0]
    rinv_ve = []
    ln_state = {}
    xn_state = {}
    mk2_state = {"ve": ve_ones}

    def emit_lnstats(g):
        p2 = g % 3
        base = 96 * p2
        gst = stats[:, base:base + 96]
        xts = []
        for ts in range(4):
            ptg = 4 * g + ts
            u, loc = ptg // 2, ptg % 2
            x_t = x_st[u % 4][:, loc * C:(loc + 1) * C]
            xts.append((u, x_t))
            E("vector", (lambda x_t, gst, ts: lambda e: e.bn_stats(
                gst[:, 16 * ts:16 * ts + 6], x_t[:, 0:512]))(x_t, gst, ts),
              wait=[x_loaded[u]], inc=[("ve", 1)])
            E("vector", (lambda x_t, gst, ts: lambda e: e.bn_stats(
                gst[:, 16 * ts + 6:16 * ts + 12], x_t[:, 512:1024]))(x_t, gst, ts),
              inc=[("ve", 1)])
        ve_stats = rec.v("ve")
        for ts in range(4):
            out_ap = bass.AP(stats, base + 64 + ts, [[288, 128], [4, 2]])
            E("vector", (lambda gst, ts, out_ap: lambda e: e.bn_aggr(
                out_ap, gst[:, 16 * ts:16 * ts + 12]))(gst, ts, out_ap),
              wait=([("ve", ve_stats), ("ac", prev_ac_rstd[p2])] if ts == 0 else []),
              inc=[("ve", 1)])
        ve_agg = rec.v("ve")
        ln_state[g] = (gst, xts, ve_agg)

    def emit_lnfin(g):
        p2 = g % 3
        gst, xts, ve_agg = ln_state.pop(g)
        VF()
        E("vector", (lambda gst: lambda e: e.tensor_scalar(
            gst[:, 68:72], gst[:, 68:72], 1e-5, None, ALU.add))(gst),
          wait=[("ve", ve_agg)], inc=[("ve", 1)])
        ve_c1 = rec.v("ve")
        E("scalar", (lambda gst: lambda e: e.activation(
            gst[:, 72:76], gst[:, 68:72], AF.Ln))(gst),
          wait=[("ve", ve_c1), ("ve", prev_ve_xn[p2])], inc=[("ac", 1)])
        ac_ln = rec.v("ac")
        E("scalar", (lambda gst: lambda e: e.activation(
            gst[:, 76:80], gst[:, 72:76], AF.Exp, scale=-0.5))(gst),
          wait=[("ac", ac_ln)], inc=[("ac", 1)])
        ac_rstd = rec.v("ac")
        prev_ac_rstd[p2] = ac_rstd
        ve_xns = []
        for ts in range(4):
            u, x_t = xts[ts]
            ti = 4 * (g % 2) + ts
            w_ts = [("ac", ac_rstd), ("ve", ve_agg), xn_free[ti]]
            if ts == 0:
                w_ts.append(("sxt", 16 * g))
            E("vector", (lambda x_t, gst, ts, ti: lambda e: e.tensor_scalar(
                xn_f32[ti][:], x_t, gst[:, 64 + ts:65 + ts], gst[:, 76 + ts:77 + ts],
                ALU.subtract, ALU.mult))(x_t, gst, ts, ti),
              wait=w_ts, inc=[("ve", 1)])
            ve_xn = rec.v("ve")
            ve_xns.append(ve_xn)
            prev_ve_xn[p2] = ve_xn
            if ts == 0:
                E("gpsimd", (lambda g, ti: lambda e: e.dma_start(
                    out=xnt_dram[:, C * g:C * (g + 1)], in_=xn_f32[ti][:]))(g, ti),
                  wait=[("ve", ve_xn)], inc=[("sxt", 16)])
            if ts % 2 == 1:
                x_consumed[u] = ve_xn
                if u + 4 < NU:
                    load_x_unit(u + 4)
        xn_state[g] = ve_xns

    tr_fillers = []
    xn_edges = {}
    tr_bank = [0]

    def queue_xn(g):
        ve_xns = xn_state.pop(g)
        items = []
        for ts in range(4):
            for c in range(NC8):
                items.append((g, ts, c, ve_xns[ts]))
        tr_fillers.extend(items)

    def _emit_tr(g, ts, c, ve_xn):
        k = 6 + (tr_bank[0] % 2)
        tr_bank[0] += 1
        w = [("ve", ve_xn), pool_free[k]]
        if g == 0 and ts == 0 and c == 0:
            w.append(SW_SMALL)
            w.append(("si", 16))
        E("tensor", (lambda ts, c, k, ti: lambda e: e.matmul(
            pool[k][:, 0:128], xn_f32[ti][:, c * 128:(c + 1) * 128],
            ident_bf[:], start=True, stop=True,
            skip_group_check=True))(ts, c, k, 4 * (g % 2) + ts),
          wait=w, inc=[("pe", 1)])
        pe_tr = rec.v("pe")
        dst = (lambda c, ts: lambda: xnT_g[:, 512 * c + 128 * ts:512 * c + 128 * (ts + 1)])(c, ts)
        if k % 2 == 0:
            E("vector", (lambda k, dst: lambda e: e.tensor_copy(dst(), pool[k][:, 0:128]))(k, dst),
              wait=[("pe", pe_tr)], inc=[("ve", 1)])
            pool_free[k] = ("ve", rec.v("ve"))
        else:
            E("scalar", (lambda k, dst: lambda e: e.copy(dst(), pool[k][:, 0:128]))(k, dst),
              wait=[("pe", pe_tr)], inc=[("ac", 1)])
            pool_free[k] = ("ac", rec.v("ac"))
        if ts == 3 and c == NC8 - 1:
            edges = [("ve", rec.v("ve")), ("ac", rec.v("ac"))]
            xn_edges[g] = edges
            for t_ in range(4):
                xn_free[4 * (g % 2) + t_] = ("pe", pe_tr)

    def drain_tr(n):
        while n > 0 and tr_fillers:
            _emit_tr(*tr_fillers.pop(0))
            n -= 1

    def emit_xn(g):
        while g not in xn_edges:
            assert tr_fillers
            _emit_tr(*tr_fillers.pop(0))
        return xn_edges[g]

    def emit_proj(g, g_done, with_bound=False):
        nonlocal pe_k2_prev
        if with_bound and g >= 2:
            emit_bound(g // 2)
        for c in range(NC8):
            E("vector", (lambda c, g: lambda e: e.tensor_copy(
                xnT_own[:, M * c + 128 * g:M * c + 128 * (g + 1)],
                xnT_g[:, 512 * c:512 * c + 128]))(c, g),
              wait=(g_done if c == 0 else []), inc=[("ve", 1)])
        ve_own = rec.v("ve")
        kt_banks = [7, 6]
        for dh in range(2):
            pb = kt_banks[dh]
            for c in range(NC8):
                E("tensor", (lambda dh, c, pb: lambda e: e.matmul(
                    pool[pb][:], wk_sb[:, D * c + 128 * dh:D * c + 128 * dh + 128],
                    xnT_g[:, 512 * c:512 * (c + 1)],
                    start=(c == 0), stop=(c == NC8 - 1)))(dh, c, pb),
                  wait=(g_done + [pool_free[pb]] if c == 0 else []),
                  inc=([("pe", 1)] if c == NC8 - 1 else []))
            pe_kt = rec.v("pe")
            if dh == 0:
                E("vector", (lambda dh, g, pb: lambda e: e.tensor_copy(
                    KT[:, KEYS * dh + 512 * g:KEYS * dh + 512 * (g + 1)], pool[pb][:]))(dh, g, pb),
                  wait=[("pe", pe_kt)], inc=[("ve", 1)])
                pool_free[pb] = ("ve", rec.v("ve"))
            else:
                E("scalar", (lambda dh, g, pb: lambda e: e.copy(
                    KT[:, KEYS * dh + 512 * g:KEYS * dh + 512 * (g + 1)], pool[pb][:]))(dh, g, pb),
                  wait=[("pe", pe_kt)], inc=[("ac", 1)])
                pool_free[pb] = ("ac", rec.v("ac"))
        ve_kt = rec.v("ve")
        ac_kt = rec.v("ac")
        E("scalar", (lambda g: lambda e: e.activation(
            kt2_sb[:, 0:512], KT[:, 512 * g:512 * (g + 1)], AF.Square))(g),
          wait=[("ve", ve_kt), ("pe", pe_k2_prev)], inc=[("ac", 1)])
        E("scalar", (lambda g: lambda e: e.activation(
            kt2_sb[:, 512:1024], KT[:, KEYS + 512 * g:KEYS + 512 * (g + 1)], AF.Square))(g),
          wait=[("ac", ac_kt)], inc=[("ac", 1)])
        ac_kt2 = rec.v("ac")
        for dh in range(2):
            pb = 6 + dh
            for c in range(NC8):
                E("tensor", (lambda dh, c, g, pb: lambda e: e.matmul(
                    pool[pb][:, 0:128], wq_sb[:, D * c + 128 * dh:D * c + 128 * dh + 128],
                    xnT_own[:, M * c + 128 * g:M * c + 128 * (g + 1)],
                    start=(c == 0), stop=(c == NC8 - 1)))(dh, c, g, pb),
                  wait=([("ve", ve_own), pool_free[pb]] if c == 0 else []),
                  inc=([("pe", 1)] if c == NC8 - 1 else []))
            pe_qt = rec.v("pe")
            if dh == 0:
                E("vector", (lambda dh, g, pb: lambda e: e.tensor_copy(
                    QT[:, M * dh + 128 * g:M * dh + 128 * (g + 1)], pool[pb][:, 0:128]))(dh, g, pb),
                  wait=[("pe", pe_qt)], inc=[("ve", 1)])
                pool_free[pb] = ("ve", rec.v("ve"))
            else:
                E("scalar", (lambda dh, g, pb: lambda e: e.copy(
                    QT[:, M * dh + 128 * g:M * dh + 128 * (g + 1)], pool[pb][:, 0:128]))(dh, g, pb),
                  wait=[("pe", pe_qt)], inc=[("ac", 1)])
                pool_free[pb] = ("ac", rec.v("ac"))
        qt_ve, qt_ac = rec.v("ve"), rec.v("ac")
        if with_bound and g < 2:
            emit_bound(g // 2)
        for ts in range(4):
            pb = kt_banks[ts % 2]
            for c in range(NC8):
                E("tensor", (lambda ts, c, pb: lambda e: e.matmul(
                    pool[pb][:, 0:D], xnT_g[:, 512 * c + 128 * ts:512 * c + 128 * (ts + 1)],
                    wvd_sb[:, D * c:D * (c + 1)],
                    start=(c == 0), stop=(c == NC8 - 1)))(ts, c, pb),
                  wait=([pool_free[pb]] if c == 0 else []),
                  inc=([("pe", 1)] if c == NC8 - 1 else []))
            pe_vd = rec.v("pe")
            if ts % 2 == 0:
                E("vector", (lambda g, ts, pb: lambda e: e.tensor_copy(
                    Vd[:, (4 * g + ts) * D:(4 * g + ts + 1) * D], pool[pb][:, 0:D]))(g, ts, pb),
                  wait=[("pe", pe_vd)], inc=[("ve", 1)])
                pool_free[pb] = ("ve", rec.v("ve"))
            else:
                E("scalar", (lambda g, ts, pb: lambda e: e.copy(
                    Vd[:, (4 * g + ts) * D:(4 * g + ts + 1) * D], pool[pb][:, 0:D]))(g, ts, pb),
                  wait=[("pe", pe_vd)], inc=[("ac", 1)])
                pool_free[pb] = ("ac", rec.v("ac"))
        if with_bound and g >= 2:
            emit_q2_update(g // 2, qt_ve, qt_ac)
        for dh in range(2):
            E("tensor", (lambda dh: lambda e: e.matmul(
                pool[7][0:1, 0:512], ones_bf[:], kt2_sb[:, 512 * dh:512 * (dh + 1)],
                start=(dh == 0), stop=(dh == 1)))(dh),
              wait=([pool_free[7], ("ac", ac_kt2), ("ve", ve_ones)] if dh == 0 else []),
              inc=([("pe", 1)] if dh == 1 else []))
        pe_k2 = rec.v("pe")
        pe_k2_prev = pe_k2
        E("vector", lambda e: e.reduce_max(mk2_bf[0:1, 1:2], pool[7][0:1, 0:512], AX.X),
          wait=[("pe", pe_k2), ("ve", mk2_state["ve"])], inc=[("ve", 1)])
        v_k2r = rec.v("ve")
        pool_free[7] = ("ve", v_k2r)
        VF()
        E("vector", lambda e: e.tensor_max(mk2_bf[0:1, 0:1], mk2_bf[0:1, 0:1], mk2_bf[0:1, 1:2]),
          wait=[("ve", v_k2r)], inc=[("ve", 1)])
        mk2_state["ve"] = rec.v("ve")
        proj_state["ve_own"] = ve_own
        if with_bound:
            emit_bound_bcast()
        return (rec.v("ve"), rec.v("ac"))

    def emit_logits(g):
        ve_own = proj_state["ve_own"]
        for c in range(NC8):
            E("tensor", (lambda c, g: lambda e: e.matmul(
                pool[6][:, 0:NE], xnT_own[:, M * c + 128 * g:M * c + 128 * (g + 1)],
                rwT_sb[:, NE * c:NE * (c + 1)],
                start=(c == 0), stop=(c == NC8 - 1)))(c, g),
              wait=([("ve", ve_own), pool_free[6]] if c == 0 else []),
              inc=([("pe", 1)] if c == NC8 - 1 else []))
        pe_lg = rec.v("pe")
        LG = lg_sb[:, 0:NE]
        EE = lg_sb[:, NE:2 * NE]
        EQ1 = lg_sb[:, 2 * NE:3 * NE]
        PM = lg_sb[:, 3 * NE:4 * NE]
        c_m1 = lg_sb[:, 4 * NE + 2:4 * NE + 3]
        c_m2 = lg_sb[:, 4 * NE + 3:4 * NE + 4]
        c_ws = lg_sb[:, 4 * NE + 4:4 * NE + 5]
        c_wi = lg_sb[:, 4 * NE + 5:4 * NE + 6]
        c_nm = lg_sb[:, 4 * NE + 6:4 * NE + 7]
        E("vector", lambda e: e.tensor_copy(LG, pool[6][:, 0:NE]),
          wait=[("pe", pe_lg)], inc=[("ve", 1)])
        v_lgc = rec.v("ve")
        pool_free[6] = ("ve", v_lgc)
        VF()
        E("vector", lambda e: e.reduce_max(c_nm, LG, AX.X, negate=True),
          wait=[("ve", v_lgc)], inc=[("ve", 1)])
        ve_lg = rec.v("ve")
        E("scalar", lambda e: e.activation(EE, LG, AF.Exp, bias=c_nm), wait=[("ve", ve_lg)],
          inc=[("ac", 1)])
        ac_e = rec.v("ac")
        E("vector", lambda e: e.reduce_max(c_m1, EE, AX.X), wait=[("ac", ac_e)], inc=[("ve", 1)])
        v_m1 = rec.v("ve")
        VF()
        E("vector", lambda e: e.tensor_scalar(EQ1, EE, c_m1, None, ALU.is_equal),
          wait=[("ve", v_m1)], inc=[("ve", 1)])
        v_eq1 = rec.v("ve")
        VF()
        E("vector", lambda e: e.scalar_tensor_tensor(PM, EQ1, -1e30, EE, ALU.mult, ALU.add),
          wait=[("ve", v_eq1)], inc=[("ve", 1)])
        v_pm = rec.v("ve")
        VF()
        E("vector", lambda e: e.reduce_max(c_m2, PM, AX.X), wait=[("ve", v_pm)], inc=[("ve", 1)])
        v_m2 = rec.v("ve")
        VF()
        E("vector", lambda e: e.tensor_scalar(PM, PM, c_m2, None, ALU.is_equal),
          wait=[("ve", v_m2)], inc=[("ve", 1)])
        v_pm2 = rec.v("ve")
        E("vector", lambda e: e.tensor_add(c_ws, c_m1, c_m2), inc=[("ve", 1)])
        v_ws = rec.v("ve")
        E("vector", lambda e: e.tensor_add(EQ1, EQ1, PM), wait=[("ve", v_pm2)], inc=[("ve", 1)])
        v_eq = rec.v("ve")
        E("vector", lambda e: e.reciprocal(c_wi, c_ws), wait=[("ve", v_ws)], inc=[("ve", 1)])
        v_wi = rec.v("ve")
        E("vector", lambda e: e.tensor_mul(EE, EE, EQ1), wait=[("ve", v_eq)], inc=[("ve", 1)])
        v_ee = rec.v("ve")
        E("vector", (lambda g: lambda e: e.tensor_copy(
            mask_f32[:, NE * g:NE * (g + 1)], EQ1))(g),
          wait=[("ve", v_eq)], inc=[("ve", 1)])
        VF()
        E("vector", (lambda g: lambda e: e.tensor_scalar(
            comb[:, NE * g:NE * (g + 1)], EE, c_wi, None, ALU.mult))(g),
          wait=[("ve", v_ee), ("ve", v_wi)], inc=[("ve", 1)])

    proj_state = {}

    bound_state = {}

    def emit_bound(p):
        nonlocal pe_q2_prev, nm_prev
        ve_pre = rec.v("ve")
        ac_pre = rec.v("ac")
        if p == 0:
            emit_q2_update(p, ve_pre, ac_pre)
            E("vector", lambda e: e.tensor_copy(bnd[0:1, 0:1], mk2_bf[0:1, 3:4]),
              wait=[("ve", mq2_state["ve"])], inc=[("ve", 1)])
            v_q2m = rec.v("ve")
            VF()
        else:
            E("vector", lambda e: e.tensor_copy(bnd[0:1, 0:1], mk2_bf[0:1, 3:4]),
              wait=[("ve", mq2_state["ve"]), ("ac", ac_bnd_prev[0])], inc=[("ve", 1)])
            v_q2m = rec.v("ve")
            VF()
        E("scalar", lambda e: e.activation(bnd[0:1, 1:2], bnd[0:1, 0:1], AF.Ln),
          wait=[("ve", v_q2m)], inc=[("ac", 1)])
        E("scalar", lambda e: e.activation(bnd[0:1, 2:3], mk2_bf[0:1, 0:1], AF.Ln),
          wait=[("ve", mk2_state["ve"] if p == 0 else mk2_prev[0])], inc=[("ac", 1)])
        ac_ln2 = rec.v("ac")
        E("vector", lambda e: e.tensor_add(bnd[0:1, 3:4], bnd[0:1, 1:2], bnd[0:1, 2:3]),
          wait=[("ac", ac_ln2)], inc=[("ve", 1)])
        v_sum = rec.v("ve")
        E("scalar", lambda e: e.activation(bnd[0:1, 4:5], bnd[0:1, 3:4], AF.Exp, scale=0.5),
          wait=[("ve", v_sum)], inc=[("ac", 1)])
        ac_m = rec.v("ac")
        ac_bnd_prev[0] = ac_m
        scale = -1.01 / 16.0 if p == 0 else -1.30 / 16.0
        E("vector", (lambda scale: lambda e: e.tensor_scalar(
            bnd[0:1, 5:6], bnd[0:1, 4:5], scale, None, ALU.mult))(scale),
          wait=[("ac", ac_m)], inc=[("ve", 1)])
        v_nm = rec.v("ve")
        VF()
        E("vector", lambda e: e.tensor_copy(nm_bf[0:1, 0:1], bnd[0:1, 5:6]),
          wait=[("ve", v_nm), ("ve", nm_prev)], inc=[("ve", 1)])
        v_nmc = rec.v("ve")
        nm_prev = v_nmc
        bound_state["v_nmc"] = v_nmc

    def emit_bound_bcast():
        v_nmc = bound_state["v_nmc"]
        E("tensor", lambda e: e.matmul(
            pool[3][:, 0:1], ones_row[0:1, :], nm_bf[0:1, 0:1],
            start=True, stop=True),
          wait=[pool_free[3], ("ve", v_nmc), ("ve", ve_ones)], inc=[("pe", 1)])
        pe_bc = rec.v("pe")
        E("vector", lambda e: e.tensor_copy(negm_sb[:], pool[3][:, 0:1]),
          wait=[("pe", pe_bc), ("ac", ac_exp_prev)], inc=[("ve", 1)])
        bound_state["ve_negm"] = rec.v("ve")
        pool_free[3] = ("ve", bound_state["ve_negm"])

    def emit_q2_update(p, ve_pre, ac_pre):
        nonlocal pe_q2_prev
        E("scalar", (lambda p: lambda e: e.activation(
            qt2_sb[:, 0:256], QT[:, 256 * p:256 * (p + 1)], AF.Square))(p),
          wait=[("ve", ve_pre), ("ac", ac_pre), ("pe", pe_q2_prev)], inc=[("ac", 1)])
        E("scalar", (lambda p: lambda e: e.activation(
            qt2_sb[:, 256:512], QT[:, M + 256 * p:M + 256 * (p + 1)], AF.Square))(p),
          inc=[("ac", 1)])
        ac_qt2 = rec.v("ac")
        for dh in range(2):
            E("tensor", (lambda dh: lambda e: e.matmul(
                pool[6][0:1, 0:256], ones_bf[:], qt2_sb[:, 256 * dh:256 * (dh + 1)],
                start=(dh == 0), stop=(dh == 1)))(dh),
              wait=([pool_free[6], ("ac", ac_qt2)] if dh == 0 else []),
              inc=([("pe", 1)] if dh == 1 else []))
        pe_q2 = rec.v("pe")
        pe_q2_prev = pe_q2
        E("vector", lambda e: e.reduce_max(mk2_bf[0:1, 2:3], pool[6][0:1, 0:256], AX.X),
          wait=[("pe", pe_q2), ("ve", mq2_state["ve"])], inc=[("ve", 1)])
        v_q = rec.v("ve")
        pool_free[6] = ("ve", v_q)
        VF()
        if p == 0:
            E("vector", lambda e: e.tensor_copy(mk2_bf[0:1, 3:4], mk2_bf[0:1, 2:3]),
              wait=[("ve", v_q)], inc=[("ve", 1)])
        else:
            E("vector", lambda e: e.tensor_max(mk2_bf[0:1, 3:4], mk2_bf[0:1, 3:4],
                                               mk2_bf[0:1, 2:3]),
              wait=[("ve", v_q)], inc=[("ve", 1)])
        mq2_state["ve"] = rec.v("ve")

    ST_BANKS = (0, 1, 5)

    def make_attn_pair(p, pre):
        nkb = 8 * p + 8
        ve_pre, ac_pre = pre
        st_of = {}
        diag_ac = [ac_exp_prev, ac_exp_prev]
        state = {"kb": 0, "acs": {}, "primed": False, "pe_ctx": None, "ve_rr": None}

        def emit_st(kb):
            sb = ST_BANKS[kb % 3]
            w = []
            for dh in range(2):
                if dh == 0:
                    w = [pool_free[sb]]
                    if kb == 0:
                        w += [("ve", ve_pre), ("ac", ac_pre)]
                E("tensor", (lambda p, kb, dh, sb: lambda e: e.matmul(
                    pool[sb][:, 0:256],
                    KT[:, KEYS * dh + 128 * kb:KEYS * dh + 128 * (kb + 1)],
                    QT[:, M * dh + 256 * p:M * dh + 256 * (p + 1)],
                    start=(dh == 0), stop=(dh == 1),
                    skip_group_check=True))(p, kb, dh, sb),
                  wait=(w if dh == 0 else []), inc=([("pe", 1)] if dh == 1 else []))
            st_of[kb] = ("pe", rec.v("pe"))

        def emit_exp(kb):
            sb = ST_BANKS[kb % 3]
            pb = kb % 4
            j = kb - 8 * p
            if j >= 0:
                db = kb % 2
                E("vector", (lambda kb, sb, j, db: lambda e: e.tensor_add(
                    STm_sb[db][:], pool[sb][:, 0:256],
                    dmask[:, 256 * j:256 * (j + 1)]))(kb, sb, j, db),
                  wait=[st_of[kb], SW_SMALL, ("ac", diag_ac[db])], inc=[("ve", 1)])
                pool_free[sb] = ("ve", rec.v("ve"))
                E("scalar", (lambda pb, db: lambda e: e.activation(
                    pt_sb[pb][:], STm_sb[db][:], AF.Exp,
                    bias=negm_sb[:], scale=0.0625))(pb, db),
                  wait=[("ve", rec.v("ve")), ptb_free[pb]], inc=[("ac", 1)])
                diag_ac[db] = rec.v("ac")
            else:
                E("scalar", (lambda kb, sb, pb: lambda e: e.activation(
                    pt_sb[pb][:], pool[sb][:, 0:256], AF.Exp,
                    bias=negm_sb[:], scale=0.0625))(kb, sb, pb),
                  wait=[st_of[kb], ("ve", bound_state["ve_negm"]), ptb_free[pb]], inc=[("ac", 1)])
                pool_free[sb] = ("ac", rec.v("ac"))
            return rec.v("ac")

        def emit_rctx(kb, ac_p):
            pb = kb % 4
            E("tensor", (lambda kb, pb, nkb: lambda e: e.matmul(
                pool[4][0:1, 0:256], ones_bf[:], pt_sb[pb][:],
                start=(kb == 0), stop=(kb == nkb - 1),
                skip_group_check=True))(kb, pb, nkb),
              wait=[("ac", ac_p)] + ([pool_free[4]] if kb == 0 else []),
              inc=[])
            for dh in range(2):
                E("tensor", (lambda kb, dh, pb, nkb: lambda e: e.matmul(
                    pool[2 + dh][:, 0:256], Vd[:, D * kb + 128 * dh:D * kb + 128 * dh + 128],
                    pt_sb[pb][:], start=(kb == 0), stop=(kb == nkb - 1),
                    skip_group_check=True))(kb, dh, pb, nkb),
                  wait=([pool_free[2 + dh]] if kb == 0 else []),
                  inc=([("pe", 1)] if dh == 1 else []))
            ptb_free[pb] = ("pe", rec.v("pe"))

        def finish():
            nonlocal ac_exp_prev
            ac_exp_prev = rec.v("ac")
            pe_ctx = rec.v("pe")
            state["pe_ctx"] = pe_ctx
            E("vector", (lambda p: lambda e: e.tensor_copy(
                ctxT[:, 256 * p:256 * (p + 1)], pool[2][:, 0:256]))(p),
              wait=[("pe", pe_ctx)], inc=[("ve", 1)])
            pool_free[2] = ("ve", rec.v("ve"))
            E("scalar", (lambda p: lambda e: e.copy(
                ctxT[:, M + 256 * p:M + 256 * (p + 1)], pool[3][:, 0:256]))(p),
              wait=[("pe", pe_ctx)], inc=[("ac", 1)])
            pool_free[3] = ("ac", rec.v("ac"))
            E("scalar", lambda e: e.copy(r_row[:], pool[4][0:1, 0:256]),
              wait=[("pe", pe_ctx), ("pe", rrow_prev[0])], inc=[("ac", 1)])
            pool_free[4] = ("ac", rec.v("ac"))
            state["ve_rr"] = ("ac", rec.v("ac"))

        def run(upto):
            if not state["primed"]:
                emit_st(0)
                if nkb > 1:
                    emit_st(1)
                state["acs"][0] = emit_exp(0)
                state["primed"] = True
            upto = min(upto, nkb)
            kb = state["kb"]
            while kb < upto:
                if kb + 2 < nkb:
                    emit_st(kb + 2)
                if kb + 1 < nkb:
                    state["acs"][kb + 1] = emit_exp(kb + 1)
                emit_rctx(kb, state["acs"].pop(kb))
                drain_tr(2)
                kb += 1
            state["kb"] = kb
            if kb == nkb and state["pe_ctx"] is None:
                finish()

        def tail():
            rr_edge = state["ve_rr"]
            for half in range(2):
                E("tensor", (lambda half: lambda e: e.matmul(
                    pool[3][:, 0:1], r_row[0:1, 128 * half:128 * (half + 1)], one_f32[:],
                    start=True, stop=True))(half),
                  wait=[rr_edge, pool_free[3]],
                  inc=[("pe", 1)])
                pe_rt = rec.v("pe")
                rrow_prev[0] = pe_rt
                E("vector", (lambda p, half: lambda e: e.tensor_copy(
                    r_all[:, 2 * p + half:2 * p + half + 1], pool[3][:, 0:1]))(p, half),
                  wait=[("pe", pe_rt)], inc=[("ve", 1)])
                pool_free[3] = ("ve", rec.v("ve"))
                ve_r = rec.v("ve")
                VF()
                E("vector", (lambda p, half: lambda e: e.reciprocal(
                    rinv_all[:, 2 * p + half:2 * p + half + 1],
                    r_all[:, 2 * p + half:2 * p + half + 1]))(p, half),
                  wait=[("ve", ve_r)], inc=[("ve", 1)])
                rinv_ve.append(rec.v("ve"))
        return run, tail

    for _wu in range(40):
        E("tensor", (lambda: lambda e: e.matmul(
            pool[6][:, 0:128], ident[:], ident[:], start=True, stop=True))(),
          wait=([("si", 16)] if _wu == 0 else []), inc=([("pe", 1)] if _wu == 39 else []))
    pool_free[6] = ("pe", rec.v("pe"))

    emit_lnstats(0)
    emit_lnfin(0)
    queue_xn(0)
    pair = None
    pair_tail = None
    for g in range(NG):
        g_done = emit_xn(g)
        if g + 1 < NG:
            emit_lnstats(g + 1)
        if pair_tail is not None:
            pair_tail()
            pair_tail = None
        mk2_prev[0] = mk2_state["ve"]
        pre = emit_proj(g, g_done, with_bound=(g % 2 == 1))
        if g % 2 == 0 and pair is not None:
            if g + 1 < NG:
                emit_lnfin(g + 1)
                queue_xn(g + 1)
            run, tl = pair
            run(10 ** 9)
            pair = None
            pair_tail = tl
        elif g % 2 == 1:
            if g + 1 < NG:
                emit_lnfin(g + 1)
                queue_xn(g + 1)
            run, tl = make_attn_pair(g // 2, pre)
            run((8 * (g // 2) + 8) // 2)
            pair = (run, tl)
        else:
            if g + 1 < NG:
                emit_lnfin(g + 1)
                queue_xn(g + 1)
        emit_logits(g)
    if pair is not None:
        run, tl = pair
        run(10 ** 9)
        tl()

    if debug:
        wA = [("ve", rec.v("ve")), ("ac", rec.v("ac")), ("pe", rec.v("pe"))]
        E("gpsimd", lambda g: g.dma_start(out=d_xnT[:], in_=xnT_own[:]), wait=wA, inc=[("so0", 16)])
        E("gpsimd", lambda g: g.dma_start(out=d_KT[:], in_=KT[:]), inc=[("so0", 16)])
        E("gpsimd", lambda g: g.dma_start(out=d_Vd[:], in_=Vd[:]), inc=[("so0", 16)])
        E("gpsimd", lambda g: g.dma_start(out=d_QT[:], in_=QT[:]), inc=[("so0", 16)])
        E("gpsimd", lambda g: g.dma_start(out=d_comb[:], in_=comb[:]), inc=[("so0", 16)])
        E("gpsimd", lambda g: g.dma_start(out=d_r[:], in_=r_all[:]), inc=[("so0", 16)])
        E("gpsimd", lambda g: g.dma_start(out=d_ctxT[:], in_=ctxT[:]), inc=[("so0", 16)])
        E("gpsimd", None, wait=[("so0", rec.v("so0"))])


    ve_A0 = rec.v("ve")

    ve_A = rec.v("ve")
    ac_A = rec.v("ac")
    pe_A = rec.v("pe")

    replay(nc, rec, sems)
    esA.close()

    rec = Rec(rec.cnt)
    E = rec.E
    barA = [("ve", ve_A), ("ac", ac_A), ("pe", pe_A)]

    NWW, CAPW, SPE, NSEC = 2, 384, 768, 6
    W1CH = 16
    W2CH = 16

    esB = ExitStack()
    xn_sel = esB.enter_context(nc.sbuf_tensor("xn_sel", [128, NC8 * SPE], BF16))
    hT = esB.enter_context(nc.sbuf_tensor("hT", [128, NH8 * SPE], BF16))
    y_half = esB.enter_context(nc.sbuf_tensor("y_half", [128, NG * C], BF16))
    y_selE = [esB.enter_context(nc.sbuf_tensor(f"y_selE{i}", [128, NSEC * C], BF16))
              for i in range(2)]
    w1sb = [esB.enter_context(nc.sbuf_tensor(f"w1sb{i}", [128, NC8 * 256], BF16)) for i in range(3)]
    w2sb = [esB.enter_context(nc.sbuf_tensor(f"w2sb{i}", [128, 2 * 512], BF16)) for i in range(4)]
    w2f32 = [esB.enter_context(nc.sbuf_tensor(f"w2f32{i}", [128, 2 * 512], F32)) for i in range(3)]
    iota_t = esB.enter_context(nc.sbuf_tensor("iota_t", [128, 384], F32))
    rel_sb = esB.enter_context(nc.sbuf_tensor("rel_sb", [128, 128], F32))
    onesB = esB.enter_context(nc.sbuf_tensor("onesB", [128, 1], BF16))
    ones1b = esB.enter_context(nc.sbuf_tensor("ones1b", [1, 128], BF16))
    st_ring = esB.enter_context(nc.sbuf_tensor("st_ring", [128, 6 * 128], BF16))
    tfeat = esB.enter_context(nc.sbuf_tensor("tfeat", [128, 2 * NG], BF16))
    tls_sb = [esB.enter_context(nc.sbuf_tensor(f"tls_sb{i}", [128, 12], F32)) for i in range(2)]
    eq_tl = esB.enter_context(nc.sbuf_tensor("eq_tl", [128, 6 * 128], BF16))
    m_col = esB.enter_context(nc.sbuf_tensor("m_col", [128, 1], F32))
    tls_snap = esB.enter_context(nc.sbuf_tensor("tls_snap", [128, 12], F32))
    yps = [esB.enter_context(nc.psum_tensor(f"ypsB{i}", [128, 512], F32)) for i in range(8)]
    es_g = ExitStack()
    xn_tok = es_g.enter_context(nc.sbuf_tensor("xn_tok", [128, NG * C], BF16))
    S8 = es_g.enter_context(nc.sbuf_tensor("S8", [128, 8 * 384], BF16))
    es_r = ExitStack()
    ltri_sb = es_r.enter_context(nc.sbuf_tensor("ltri_sb", [128, 128], BF16))
    gw_sb = es_r.enter_context(nc.sbuf_tensor("gw_sb", [128, 128], BF16))
    mask_bf = es_r.enter_context(nc.sbuf_tensor("mask_bf", [128, 128], BF16))
    offs_sb = es_r.enter_context(nc.sbuf_tensor("offs_sb", [1, 128], F32))
    tot_bfc = es_r.enter_context(nc.sbuf_tensor("tot_bfc", [128, 1], BF16))
    offs_hi = es_r.enter_context(nc.sbuf_tensor("offs_hi", [1, 128], BF16))
    offs_lo = es_r.enter_context(nc.sbuf_tensor("offs_lo", [1, 128], BF16))
    offs_hf = es_r.enter_context(nc.sbuf_tensor("offs_hf", [1, 128], F32))
    if debug:
        d_rel = nc.dram_tensor("d_rel", [128, 128], F32, kind="ExternalOutput")
        d_yh = nc.dram_tensor("d_yh", [128, NG * C], F32, kind="ExternalOutput")
        d_xsel = nc.dram_tensor("d_xsel", [128, NC8 * SPE], F32, kind="ExternalOutput")

    poolB = [list(barA) for _ in range(8)]

    E("gpsimd", lambda g: g.dma_start(out=ltri_sb[:], in_=ltri_in[:]), inc=[("ss", 16)])
    E("gpsimd", lambda g: g.dma_start(out=gw_sb[:], in_=gw_in[:]), inc=[("ss", 16)])
    E("gpsimd", lambda g: g.dma_start(out=iota_t[:], in_=iota_in[:]), inc=[("ss", 16)])
    E("gpsimd", lambda g: g.dma_start(out=tfeat[:], in_=tfeat_in[:]), inc=[("ss", 16)])
    SS_B = ("ss", rec.v("ss"))
    E("sync", lambda e: e.dma_start(out=xn_tok[:, 0:8 * C], in_=xnt_dram[:, 0:8 * C]),
      wait=[("sxt", 16 * NG)], inc=[("swh", 16)])
    E("sync", lambda e: e.dma_start(out=xn_tok[:, 8 * C:NG * C], in_=xnt_dram[:, 8 * C:NG * C]),
      inc=[("swh", 16)])
    SI_XT = ("swh", rec.v("swh"))

    E("vector", lambda e: e.memset(onesB[:], 1.0), wait=barA, inc=[("ve", 1)])
    E("vector", lambda e: e.memset(ones1b[:], 1.0), inc=[("ve", 1)])
    ve_onesB = rec.v("ve")

    E("vector", lambda e: e.tensor_copy(mask_bf[:], mask_f32[:]), inc=[("ve", 1)])
    ve_mbf = rec.v("ve")
    E("tensor", lambda t: t.matmul(yps[0][:, 0:128], ltri_sb[:], mask_bf[:],
                                   start=True, stop=True),
      wait=poolB[0] + [("ve", ve_mbf), SS_B], inc=[("pe", 1)])
    pe_pp = rec.v("pe")
    E("vector", lambda e: e.tensor_copy(rel_sb[:], yps[0][:, 0:128]),
      wait=[("pe", pe_pp)], inc=[("ve", 1)])
    ve_pp = rec.v("ve")
    poolB[0] = [("ve", ve_pp)]
    E("tensor", lambda t: t.matmul(yps[1][:, 0:1], mask_bf[:], onesB[:],
                                   start=True, stop=True),
      wait=poolB[1] + [("ve", ve_onesB), ("ve", ve_mbf)], inc=[("pe", 1)])
    pe_tot = rec.v("pe")
    E("scalar", lambda s: s.copy(tot_bfc[:], yps[1][:, 0:1]),
      wait=[("pe", pe_tot)], inc=[("ac", 1)])
    ac_tot = rec.v("ac")
    poolB[1] = [("ac", ac_tot)]
    E("tensor", lambda t: t.matmul(yps[2][0:1, 0:128], tot_bfc[:], gw_sb[:],
                                   start=True, stop=True),
      wait=poolB[2] + [("ac", ac_tot)], inc=[("pe", 1)])
    pe_of = rec.v("pe")
    E("vector", lambda e: e.tensor_copy(offs_sb[:], yps[2][0:1, 0:128]),
      wait=[("pe", pe_of)], inc=[("ve", 1)])
    ve_of = rec.v("ve")
    poolB[2] = [("ve", ve_of)]
    E("scalar", lambda s: s.copy(offs_hi[:], offs_sb[:]),
      wait=[("ve", ve_of)], inc=[("ac", 1)])
    ac_oh = rec.v("ac")
    E("scalar", lambda s: s.copy(offs_hf[:], offs_hi[:]), wait=[("ac", ac_oh)], inc=[("ac", 1)])
    ac_ohf = rec.v("ac")
    E("vector", lambda v: v.tensor_tensor(offs_sb[:], offs_sb[:], offs_hf[:], ALU.subtract),
      wait=[("ac", ac_ohf)], inc=[("ve", 1)])
    v_lf = rec.v("ve")
    VF()
    E("vector", lambda v: v.tensor_copy(offs_lo[:], offs_sb[:]), wait=[("ve", v_lf)],
      inc=[("ve", 1)])
    ve_ol = rec.v("ve")
    for part, wsem in ((offs_hi, ("ac", ac_oh)), (offs_lo, ("ve", ve_ol))):
        E("tensor", (lambda part: lambda t: t.matmul(
            yps[3][:, 0:128], ones1b[:], part[:],
            start=(part is offs_hi), stop=(part is offs_lo)))(part),
          wait=(poolB[3] + [wsem, ("ve", ve_onesB)] if part is offs_hi else [wsem]),
          inc=([("pe", 1)] if part is offs_lo else []))
    pe_ob = rec.v("pe")
    VF()
    E("vector", lambda e: e.tensor_tensor(rel_sb[:], rel_sb[:], yps[3][:, 0:128], ALU.add),
      wait=[("pe", pe_ob), ("ve", ve_pp)], inc=[("ve", 1)])
    v_r1 = rec.v("ve")
    VF()
    E("vector", lambda e: e.tensor_tensor(rel_sb[:], rel_sb[:], mask_f32[:], ALU.mult),
      wait=[("ve", v_r1)], inc=[("ve", 1)])
    v_r2 = rec.v("ve")
    VF()
    E("vector", lambda e: e.tensor_scalar(rel_sb[:], rel_sb[:], 1.0, None, ALU.subtract),
      wait=[("ve", v_r2)], inc=[("ve", 1)])
    ve_rel = rec.v("ve")
    poolB[3] = [("ve", rec.v("ve"))]
    if debug:
        E("gpsimd", lambda g: g.dma_start(out=d_rel[:], in_=rel_sb[:]),
          wait=[("ve", ve_rel)], inc=[("so0", 16)])

    es_r.close()

    w1_loaded = {}
    w1_free = {}
    w1_ptr = [0]
    W1SEM = ("swa", "swb", "swc")

    def issue_w1():
        idx = w1_ptr[0]
        if idx >= NE * W1CH:
            return
        w1_ptr[0] += 1
        e_, ch = idx // W1CH, idx % W1CH
        buf = idx % 3
        waits = [w1_free[idx - 3]] if idx >= 3 else []
        E("gpsimd", (lambda e_, ch, buf: lambda g: g.dma_start(
            out=w1sb[buf].rearrange("p (c h) -> p c h", c=NC8),
            in_=w1[e_, :, 256 * ch:256 * (ch + 1)].rearrange("(c p) h -> p c h", p=128)))(e_, ch, buf),
            wait=waits, inc=[(W1SEM[buf], 16)])
        w1_loaded[idx] = (W1SEM[buf], rec.v(W1SEM[buf]))

    w2_loaded = {}
    w2_free = {}
    w2sb_free = {}
    w2_cast = {}
    w2_ptr = [0]
    w2_cast_ptr = [0]
    W2SEM = ("swe", "swf", "swg", "swh")

    def issue_w2():
        idx = w2_ptr[0]
        if idx >= NE * 2 * W2CH:
            return
        w2_ptr[0] += 1
        e_ = idx // (2 * W2CH)
        cc = (idx // W2CH) % 2
        ch = idx % W2CH
        buf = idx % 3
        q = ("sync", "scalar", "sync")[buf]
        sem = ("swe", "swf", "swg")[buf]
        waits = [w2_free[idx - 3]] if idx >= 3 else []
        E(q, (lambda e_, cc, ch, buf: lambda g: g.dma_start(
            out=w2f32[buf].rearrange("p (h c) -> p h c", h=2),
            in_=w2[e_, 256 * ch:256 * (ch + 1), 512 * cc:512 * (cc + 1)]
                .rearrange("(h p) c -> p h c", p=128)))(e_, cc, ch, buf),
            wait=waits, inc=[(sem, 16)])
        w2_loaded[idx] = (sem, rec.v(sem))

    def cast_w2_upto(idx):
        while w2_cast_ptr[0] <= idx:
            k = w2_cast_ptr[0]
            w2_cast_ptr[0] += 1
            fb, bb = k % 3, k % 4
            waits = [w2_loaded[k]]
            if k >= 4:
                waits.append(w2sb_free[k - 4])
            if k % 2 == 0:
                E("scalar", (lambda fb, bb: lambda s: s.copy(w2sb[bb][:], w2f32[fb][:]))(fb, bb),
                  wait=waits, inc=[("ac", 1)])
                w2_cast[k] = ("ac", rec.v("ac"))
            else:
                E("vector", (lambda fb, bb: lambda v: v.tensor_copy(w2sb[bb][:], w2f32[fb][:]))(fb, bb),
                  wait=waits, inc=[("ve", 1)])
                w2_cast[k] = ("ve", rec.v("ve"))
            w2_free[k] = w2_cast[k]

    for _ in range(3):
        issue_w1()
        issue_w2()

    s8_use_pe = [0]
    gather_done_ve = {}

    def emit_s8_builds(e_, w):
        for k in range(8):
            g_ = w * 8 + k
            wts = [("ve", ve_rel)] if (k == 0) else []
            if k == 0 and s8_use_pe[0]:
                wts.append(("pe", s8_use_pe[0]))
            E("vector", (lambda k, g_, e_: lambda v: v.tensor_scalar(
                S8[:, 384 * k:384 * (k + 1)], iota_t[:],
                rel_sb[:, 8 * g_ + e_:8 * g_ + e_ + 1], None, ALU.is_equal))(k, g_, e_),
              wait=wts, inc=[("ve", 1)])
        return rec.v("ve")

    def emit_gather_tile(e_, idx, bank, ve_s8):
        w, c = idx // 8, idx % 8
        for k in range(8):
            wts = []
            if k == 0:
                wts = poolB[bank] + [("ve", ve_s8), SI_XT]
            E("tensor", (lambda w, c, k, bank: lambda t: t.matmul(
                yps[bank][:, 0:384],
                xn_tok[:, (w * 8 + k) * C + 128 * c:(w * 8 + k) * C + 128 * (c + 1)],
                S8[:, 384 * k:384 * (k + 1)],
                start=(k == 0), stop=(k == 7), skip_group_check=True))(w, c, k, bank),
              wait=wts, inc=([("pe", 1)] if k == 7 else []))
        pe_g = rec.v("pe")
        s8_use_pe[0] = pe_g
        E("vector", (lambda w, c, bank: lambda v: v.tensor_copy(
            xn_sel[:, c * SPE + w * CAPW:c * SPE + w * CAPW + 384], yps[bank][:, 0:384]))(w, c, bank),
          wait=[("pe", pe_g)], inc=[("ve", 1)])
        poolB[bank] = [("ve", rec.v("ve"))]

    def emit_gather_tls(e_, w, bank):
        buf = e_ % 2
        for sec3 in range(3):
            for k in range(8):
                wts = []
                if k == 0:
                    wts = list(poolB[bank]) + [SS_B]
                E("tensor", (lambda w, k, sec3, bank: lambda t: t.matmul(
                    yps[bank][:, 0:2],
                    S8[:, 384 * k + 128 * sec3:384 * k + 128 * sec3 + 128],
                    tfeat[:, 2 * (w * 8 + k):2 * (w * 8 + k) + 2],
                    start=(k == 0), stop=(k == 7), skip_group_check=True))(w, k, sec3, bank),
                  wait=wts, inc=([("pe", 1)] if k == 7 else []))
            pe_t = rec.v("pe")
            s8_use_pe[0] = pe_t
            E("vector", (lambda w, sec3, bank, buf: lambda v: v.tensor_copy(
                tls_sb[buf][:, (w * 3 + sec3) * 2:(w * 3 + sec3) * 2 + 2],
                yps[bank][:, 0:2]))(w, sec3, bank, buf),
              wait=[("pe", pe_t)], inc=[("ve", 1)])
            poolB[bank] = [("ve", rec.v("ve"))]

    gelu_chunk = {}
    hb_rr = [0]

    def emit_h_chunk(e_, ch):
        w1idx = e_ * W1CH + ch
        for hp2 in range(2):
            for sg in range(2):
                hb = 6 + (hb_rr[0] % 2)
                hb_rr[0] += 1
                for c in range(NC8):
                    wts = []
                    if c == 0:
                        wts = poolB[hb] + [w1_loaded[w1idx]]
                        if ch == 0 and hp2 == 0 and sg == 0:
                            wts.append(("ve", gather_done_ve[e_]))
                    E("tensor", (lambda ch, hp2, sg, c, hb, buf: lambda t: t.matmul(
                        yps[hb][:, 0:384],
                        w1sb[buf][:, 256 * c + 128 * hp2:256 * c + 128 * (hp2 + 1)],
                        xn_sel[:, c * SPE + sg * 384:c * SPE + (sg + 1) * 384],
                        start=(c == 0), stop=(c == NC8 - 1),
                        skip_group_check=True))(ch, hp2, sg, c, hb, w1idx % 3),
                      wait=wts, inc=([("pe", 1)] if c == NC8 - 1 else []))
                pe_h = rec.v("pe")
                H8 = 2 * ch + hp2
                E("scalar", (lambda H8, sg, hb: lambda s: s.activation(
                    hT[:, H8 * SPE + sg * 384:H8 * SPE + (sg + 1) * 384],
                    yps[hb][:, 0:384], AF.Gelu))(H8, sg, hb),
                  wait=[("pe", pe_h)], inc=[("ac", 1)])
                poolB[hb] = [("ac", rec.v("ac"))]
        w1_free[w1idx] = ("pe", rec.v("pe"))
        issue_w1()
        gelu_chunk[(e_, ch)] = rec.v("ac")

    yevac_ve = {}
    yev_sect = {}

    def emit_y_half(e_, cc, ch, shalf):
        w2idx = e_ * 2 * W2CH + cc * W2CH + ch
        if shalf == 0:
            cast_w2_upto(min(w2idx + 1, NE * 2 * W2CH - 1))
        for s6 in range(3 * shalf, 3 * shalf + 3):
            for h8x in range(2):
                H8 = 2 * ch + h8x
                first = (ch == 0 and h8x == 0)
                last = (ch == W2CH - 1 and h8x == 1)
                wts = []
                if h8x == 0 and s6 == 3 * shalf:
                    wts = [w2_cast[w2idx], ("ac", gelu_chunk[(e_, ch)])]
                if first:
                    wts = wts + poolB[s6]
                E("tensor", (lambda s6, H8, h8x, first, last, buf: lambda t: t.matmul(
                    yps[s6][:],
                    hT[:, H8 * SPE + s6 * 128:H8 * SPE + (s6 + 1) * 128],
                    w2sb[buf][:, h8x * 512:(h8x + 1) * 512],
                    start=first, stop=last,
                    skip_group_check=True))(s6, H8, h8x, first, last, w2idx % 4),
                  wait=wts,
                  inc=([("pe", 1)] if (last or (h8x == 1 and s6 == 3 * shalf + 2)) else []))
        if shalf == 1:
            w2sb_free[w2idx] = ("pe", rec.v("pe"))
            issue_w2()

    def emit_y_evacs(e_, cc):
        pe_y = rec.v("pe")
        buf = e_ % 2
        for s6 in range(NSEC):
            E("vector", (lambda s6, cc, buf: lambda v: v.tensor_copy(
                y_selE[buf][:, s6 * C + cc * 512:s6 * C + cc * 512 + 512], yps[s6][:]))(s6, cc, buf),
              wait=[("pe", pe_y)], inc=[("ve", 1)])
            poolB[s6] = [("ve", rec.v("ve"))]
            yev_sect[(e_, cc, s6)] = rec.v("ve")
        if cc == 1:
            yevac_ve[e_] = rec.v("ve")

    ring_free = [None] * 6
    ring_ve = [0] * 6
    eq_state = {"e": None, "ve": 0}
    mcol_reader = [0]

    def emit_eq_tl(se):
        buf = se % 2
        E("vector", (lambda buf: lambda v: v.tensor_copy(tls_snap[:], tls_sb[buf][:]))(buf),
          wait=[("ve", mcol_reader[0])], inc=[("ve", 1)])
        for s6 in range(NSEC):
            E("vector", (lambda s6, buf: lambda v: v.tensor_scalar(
                eq_tl[:, s6 * 128:(s6 + 1) * 128], iota_t[:, 0:128],
                tls_sb[buf][:, s6 * 2:s6 * 2 + 1], None, ALU.is_equal))(s6, buf),
              inc=[("ve", 1)])
        eq_state["e"] = se
        eq_state["ve"] = rec.v("ve")

    def emit_st_one(se, g_, sloc):
        buf = se % 2
        s6 = (g_ // 8) * 3 + sloc
        E("vector", (lambda g_, s6: lambda v: v.tensor_scalar(
            m_col[:], tls_snap[:, s6 * 2 + 1:s6 * 2 + 2], float(g_), None,
            ALU.is_equal))(g_, s6),
          wait=[("ve", max(mcol_reader[0], eq_state["ve"]))], inc=[("ve", 1)])
        VF()
        rslot = (g_ * 3 + sloc) % 6
        wst = [("ve", rec.v("ve"))]
        if ring_free[rslot] is not None:
            wst.append(ring_free[rslot])
        E("vector", (lambda s6, rslot: lambda v: v.tensor_scalar(
            st_ring[:, rslot * 128:(rslot + 1) * 128], eq_tl[:, s6 * 128:(s6 + 1) * 128],
            m_col[:], None, ALU.mult))(s6, rslot),
          wait=wst, inc=[("ve", 1)])
        ring_ve[rslot] = rec.v("ve")
        mcol_reader[0] = rec.v("ve")

    def emit_partial(se, g_, cc, first_of_se, via_scalar=False):
        w = g_ // 8
        buf = se % 2
        for sloc in range(3):
            rslot = (g_ * 3 + sloc) % 6
            wts = []
            if sloc == 0:
                wts = poolB[7] + [("ve", max(ring_ve[(g_ * 3 + s) % 6] for s in range(3)))]
                if first_of_se:
                    wts.append(("ve", yevac_ve[se]))
            E("tensor", (lambda sloc, rslot, w, cc, buf: lambda t: t.matmul(
                yps[7][:],
                st_ring[:, rslot * 128:(rslot + 1) * 128],
                y_selE[buf][:, (w * 3 + sloc) * C + cc * 512:(w * 3 + sloc) * C + cc * 512 + 512],
                start=(sloc == 0), stop=(sloc == 2),
                skip_group_check=True))(sloc, rslot, w, cc, buf),
              wait=wts, inc=([("pe", 1)] if sloc == 2 else []))
        pe_p = rec.v("pe")
        if cc == 1:
            for sloc in range(3):
                ring_free[(g_ * 3 + sloc) % 6] = ("pe", pe_p)
        cmb = comb[:, NE * g_ + se:NE * g_ + se + 1]
        dst = y_half[:, g_ * C + cc * 512:g_ * C + cc * 512 + 512]
        if se == 0:
            E("vector", (lambda dst, cmb: lambda v: v.tensor_scalar(
                dst, yps[7][:], cmb, None, ALU.mult))(dst, cmb),
              wait=[("pe", pe_p)], inc=[("ve", 1)])
        elif via_scalar:
            pb = ptmp[cc]
            E("scalar", (lambda pb, cmb: lambda s: s.mul(pb[:], yps[7][:], cmb))(pb, cmb),
              wait=[("pe", pe_p)] + ptmp_free[cc], inc=[("ac", 1)])
            ac_pm = rec.v("ac")
            E("vector", (lambda dst, pb: lambda v: v.tensor_tensor(
                dst, pb[:], dst, ALU.add))(dst, pb),
              wait=[("ac", ac_pm)], inc=[("ve", 1)])
            ptmp_free[cc] = [("ve", rec.v("ve"))]
            poolB[7] = [("ac", ac_pm)]
            return rec.v("ve")
        else:
            E("vector", (lambda dst, cmb: lambda v: v.scalar_tensor_tensor(
                dst, yps[7][:], cmb, dst, ALU.mult, ALU.add))(dst, cmb),
              wait=[("pe", pe_p)], inc=[("ve", 1)])
        poolB[7] = [("ve", rec.v("ve"))]
        return rec.v("ve")

    stt_done = {}

    xres_loaded = {}
    xres_free = [[], []]
    out_free = [("so0", rec.v("so0")), ("so1", rec.v("so1"))]
    gtmp_free = [[], []]
    stmp_free = [[], []]

    def load_xres(g_):
        buf = g_ % 2
        sem = "sxr0" if buf == 0 else "sxr1"
        E("sync", (lambda g_, buf: lambda e: e.dma_start(
            out=xres[buf][:], in_=x[512 * g_:512 * g_ + 128, :]))(g_, buf),
            wait=xres_free[buf], inc=[(sem, 16)])
        xres_loaded[g_] = (sem, rec.v(sem))

    def emit_combine(g_):
        gtmp = gtmp2[g_ % 2]
        stmp = stmp2[g_ % 2]
        pe_sv = [0, 0]
        for cc in range(2):
            bank = (g_ % 2) * 2 + cc
            for dh in range(2):
                wts = []
                if dh == 0:
                    wts = poolB[bank]
                E("tensor", (lambda g_, cc, dh, bank: lambda t: t.matmul(
                    yps[bank][:],
                    ctxT[:, M * dh + 128 * g_:M * dh + 128 * (g_ + 1)],
                    wvup_sb[:, C * dh + 512 * cc:C * dh + 512 * (cc + 1)],
                    start=(dh == 0), stop=(dh == 1),
                    skip_group_check=True))(g_, cc, dh, bank),
                  wait=wts, inc=([("pe", 1)] if dh == 1 else []))
            pe_sv[cc] = rec.v("pe")
        for cc in range(2):
            E("scalar", (lambda g_, cc, gtmp: lambda s: s.activation(
                gtmp[:, 512 * cc:512 * (cc + 1)],
                y_half[:, g_ * C + cc * 512:g_ * C + cc * 512 + 512], AF.Silu))(g_, cc, gtmp),
              wait=[("ve", stt_done[g_])] + (gtmp_free[g_ % 2] if cc == 0 else []), inc=[("ac", 1)])
        ac_g = rec.v("ac")
        for cc in range(2):
            bank = (g_ % 2) * 2 + cc
            E("vector", (lambda g_, cc, bank, stmp, gtmp: lambda v: v.scalar_tensor_tensor(
                stmp[:, 512 * cc:512 * (cc + 1)], yps[bank][:],
                rinv_all[:, g_:g_ + 1], gtmp[:, 512 * cc:512 * (cc + 1)],
                ALU.mult, ALU.mult))(g_, cc, bank, stmp, gtmp),
              wait=[("pe", pe_sv[cc]), ("ve", rinv_ve[g_]), ("ac", ac_g)]
                   + (stmp_free[g_ % 2] if cc == 0 else []),
              inc=[("ve", 1)])
            poolB[bank] = [("ve", rec.v("ve"))]
        v_t1 = rec.v("ve")
        VF()
        for cc in range(2):
            E("vector", (lambda g_, cc, stmp, gtmp: lambda v: v.tensor_add(
                gtmp[:, 512 * cc:512 * (cc + 1)], stmp[:, 512 * cc:512 * (cc + 1)],
                xres[g_ % 2][:, 512 * cc:512 * (cc + 1)]))(g_, cc, stmp, gtmp),
              wait=[("ve", v_t1), xres_loaded[g_]] if cc == 0 else [],
              inc=[("ve", 1)])
        stmp_free[g_ % 2] = [("ve", rec.v("ve"))]
        ve_out = rec.v("ve")
        xres_free[g_ % 2] = [("ve", ve_out)]
        if g_ + 2 < NG:
            load_xres(g_ + 2)
        sem = "so0" if g_ % 2 == 0 else "so1"
        E("sync", (lambda g_, gtmp: lambda e: e.dma_start(
            out=out[128 * g_:128 * (g_ + 1), :], in_=gtmp[:]))(g_, gtmp),
          wait=[("ve", ve_out)], inc=[(sem, 16)])
        out_free[g_ % 2] = (sem, rec.v(sem))
        gtmp_free[g_ % 2] = [(sem, rec.v(sem))]

    ve_s8 = emit_s8_builds(0, 0)
    for idx in range(16):
        if idx == 8:
            emit_gather_tls(0, 0, 6)
            ve_s8 = emit_s8_builds(0, 1)
        emit_gather_tile(0, idx, 6 + idx % 2, ve_s8)
    emit_gather_tls(0, 1, 6)
    gather_done_ve[0] = rec.v("ve")

    for e_ in range(NE):
        for ch in range(W1CH):
            emit_h_chunk(e_, ch)
            if ch >= 1:
                emit_y_half(e_, 0, ch - 1, 0)
                emit_y_half(e_, 0, ch - 1, 1)
        emit_y_half(e_, 0, W1CH - 1, 0)
        emit_y_half(e_, 0, W1CH - 1, 1)
        emit_y_evacs(e_, 0)
        se = e_ - 1
        ve_s8 = None
        if se >= 0:
            emit_eq_tl(se)
        for j in range(W2CH):
            if e_ + 1 < NE and j in (0, 8):
                ve_s8 = emit_s8_builds(e_ + 1, j // 8)
            if se >= 0:
                emit_st_one(se, j, 0)
                emit_st_one(se, j, 1)
                emit_st_one(se, j, 2)
            emit_y_half(e_, 1, j, 0)
            if se >= 0 and j >= 1:
                emit_partial(se, j - 1, 0, first_of_se=(j == 1))
            emit_y_half(e_, 1, j, 1)
            if e_ + 1 < NE:
                emit_gather_tile(e_ + 1, j, 6, ve_s8)
                if j in (7, 15):
                    emit_gather_tls(e_ + 1, j // 8, 6)
            if se >= 0 and j >= 1:
                emit_partial(se, j - 1, 1, False)
        if e_ + 1 < NE:
            gather_done_ve[e_ + 1] = rec.v("ve")
        emit_y_evacs(e_, 1)
        if se >= 0:
            emit_partial(se, W2CH - 1, 0, first_of_se=False)
            emit_partial(se, W2CH - 1, 1, False)

    es_g.close()
    es_t = ExitStack()
    gtmp2 = [es_t.enter_context(nc.sbuf_tensor(f"gtmpB{i}", [128, C], F32)) for i in range(2)]
    stmp2 = [es_t.enter_context(nc.sbuf_tensor(f"stmpB{i}", [128, C], F32)) for i in range(2)]
    xres = [es_t.enter_context(nc.sbuf_tensor(f"xresB{i}", [128, C], F32)) for i in range(2)]
    ptmp = [es_t.enter_context(nc.sbuf_tensor(f"ptmpB{i}", [128, 512], BF16)) for i in range(2)]
    ptmp_free = [[], []]
    load_xres(0)
    load_xres(1)
    se = NE - 1
    emit_eq_tl(se)
    for sl in range(3):
        emit_st_one(se, 0, sl)
    for g_ in range(NG):
        if g_ + 1 < NG:
            for sl in range(3):
                emit_st_one(se, g_ + 1, sl)
        emit_partial(se, g_, 0, first_of_se=(g_ == 0), via_scalar=True)
        if g_ >= 1:
            emit_combine(g_ - 1)
        stt_done[g_] = emit_partial(se, g_, 1, False, via_scalar=True)
    emit_combine(NG - 1)

    if debug:
        E("gpsimd", lambda g: g.dma_start(out=d_yh[:], in_=y_half[:]),
          wait=[("ve", rec.v("ve"))], inc=[("so0", 16)])
        E("gpsimd", lambda g: g.dma_start(out=d_xsel[:], in_=xn_sel[:]), inc=[("so0", 16)])
    E("sync", None, wait=[("so0", rec.v("so0")), ("so1", rec.v("so1"))])
    replay(nc, rec, sems)
    es_t.close()
    esB.close()
    es0.close()
    return nc


def _perm_and_mask(cm):
    own = [r for r in range(512) if r % 4 == cm]
    rest = [r for r in range(512) if r % 4 != cm]
    perm = np.array(own + rest, dtype=np.int64)
    p = np.arange(128)[:, None]
    orig = perm[None, :]
    dmask = np.where(orig <= 4 * p + cm, 0.0, NEG).astype(np.float32)
    dm = np.zeros((128, 2048), dtype=np.float32)
    for j in range(4):
        dT = dmask[:, 128 * j:128 * (j + 1)].T
        dm[:, 256 * j:256 * j + 128] = dT
        dm[:, 256 * (4 + j) + 128:256 * (4 + j) + 256] = dT
        dm[:, 256 * (4 + j):256 * (4 + j) + 128] = NEG
    return perm, dm


def kernel(L0_attn_inputs, L0_attn_Wq, L0_attn_Wk, L0_attn_Wvdown, L0_attn_Wvup,
           L0_ffn_router_w, L0_ffn_w1, L0_ffn_w2):
    x = np.asarray(L0_attn_inputs, dtype=np.float32)
    wq = np.ascontiguousarray(np.asarray(L0_attn_Wq, dtype=np.float32))
    wk = np.ascontiguousarray(np.asarray(L0_attn_Wk, dtype=np.float32))
    wvd = np.ascontiguousarray(np.asarray(L0_attn_Wvdown, dtype=np.float32))
    wvup = np.ascontiguousarray(np.asarray(L0_attn_Wvup, dtype=np.float32))
    rwT = np.ascontiguousarray(np.asarray(L0_ffn_router_w, dtype=np.float32).T)
    w1 = np.ascontiguousarray(np.asarray(L0_ffn_w1, dtype=np.float32))
    w2 = np.ascontiguousarray(np.asarray(L0_ffn_w2, dtype=np.float32))
    ident = np.eye(128, dtype=np.float32)
    ltri = np.triu(np.ones((128, 128), dtype=np.float32))
    ii = np.arange(128)
    gi, ei = ii // 8, ii % 8
    gw_m = ((ei[:, None] == ei[None, :]) & ((gi[:, None] // 8) == (gi[None, :] // 8))
            & (gi[:, None] < gi[None, :])).astype(np.float32)
    iota = np.tile(np.arange(384, dtype=np.float32), (128, 1))
    tfeat = np.zeros((128, 32), dtype=np.float32)
    tfeat[:, 0::2] = np.arange(128, dtype=np.float32)[:, None]
    tfeat[:, 1::2] = np.arange(16, dtype=np.float32)[None, :]

    nc = build_nc(NG=16, NE=8, H=4096)
    in_maps = []
    for c in range(8):
        b, cm = c // 4, c % 4
        perm, dmask = _perm_and_mask(cm)
        xb = np.ascontiguousarray(
            x[b].reshape(16, 512, 1024)[:, perm, :].reshape(8192, 1024))
        in_maps.append({"x": xb, "wq": wq, "wk": wk, "wvd": wvd, "wvup": wvup,
                        "rwT": rwT, "w1": w1, "w2": w2,
                        "ident_in": ident, "dmask_in": dmask,
                        "ltri_in": ltri, "gw_in": gw_m, "iota_in": iota,
                        "tfeat_in": tfeat})
    res = run_bass_kernel_spmd(nc, in_maps, core_ids=list(range(8)))
    out = np.empty((2, 8192, 1024), dtype=np.float32)
    for c in range(8):
        b, cm = c // 4, c % 4
        out[b, cm::4, :] = res.results[c]["out"]
    return out


# version 44 (final)
# speedup vs baseline: 1.1970x; 1.0025x over previous
import numpy as np
from contextlib import ExitStack

import concourse.bass as bass
import concourse.mybir as mybir
from concourse.bass_utils import run_bass_kernel_spmd

F32 = mybir.dt.float32
BF16 = mybir.dt.bfloat16
AF = mybir.ActivationFunctionType
ALU = mybir.AluOpType
AX = mybir.AxisListType

NEG = -3.0e38


class Rec:

    def __init__(self, cnt=None):
        self.ops = []
        self.cnt = dict(cnt or {})

    def E(self, eng, fn, wait=(), inc=()):
        w = tuple((s, v) for (s, v) in wait if v > 0)
        vals = []
        for s, a in inc:
            self.cnt[s] = self.cnt.get(s, 0) + a
            vals.append((s, a))
        self.ops.append((eng, w, fn, tuple(vals)))

    def v(self, s):
        return self.cnt.get(s, 0)


def replay(nc, rec, sems):
    engs = [e for e in ("sync", "gpsimd", "tensor", "vector", "scalar")
            if any(op[0] == e for op in rec.ops)]
    with nc.Block() as block:
        def mk(ename):
            ops = [op for op in rec.ops if op[0] == ename]

            def emit(eng):
                for _, waits, fn, incs in ops:
                    for (s, v) in waits:
                        eng.wait_ge(sems[s], v)
                    if fn is not None:
                        inst = fn(eng)
                        for (s, a) in incs:
                            inst.then_inc(sems[s], a)
            emit.__name__ = "_" + ename
            return emit

        for e in engs:
            getattr(block, e)(mk(e))


def build_nc(NG=16, NE=8, H=4096, debug=False):
    C, D = 1024, 256
    M = 128 * NG
    KEYS = 512 * NG
    MH = M // 2
    NC8 = C // 128
    NH8 = H // 128
    NHC = H // 512
    H8G = min(8, NH8)
    NHG = NH8 // H8G
    TGW = min(512, MH)
    NTG = MH // TGW
    NTP = MH // 128

    nc = bass.Bass("TRN2", target_bir_lowering=False)

    x = nc.dram_tensor("x", [KEYS, C], F32, kind="ExternalInput")
    wq = nc.dram_tensor("wq", [C, D], F32, kind="ExternalInput")
    wk = nc.dram_tensor("wk", [C, D], F32, kind="ExternalInput")
    wvd = nc.dram_tensor("wvd", [C, D], F32, kind="ExternalInput")
    wvup = nc.dram_tensor("wvup", [D, C], F32, kind="ExternalInput")
    rwT = nc.dram_tensor("rwT", [C, NE], F32, kind="ExternalInput")
    w1 = nc.dram_tensor("w1", [NE, C, H], F32, kind="ExternalInput")
    w2 = nc.dram_tensor("w2", [NE, H, C], F32, kind="ExternalInput")
    ident_in = nc.dram_tensor("ident_in", [128, 128], F32, kind="ExternalInput")
    dmask_in = nc.dram_tensor("dmask_in", [128, 2048], F32, kind="ExternalInput")
    ltri_in = nc.dram_tensor("ltri_in", [128, 128], F32, kind="ExternalInput")
    gw_in = nc.dram_tensor("gw_in", [128, 128], F32, kind="ExternalInput")
    iota_in = nc.dram_tensor("iota_in", [128, 384], F32, kind="ExternalInput")
    tfeat_in = nc.dram_tensor("tfeat_in", [128, 2 * NG], F32, kind="ExternalInput")
    xnt_dram = nc.dram_tensor("xnt_scratch", [128, NG * C], BF16, kind="ExternalOutput")
    out = nc.dram_tensor("out", [M, C], F32, kind="ExternalOutput")
    if debug:
        d_xnT = nc.dram_tensor("d_xnT", [128, NC8 * M], F32, kind="ExternalOutput")
        d_KT = nc.dram_tensor("d_KT", [128, 2 * KEYS], F32, kind="ExternalOutput")
        d_Vd = nc.dram_tensor("d_Vd", [128, (KEYS // 128) * D], F32, kind="ExternalOutput")
        d_QT = nc.dram_tensor("d_QT", [128, 2 * M], F32, kind="ExternalOutput")
        d_P = nc.dram_tensor("d_P", [128, KEYS], F32, kind="ExternalOutput")
        d_comb = nc.dram_tensor("d_comb", [128, NG * NE], F32, kind="ExternalOutput")
        d_r = nc.dram_tensor("d_r", [128, NG], F32, kind="ExternalOutput")
        d_ctxT = nc.dram_tensor("d_ctxT", [128, 2 * M], F32, kind="ExternalOutput")
        d_y = nc.dram_tensor("d_y", [128, NTP * C], F32, kind="ExternalOutput")
        dbg_sb = None

    es0 = ExitStack()
    sems = {}
    for s in ("sxa", "sxb", "sxc", "sxd", "ss", "si", "swa", "swb", "swc", "swd",
              "swe", "swf", "swg", "swh", "sxt",
              "sxr0", "sxr1", "so0", "so1", "pe", "ve", "ac"):
        sems[s] = es0.enter_context(nc.semaphore(s))

    ctxT = es0.enter_context(nc.sbuf_tensor("ctxT", [128, 2 * M], BF16))
    comb = es0.enter_context(nc.sbuf_tensor("comb", [128, NG * NE], F32))
    mask_f32 = es0.enter_context(nc.sbuf_tensor("mask_f32", [128, NG * NE], F32))
    wvup_sb = es0.enter_context(nc.sbuf_tensor("wvup_sb", [128, 2 * C], BF16))
    rinv_all = es0.enter_context(nc.sbuf_tensor("rinv_all", [128, NG], F32))
    r_all = es0.enter_context(nc.sbuf_tensor("r_all", [128, NG], F32))
    ident = es0.enter_context(nc.sbuf_tensor("ident", [128, 128], F32))
    fcol = es0.enter_context(nc.sbuf_tensor("fcol", [128, 64], F32))
    _vfk = [0]

    def VF():
        k = _vfk[0] % 64
        _vfk[0] += 1
        rec.E("vector", (lambda k: lambda e: e.memset(fcol[:, k:k + 1], 0.0))(k))

    rec = Rec()
    E = rec.E

    esA = ExitStack()
    xnT_own = esA.enter_context(nc.sbuf_tensor("xnT_own", [128, NC8 * M], BF16))
    x_st = [esA.enter_context(nc.sbuf_tensor(f"x_st{i}", [128, 2 * C], F32)) for i in range(4)]
    stats = esA.enter_context(nc.sbuf_tensor("stats", [128, 192], F32))
    xn_f32 = [esA.enter_context(nc.sbuf_tensor(f"xn_f32{i}", [128, C], BF16)) for i in range(8)]
    ident_bf = esA.enter_context(nc.sbuf_tensor("ident_bfA", [128, 128], BF16))
    xnT_g = esA.enter_context(nc.sbuf_tensor("xnT_g", [128, NC8 * 512], BF16))
    KT = esA.enter_context(nc.sbuf_tensor("KT", [128, 2 * KEYS], BF16))
    Vd = esA.enter_context(nc.sbuf_tensor("Vd", [128, (KEYS // 128) * D], BF16))
    QT = esA.enter_context(nc.sbuf_tensor("QT", [128, 2 * M], BF16))
    STm_sb = [esA.enter_context(nc.sbuf_tensor(f"STm_sb{i}", [128, 256], F32)) for i in range(2)]
    kt2_sb = esA.enter_context(nc.sbuf_tensor("kt2_sb", [128, 2 * 512], BF16))
    qt2_sb = esA.enter_context(nc.sbuf_tensor("qt2_sb", [128, 2 * 256], BF16))
    ones_bf = esA.enter_context(nc.sbuf_tensor("ones_bf", [128, 1], BF16))
    bnd = esA.enter_context(nc.sbuf_tensor("bnd", [128, 16], F32))
    negm_sb = esA.enter_context(nc.sbuf_tensor("negm_sb", [128, 1], F32))
    mk2_bf = esA.enter_context(nc.sbuf_tensor("mk2_bf", [1, 4], F32))
    nm_bf = esA.enter_context(nc.sbuf_tensor("nm_bf", [1, 2], BF16))
    ones_row = esA.enter_context(nc.sbuf_tensor("ones_row", [1, 128], BF16))
    one_f32 = esA.enter_context(nc.sbuf_tensor("one_f32", [1, 1], F32))
    r_row = esA.enter_context(nc.sbuf_tensor("r_row", [1, 256], F32))
    wq_sb = esA.enter_context(nc.sbuf_tensor("wq_sb", [128, NC8 * D], BF16))
    wk_sb = esA.enter_context(nc.sbuf_tensor("wk_sb", [128, NC8 * D], BF16))
    wvd_sb = esA.enter_context(nc.sbuf_tensor("wvd_sb", [128, NC8 * D], BF16))
    rwT_sb = esA.enter_context(nc.sbuf_tensor("rwT_sb", [128, NC8 * NE], BF16))
    dmask = esA.enter_context(nc.sbuf_tensor("dmask", [128, 2048], F32))
    pt_sb = [esA.enter_context(nc.sbuf_tensor(f"pt_sb{i}", [128, 256], BF16)) for i in range(4)]

    lg_sb = esA.enter_context(nc.sbuf_tensor("lg_sb", [128, 4 * NE + 8], F32))
    pool = [esA.enter_context(nc.psum_tensor(f"pool{i}", [128, 512], F32)) for i in range(8)]

    E("gpsimd", lambda g: g.dma_start(out=ident[:], in_=ident_in[:]), inc=[("si", 16)])
    E("gpsimd", lambda g: g.dma_start(
        out=wq_sb.rearrange("p (c d) -> p c d", c=NC8),
        in_=wq.rearrange("(c p) d -> p c d", p=128)), inc=[("ss", 16)])
    E("gpsimd", lambda g: g.dma_start(
        out=wk_sb.rearrange("p (c d) -> p c d", c=NC8),
        in_=wk.rearrange("(c p) d -> p c d", p=128)), inc=[("ss", 16)])
    E("gpsimd", lambda g: g.dma_start(
        out=wvd_sb.rearrange("p (c d) -> p c d", c=NC8),
        in_=wvd.rearrange("(c p) d -> p c d", p=128)), inc=[("ss", 16)])
    E("gpsimd", lambda g: g.dma_start(
        out=wvup_sb.rearrange("p (h c) -> p h c", h=2),
        in_=wvup.rearrange("(h p) c -> p h c", p=128)), inc=[("ss", 16)])
    E("gpsimd", lambda g: g.dma_start(
        out=rwT_sb.rearrange("p (c n) -> p c n", c=NC8),
        in_=rwT.rearrange("(c p) n -> p c n", p=128)), inc=[("ss", 16)])
    E("gpsimd", lambda g: g.dma_start(out=dmask[:], in_=dmask_in[:]), inc=[("ss", 16)])
    SW_SMALL = ("ss", rec.v("ss"))

    z = ("ve", 0)
    pool_free = [z] * 8
    ptb_free = [("pe", 0)] * 4
    xn_free = [("pe", 0)] * 8
    tr_rr = [0]

    NU = KEYS // 256
    x_loaded = [None] * NU
    x_consumed = [None] * NU

    def load_x_unit(u):
        waits = []
        if u >= 4:
            waits.append(("ve", x_consumed[u - 4]))
        sem = ("sxa", "sxb", "sxc", "sxd")[u % 4]
        E("sync", (lambda u: lambda e: e.dma_start(
            out=x_st[u % 4].rearrange("p (t e) -> p t e", t=2),
            in_=x[256 * u:256 * (u + 1), :].rearrange("(t p) e -> p t e", p=128)))(u),
            wait=waits, inc=[(sem, 16)])
        x_loaded[u] = (sem, rec.v(sem))

    for _u in range(min(4, NU)):
        load_x_unit(_u)

    E("vector", lambda e: e.tensor_copy(ident_bf[:], ident[:]), wait=[("si", 16)],
      inc=[("ve", 1)])
    E("vector", lambda e: e.memset(ones_bf[:], 1.0), inc=[("ve", 1)])
    E("vector", lambda e: e.memset(ones_row[:], 1.0), inc=[("ve", 1)])
    E("vector", lambda e: e.memset(one_f32[:], 1.0), inc=[("ve", 1)])
    E("vector", lambda e: e.memset(mk2_bf[:, 0:1], 0.0), inc=[("ve", 1)])
    ve_ones = rec.v("ve")

    prev_ve_xn = [0, 0]
    prev_ac_rstd = [0, 0]

    ac_exp_prev = 0
    pe_q2_prev = 0
    pe_k2_prev = 0
    nm_prev = 0
    rrow_prev = [0]
    ac_bnd_prev = [0]
    mq2_state = {"ve": 0}
    mk2_prev = [0]
    rinv_ve = []
    ln_state = {}
    xn_state = {}
    mk2_state = {"ve": ve_ones}

    def emit_lnstats(g):
        p2 = g % 2
        base = 96 * p2
        gst = stats[:, base:base + 96]
        xts = []
        for ts in range(4):
            ptg = 4 * g + ts
            u, loc = ptg // 2, ptg % 2
            x_t = x_st[u % 4][:, loc * C:(loc + 1) * C]
            xts.append((u, x_t))
            E("vector", (lambda x_t, gst, ts: lambda e: e.bn_stats(
                gst[:, 16 * ts:16 * ts + 6], x_t[:, 0:512]))(x_t, gst, ts),
              wait=[x_loaded[u]], inc=[("ve", 1)])
            E("vector", (lambda x_t, gst, ts: lambda e: e.bn_stats(
                gst[:, 16 * ts + 6:16 * ts + 12], x_t[:, 512:1024]))(x_t, gst, ts),
              inc=[("ve", 1)])
        ve_stats = rec.v("ve")
        for ts in range(4):
            out_ap = bass.AP(stats, base + 64 + ts, [[192, 128], [4, 2]])
            E("vector", (lambda gst, ts, out_ap: lambda e: e.bn_aggr(
                out_ap, gst[:, 16 * ts:16 * ts + 12]))(gst, ts, out_ap),
              wait=([("ve", ve_stats), ("ac", prev_ac_rstd[p2])] if ts == 0 else []),
              inc=[("ve", 1)])
        ve_agg = rec.v("ve")
        ln_state[g] = (gst, xts, ve_agg)

    def emit_lnfin(g):
        p2 = g % 2
        gst, xts, ve_agg = ln_state.pop(g)
        VF()
        E("vector", (lambda gst: lambda e: e.tensor_scalar(
            gst[:, 68:72], gst[:, 68:72], 1e-5, None, ALU.add))(gst),
          wait=[("ve", ve_agg)], inc=[("ve", 1)])
        ve_c1 = rec.v("ve")
        E("scalar", (lambda gst: lambda e: e.activation(
            gst[:, 72:76], gst[:, 68:72], AF.Ln))(gst),
          wait=[("ve", ve_c1), ("ve", prev_ve_xn[p2])], inc=[("ac", 1)])
        ac_ln = rec.v("ac")
        E("scalar", (lambda gst: lambda e: e.activation(
            gst[:, 76:80], gst[:, 72:76], AF.Exp, scale=-0.5))(gst),
          wait=[("ac", ac_ln)], inc=[("ac", 1)])
        ac_rstd = rec.v("ac")
        prev_ac_rstd[p2] = ac_rstd
        ve_xns = []
        for ts in range(4):
            u, x_t = xts[ts]
            ti = 4 * (g % 2) + ts
            w_ts = [("ac", ac_rstd), ("ve", ve_agg), xn_free[ti]]
            if ts == 0:
                w_ts.append(("sxt", 16 * g))
            E("vector", (lambda x_t, gst, ts, ti: lambda e: e.tensor_scalar(
                xn_f32[ti][:], x_t, gst[:, 64 + ts:65 + ts], gst[:, 76 + ts:77 + ts],
                ALU.subtract, ALU.mult))(x_t, gst, ts, ti),
              wait=w_ts, inc=[("ve", 1)])
            ve_xn = rec.v("ve")
            ve_xns.append(ve_xn)
            prev_ve_xn[p2] = ve_xn
            if ts == 0:
                E("gpsimd", (lambda g, ti: lambda e: e.dma_start(
                    out=xnt_dram[:, C * g:C * (g + 1)], in_=xn_f32[ti][:]))(g, ti),
                  wait=[("ve", ve_xn)], inc=[("sxt", 16)])
            if ts % 2 == 1:
                x_consumed[u] = ve_xn
                if u + 4 < NU:
                    load_x_unit(u + 4)
        xn_state[g] = ve_xns

    tr_fillers = []
    xn_edges = {}
    tr_bank = [0]

    def queue_xn(g):
        ve_xns = xn_state.pop(g)
        items = []
        for ts in range(4):
            for c in range(NC8):
                items.append((g, ts, c, ve_xns[ts]))
        tr_fillers.extend(items)

    def _emit_tr(g, ts, c, ve_xn):
        k = 6 + (tr_bank[0] % 2)
        tr_bank[0] += 1
        w = [("ve", ve_xn), pool_free[k]]
        if g == 0 and ts == 0 and c == 0:
            w.append(SW_SMALL)
            w.append(("si", 16))
        E("tensor", (lambda ts, c, k, ti: lambda e: e.matmul(
            pool[k][:, 0:128], xn_f32[ti][:, c * 128:(c + 1) * 128],
            ident_bf[:], start=True, stop=True,
            skip_group_check=True))(ts, c, k, 4 * (g % 2) + ts),
          wait=w, inc=[("pe", 1)])
        pe_tr = rec.v("pe")
        dst = (lambda c, ts: lambda: xnT_g[:, 512 * c + 128 * ts:512 * c + 128 * (ts + 1)])(c, ts)
        if tr_bank[0] % 3 != 0:
            E("vector", (lambda k, dst: lambda e: e.tensor_copy(dst(), pool[k][:, 0:128]))(k, dst),
              wait=[("pe", pe_tr)], inc=[("ve", 1)])
            pool_free[k] = ("ve", rec.v("ve"))
        else:
            E("scalar", (lambda k, dst: lambda e: e.copy(dst(), pool[k][:, 0:128]))(k, dst),
              wait=[("pe", pe_tr)], inc=[("ac", 1)])
            pool_free[k] = ("ac", rec.v("ac"))
        if ts == 3 and c == NC8 - 1:
            edges = [("ve", rec.v("ve")), ("ac", rec.v("ac"))]
            xn_edges[g] = edges
            for t_ in range(4):
                xn_free[4 * (g % 2) + t_] = ("pe", pe_tr)

    def drain_tr(n):
        while n > 0 and tr_fillers:
            _emit_tr(*tr_fillers.pop(0))
            n -= 1

    def emit_xn(g):
        while g not in xn_edges:
            assert tr_fillers
            _emit_tr(*tr_fillers.pop(0))
        return xn_edges[g]

    def emit_proj(g, g_done, with_bound=False):
        nonlocal pe_k2_prev
        if with_bound and g >= 2:
            emit_bound(g // 2)
        for c in range(NC8):
            E("vector", (lambda c, g: lambda e: e.tensor_copy(
                xnT_own[:, M * c + 128 * g:M * c + 128 * (g + 1)],
                xnT_g[:, 512 * c:512 * c + 128]))(c, g),
              wait=(g_done if c == 0 else []), inc=[("ve", 1)])
        ve_own = rec.v("ve")
        kt_banks = [7, 6]
        for dh in range(2):
            pb = kt_banks[dh]
            for c in range(NC8):
                E("tensor", (lambda dh, c, pb: lambda e: e.matmul(
                    pool[pb][:], wk_sb[:, D * c + 128 * dh:D * c + 128 * dh + 128],
                    xnT_g[:, 512 * c:512 * (c + 1)],
                    start=(c == 0), stop=(c == NC8 - 1)))(dh, c, pb),
                  wait=(g_done + [pool_free[pb]] if c == 0 else []),
                  inc=([("pe", 1)] if c == NC8 - 1 else []))
            pe_kt = rec.v("pe")
            if dh == 0:
                E("vector", (lambda dh, g, pb: lambda e: e.tensor_copy(
                    KT[:, KEYS * dh + 512 * g:KEYS * dh + 512 * (g + 1)], pool[pb][:]))(dh, g, pb),
                  wait=[("pe", pe_kt)], inc=[("ve", 1)])
                pool_free[pb] = ("ve", rec.v("ve"))
            else:
                E("scalar", (lambda dh, g, pb: lambda e: e.copy(
                    KT[:, KEYS * dh + 512 * g:KEYS * dh + 512 * (g + 1)], pool[pb][:]))(dh, g, pb),
                  wait=[("pe", pe_kt)], inc=[("ac", 1)])
                pool_free[pb] = ("ac", rec.v("ac"))
        ve_kt = rec.v("ve")
        ac_kt = rec.v("ac")
        E("scalar", (lambda g: lambda e: e.activation(
            kt2_sb[:, 0:512], KT[:, 512 * g:512 * (g + 1)], AF.Square))(g),
          wait=[("ve", ve_kt), ("pe", pe_k2_prev)], inc=[("ac", 1)])
        E("scalar", (lambda g: lambda e: e.activation(
            kt2_sb[:, 512:1024], KT[:, KEYS + 512 * g:KEYS + 512 * (g + 1)], AF.Square))(g),
          wait=[("ac", ac_kt)], inc=[("ac", 1)])
        ac_kt2 = rec.v("ac")
        for dh in range(2):
            pb = 6 + dh
            for c in range(NC8):
                E("tensor", (lambda dh, c, g, pb: lambda e: e.matmul(
                    pool[pb][:, 0:128], wq_sb[:, D * c + 128 * dh:D * c + 128 * dh + 128],
                    xnT_own[:, M * c + 128 * g:M * c + 128 * (g + 1)],
                    start=(c == 0), stop=(c == NC8 - 1)))(dh, c, g, pb),
                  wait=([("ve", ve_own), pool_free[pb]] if c == 0 else []),
                  inc=([("pe", 1)] if c == NC8 - 1 else []))
            pe_qt = rec.v("pe")
            if dh == 0:
                E("vector", (lambda dh, g, pb: lambda e: e.tensor_copy(
                    QT[:, M * dh + 128 * g:M * dh + 128 * (g + 1)], pool[pb][:, 0:128]))(dh, g, pb),
                  wait=[("pe", pe_qt)], inc=[("ve", 1)])
                pool_free[pb] = ("ve", rec.v("ve"))
            else:
                E("scalar", (lambda dh, g, pb: lambda e: e.copy(
                    QT[:, M * dh + 128 * g:M * dh + 128 * (g + 1)], pool[pb][:, 0:128]))(dh, g, pb),
                  wait=[("pe", pe_qt)], inc=[("ac", 1)])
                pool_free[pb] = ("ac", rec.v("ac"))
        qt_ve, qt_ac = rec.v("ve"), rec.v("ac")
        if with_bound and g < 2:
            emit_bound(g // 2)
        for ts in range(4):
            pb = kt_banks[ts % 2]
            for c in range(NC8):
                E("tensor", (lambda ts, c, pb: lambda e: e.matmul(
                    pool[pb][:, 0:D], xnT_g[:, 512 * c + 128 * ts:512 * c + 128 * (ts + 1)],
                    wvd_sb[:, D * c:D * (c + 1)],
                    start=(c == 0), stop=(c == NC8 - 1)))(ts, c, pb),
                  wait=([pool_free[pb]] if c == 0 else []),
                  inc=([("pe", 1)] if c == NC8 - 1 else []))
            pe_vd = rec.v("pe")
            if ts % 2 == 0:
                E("vector", (lambda g, ts, pb: lambda e: e.tensor_copy(
                    Vd[:, (4 * g + ts) * D:(4 * g + ts + 1) * D], pool[pb][:, 0:D]))(g, ts, pb),
                  wait=[("pe", pe_vd)], inc=[("ve", 1)])
                pool_free[pb] = ("ve", rec.v("ve"))
            else:
                E("scalar", (lambda g, ts, pb: lambda e: e.copy(
                    Vd[:, (4 * g + ts) * D:(4 * g + ts + 1) * D], pool[pb][:, 0:D]))(g, ts, pb),
                  wait=[("pe", pe_vd)], inc=[("ac", 1)])
                pool_free[pb] = ("ac", rec.v("ac"))
        if with_bound and g >= 2:
            emit_q2_update(g // 2, qt_ve, qt_ac)
        for dh in range(2):
            E("tensor", (lambda dh: lambda e: e.matmul(
                pool[7][0:1, 0:512], ones_bf[:], kt2_sb[:, 512 * dh:512 * (dh + 1)],
                start=(dh == 0), stop=(dh == 1)))(dh),
              wait=([pool_free[7], ("ac", ac_kt2), ("ve", ve_ones)] if dh == 0 else []),
              inc=([("pe", 1)] if dh == 1 else []))
        pe_k2 = rec.v("pe")
        pe_k2_prev = pe_k2
        E("vector", lambda e: e.reduce_max(mk2_bf[0:1, 1:2], pool[7][0:1, 0:512], AX.X),
          wait=[("pe", pe_k2), ("ve", mk2_state["ve"])], inc=[("ve", 1)])
        v_k2r = rec.v("ve")
        pool_free[7] = ("ve", v_k2r)
        VF()
        E("vector", lambda e: e.tensor_max(mk2_bf[0:1, 0:1], mk2_bf[0:1, 0:1], mk2_bf[0:1, 1:2]),
          wait=[("ve", v_k2r)], inc=[("ve", 1)])
        mk2_state["ve"] = rec.v("ve")
        proj_state["ve_own"] = ve_own
        if with_bound:
            emit_bound_bcast()
        return (rec.v("ve"), rec.v("ac"))

    def emit_logits(g):
        ve_own = proj_state["ve_own"]
        for c in range(NC8):
            E("tensor", (lambda c, g: lambda e: e.matmul(
                pool[6][:, 0:NE], xnT_own[:, M * c + 128 * g:M * c + 128 * (g + 1)],
                rwT_sb[:, NE * c:NE * (c + 1)],
                start=(c == 0), stop=(c == NC8 - 1)))(c, g),
              wait=([("ve", ve_own), pool_free[6]] if c == 0 else []),
              inc=([("pe", 1)] if c == NC8 - 1 else []))
        pe_lg = rec.v("pe")
        LG = lg_sb[:, 0:NE]
        EE = lg_sb[:, NE:2 * NE]
        EQ1 = lg_sb[:, 2 * NE:3 * NE]
        PM = lg_sb[:, 3 * NE:4 * NE]
        c_m1 = lg_sb[:, 4 * NE + 2:4 * NE + 3]
        c_m2 = lg_sb[:, 4 * NE + 3:4 * NE + 4]
        c_ws = lg_sb[:, 4 * NE + 4:4 * NE + 5]
        c_wi = lg_sb[:, 4 * NE + 5:4 * NE + 6]
        c_nm = lg_sb[:, 4 * NE + 6:4 * NE + 7]
        E("vector", lambda e: e.tensor_copy(LG, pool[6][:, 0:NE]),
          wait=[("pe", pe_lg)], inc=[("ve", 1)])
        v_lgc = rec.v("ve")
        pool_free[6] = ("ve", v_lgc)
        VF()
        E("vector", lambda e: e.reduce_max(c_nm, LG, AX.X, negate=True),
          wait=[("ve", v_lgc)], inc=[("ve", 1)])
        ve_lg = rec.v("ve")
        E("scalar", lambda e: e.activation(EE, LG, AF.Exp, bias=c_nm), wait=[("ve", ve_lg)],
          inc=[("ac", 1)])
        ac_e = rec.v("ac")
        E("vector", lambda e: e.reduce_max(c_m1, EE, AX.X), wait=[("ac", ac_e)], inc=[("ve", 1)])
        v_m1 = rec.v("ve")
        VF()
        E("vector", lambda e: e.tensor_scalar(EQ1, EE, c_m1, None, ALU.is_equal),
          wait=[("ve", v_m1)], inc=[("ve", 1)])
        v_eq1 = rec.v("ve")
        VF()
        E("vector", lambda e: e.scalar_tensor_tensor(PM, EQ1, -1e30, EE, ALU.mult, ALU.add),
          wait=[("ve", v_eq1)], inc=[("ve", 1)])
        v_pm = rec.v("ve")
        VF()
        E("vector", lambda e: e.reduce_max(c_m2, PM, AX.X), wait=[("ve", v_pm)], inc=[("ve", 1)])
        v_m2 = rec.v("ve")
        VF()
        E("vector", lambda e: e.tensor_scalar(PM, PM, c_m2, None, ALU.is_equal),
          wait=[("ve", v_m2)], inc=[("ve", 1)])
        v_pm2 = rec.v("ve")
        E("vector", lambda e: e.tensor_add(c_ws, c_m1, c_m2), inc=[("ve", 1)])
        v_ws = rec.v("ve")
        E("vector", lambda e: e.tensor_add(EQ1, EQ1, PM), wait=[("ve", v_pm2)], inc=[("ve", 1)])
        v_eq = rec.v("ve")
        E("vector", lambda e: e.reciprocal(c_wi, c_ws), wait=[("ve", v_ws)], inc=[("ve", 1)])
        v_wi = rec.v("ve")
        E("vector", lambda e: e.tensor_mul(EE, EE, EQ1), wait=[("ve", v_eq)], inc=[("ve", 1)])
        v_ee = rec.v("ve")
        E("vector", (lambda g: lambda e: e.tensor_copy(
            mask_f32[:, NE * g:NE * (g + 1)], EQ1))(g),
          wait=[("ve", v_eq)], inc=[("ve", 1)])
        VF()
        E("vector", (lambda g: lambda e: e.tensor_scalar(
            comb[:, NE * g:NE * (g + 1)], EE, c_wi, None, ALU.mult))(g),
          wait=[("ve", v_ee), ("ve", v_wi)], inc=[("ve", 1)])

    proj_state = {}

    bound_state = {}

    def emit_bound(p):
        nonlocal pe_q2_prev, nm_prev
        ve_pre = rec.v("ve")
        ac_pre = rec.v("ac")
        if p == 0:
            emit_q2_update(p, ve_pre, ac_pre)
            E("vector", lambda e: e.tensor_copy(bnd[0:1, 0:1], mk2_bf[0:1, 3:4]),
              wait=[("ve", mq2_state["ve"])], inc=[("ve", 1)])
            v_q2m = rec.v("ve")
            VF()
        else:
            E("vector", lambda e: e.tensor_copy(bnd[0:1, 0:1], mk2_bf[0:1, 3:4]),
              wait=[("ve", mq2_state["ve"]), ("ac", ac_bnd_prev[0])], inc=[("ve", 1)])
            v_q2m = rec.v("ve")
            VF()
        E("scalar", lambda e: e.activation(bnd[0:1, 1:2], bnd[0:1, 0:1], AF.Ln),
          wait=[("ve", v_q2m)], inc=[("ac", 1)])
        E("scalar", lambda e: e.activation(bnd[0:1, 2:3], mk2_bf[0:1, 0:1], AF.Ln),
          wait=[("ve", mk2_state["ve"] if p == 0 else mk2_prev[0])], inc=[("ac", 1)])
        ac_ln2 = rec.v("ac")
        E("vector", lambda e: e.tensor_add(bnd[0:1, 3:4], bnd[0:1, 1:2], bnd[0:1, 2:3]),
          wait=[("ac", ac_ln2)], inc=[("ve", 1)])
        v_sum = rec.v("ve")
        E("scalar", lambda e: e.activation(bnd[0:1, 4:5], bnd[0:1, 3:4], AF.Exp, scale=0.5),
          wait=[("ve", v_sum)], inc=[("ac", 1)])
        ac_m = rec.v("ac")
        ac_bnd_prev[0] = ac_m
        scale = -1.01 / 16.0 if p == 0 else -1.30 / 16.0
        E("vector", (lambda scale: lambda e: e.tensor_scalar(
            bnd[0:1, 5:6], bnd[0:1, 4:5], scale, None, ALU.mult))(scale),
          wait=[("ac", ac_m)], inc=[("ve", 1)])
        v_nm = rec.v("ve")
        VF()
        E("vector", lambda e: e.tensor_copy(nm_bf[0:1, 0:1], bnd[0:1, 5:6]),
          wait=[("ve", v_nm), ("ve", nm_prev)], inc=[("ve", 1)])
        v_nmc = rec.v("ve")
        nm_prev = v_nmc
        bound_state["v_nmc"] = v_nmc

    def emit_bound_bcast():
        v_nmc = bound_state["v_nmc"]
        E("tensor", lambda e: e.matmul(
            pool[3][:, 0:1], ones_row[0:1, :], nm_bf[0:1, 0:1],
            start=True, stop=True),
          wait=[pool_free[3], ("ve", v_nmc), ("ve", ve_ones)], inc=[("pe", 1)])
        pe_bc = rec.v("pe")
        E("vector", lambda e: e.tensor_copy(negm_sb[:], pool[3][:, 0:1]),
          wait=[("pe", pe_bc), ("ac", ac_exp_prev)], inc=[("ve", 1)])
        bound_state["ve_negm"] = rec.v("ve")
        pool_free[3] = ("ve", bound_state["ve_negm"])

    def emit_q2_update(p, ve_pre, ac_pre):
        nonlocal pe_q2_prev
        E("scalar", (lambda p: lambda e: e.activation(
            qt2_sb[:, 0:256], QT[:, 256 * p:256 * (p + 1)], AF.Square))(p),
          wait=[("ve", ve_pre), ("ac", ac_pre), ("pe", pe_q2_prev)], inc=[("ac", 1)])
        E("scalar", (lambda p: lambda e: e.activation(
            qt2_sb[:, 256:512], QT[:, M + 256 * p:M + 256 * (p + 1)], AF.Square))(p),
          inc=[("ac", 1)])
        ac_qt2 = rec.v("ac")
        for dh in range(2):
            E("tensor", (lambda dh: lambda e: e.matmul(
                pool[6][0:1, 0:256], ones_bf[:], qt2_sb[:, 256 * dh:256 * (dh + 1)],
                start=(dh == 0), stop=(dh == 1)))(dh),
              wait=([pool_free[6], ("ac", ac_qt2)] if dh == 0 else []),
              inc=([("pe", 1)] if dh == 1 else []))
        pe_q2 = rec.v("pe")
        pe_q2_prev = pe_q2
        E("vector", lambda e: e.reduce_max(mk2_bf[0:1, 2:3], pool[6][0:1, 0:256], AX.X),
          wait=[("pe", pe_q2), ("ve", mq2_state["ve"])], inc=[("ve", 1)])
        v_q = rec.v("ve")
        pool_free[6] = ("ve", v_q)
        VF()
        if p == 0:
            E("vector", lambda e: e.tensor_copy(mk2_bf[0:1, 3:4], mk2_bf[0:1, 2:3]),
              wait=[("ve", v_q)], inc=[("ve", 1)])
        else:
            E("vector", lambda e: e.tensor_max(mk2_bf[0:1, 3:4], mk2_bf[0:1, 3:4],
                                               mk2_bf[0:1, 2:3]),
              wait=[("ve", v_q)], inc=[("ve", 1)])
        mq2_state["ve"] = rec.v("ve")

    ST_BANKS = (0, 1, 5)

    def make_attn_pair(p, pre):
        nkb = 8 * p + 8
        ve_pre, ac_pre = pre
        st_of = {}
        diag_ac = [ac_exp_prev, ac_exp_prev]
        state = {"kb": 0, "acs": {}, "primed": False, "pe_ctx": None, "ve_rr": None}

        def emit_st(kb):
            sb = ST_BANKS[kb % 3]
            w = []
            for dh in range(2):
                if dh == 0:
                    w = [pool_free[sb]]
                    if kb == 0:
                        w += [("ve", ve_pre), ("ac", ac_pre)]
                E("tensor", (lambda p, kb, dh, sb: lambda e: e.matmul(
                    pool[sb][:, 0:256],
                    KT[:, KEYS * dh + 128 * kb:KEYS * dh + 128 * (kb + 1)],
                    QT[:, M * dh + 256 * p:M * dh + 256 * (p + 1)],
                    start=(dh == 0), stop=(dh == 1),
                    skip_group_check=True))(p, kb, dh, sb),
                  wait=(w if dh == 0 else []), inc=([("pe", 1)] if dh == 1 else []))
            st_of[kb] = ("pe", rec.v("pe"))

        def emit_exp(kb):
            sb = ST_BANKS[kb % 3]
            pb = kb % 4
            j = kb - 8 * p
            if j >= 0:
                db = kb % 2
                E("vector", (lambda kb, sb, j, db: lambda e: e.tensor_add(
                    STm_sb[db][:], pool[sb][:, 0:256],
                    dmask[:, 256 * j:256 * (j + 1)]))(kb, sb, j, db),
                  wait=[st_of[kb], SW_SMALL, ("ac", diag_ac[db])], inc=[("ve", 1)])
                pool_free[sb] = ("ve", rec.v("ve"))
                E("scalar", (lambda pb, db: lambda e: e.activation(
                    pt_sb[pb][:], STm_sb[db][:], AF.Exp,
                    bias=negm_sb[:], scale=0.0625))(pb, db),
                  wait=[("ve", rec.v("ve")), ptb_free[pb]], inc=[("ac", 1)])
                diag_ac[db] = rec.v("ac")
            else:
                E("scalar", (lambda kb, sb, pb: lambda e: e.activation(
                    pt_sb[pb][:], pool[sb][:, 0:256], AF.Exp,
                    bias=negm_sb[:], scale=0.0625))(kb, sb, pb),
                  wait=[st_of[kb], ("ve", bound_state["ve_negm"]), ptb_free[pb]], inc=[("ac", 1)])
                pool_free[sb] = ("ac", rec.v("ac"))
            return rec.v("ac")

        def emit_rctx(kb, ac_p):
            pb = kb % 4
            E("tensor", (lambda kb, pb, nkb: lambda e: e.matmul(
                pool[4][0:1, 0:256], ones_bf[:], pt_sb[pb][:],
                start=(kb == 0), stop=(kb == nkb - 1),
                skip_group_check=True))(kb, pb, nkb),
              wait=[("ac", ac_p)] + ([pool_free[4]] if kb == 0 else []),
              inc=[])
            for dh in range(2):
                E("tensor", (lambda kb, dh, pb, nkb: lambda e: e.matmul(
                    pool[2 + dh][:, 0:256], Vd[:, D * kb + 128 * dh:D * kb + 128 * dh + 128],
                    pt_sb[pb][:], start=(kb == 0), stop=(kb == nkb - 1),
                    skip_group_check=True))(kb, dh, pb, nkb),
                  wait=([pool_free[2 + dh]] if kb == 0 else []),
                  inc=([("pe", 1)] if dh == 1 else []))
            ptb_free[pb] = ("pe", rec.v("pe"))

        def finish():
            nonlocal ac_exp_prev
            ac_exp_prev = rec.v("ac")
            pe_ctx = rec.v("pe")
            state["pe_ctx"] = pe_ctx
            E("vector", (lambda p: lambda e: e.tensor_copy(
                ctxT[:, 256 * p:256 * (p + 1)], pool[2][:, 0:256]))(p),
              wait=[("pe", pe_ctx)], inc=[("ve", 1)])
            pool_free[2] = ("ve", rec.v("ve"))
            E("scalar", (lambda p: lambda e: e.copy(
                ctxT[:, M + 256 * p:M + 256 * (p + 1)], pool[3][:, 0:256]))(p),
              wait=[("pe", pe_ctx)], inc=[("ac", 1)])
            pool_free[3] = ("ac", rec.v("ac"))
            E("scalar", lambda e: e.copy(r_row[:], pool[4][0:1, 0:256]),
              wait=[("pe", pe_ctx), ("pe", rrow_prev[0])], inc=[("ac", 1)])
            pool_free[4] = ("ac", rec.v("ac"))
            state["ve_rr"] = ("ac", rec.v("ac"))

        def run(upto):
            if not state["primed"]:
                emit_st(0)
                if nkb > 1:
                    emit_st(1)
                state["acs"][0] = emit_exp(0)
                state["primed"] = True
            upto = min(upto, nkb)
            kb = state["kb"]
            while kb < upto:
                if kb + 2 < nkb:
                    emit_st(kb + 2)
                if kb + 1 < nkb:
                    state["acs"][kb + 1] = emit_exp(kb + 1)
                emit_rctx(kb, state["acs"].pop(kb))
                drain_tr(2)
                kb += 1
            state["kb"] = kb
            if kb == nkb and state["pe_ctx"] is None:
                finish()

        def tail():
            rr_edge = state["ve_rr"]
            for half in range(2):
                E("tensor", (lambda half: lambda e: e.matmul(
                    pool[3][:, 0:1], r_row[0:1, 128 * half:128 * (half + 1)], one_f32[:],
                    start=True, stop=True))(half),
                  wait=[rr_edge, pool_free[3]],
                  inc=[("pe", 1)])
                pe_rt = rec.v("pe")
                rrow_prev[0] = pe_rt
                E("vector", (lambda p, half: lambda e: e.tensor_copy(
                    r_all[:, 2 * p + half:2 * p + half + 1], pool[3][:, 0:1]))(p, half),
                  wait=[("pe", pe_rt)], inc=[("ve", 1)])
                pool_free[3] = ("ve", rec.v("ve"))
                ve_r = rec.v("ve")
                VF()
                E("vector", (lambda p, half: lambda e: e.reciprocal(
                    rinv_all[:, 2 * p + half:2 * p + half + 1],
                    r_all[:, 2 * p + half:2 * p + half + 1]))(p, half),
                  wait=[("ve", ve_r)], inc=[("ve", 1)])
                rinv_ve.append(rec.v("ve"))
        return run, tail

    for _wu in range(40):
        E("tensor", (lambda: lambda e: e.matmul(
            pool[6][:, 0:128], ident[:], ident[:], start=True, stop=True))(),
          wait=([("si", 16)] if _wu == 0 else []), inc=([("pe", 1)] if _wu == 39 else []))
    pool_free[6] = ("pe", rec.v("pe"))

    emit_lnstats(0)
    emit_lnfin(0)
    queue_xn(0)
    pair = None
    pair_tail = None
    for g in range(NG):
        g_done = emit_xn(g)
        if g + 1 < NG:
            emit_lnstats(g + 1)
        if pair_tail is not None:
            pair_tail()
            pair_tail = None
        mk2_prev[0] = mk2_state["ve"]
        pre = emit_proj(g, g_done, with_bound=(g % 2 == 1))
        if g % 2 == 0 and pair is not None:
            if g + 1 < NG:
                emit_lnfin(g + 1)
                queue_xn(g + 1)
            run, tl = pair
            run(10 ** 9)
            pair = None
            pair_tail = tl
        elif g % 2 == 1:
            if g + 1 < NG:
                emit_lnfin(g + 1)
                queue_xn(g + 1)
            run, tl = make_attn_pair(g // 2, pre)
            run((8 * (g // 2) + 8) // 2)
            pair = (run, tl)
        else:
            if g + 1 < NG:
                emit_lnfin(g + 1)
                queue_xn(g + 1)
        emit_logits(g)
    if pair is not None:
        run, tl = pair
        run(10 ** 9)
        tl()

    if debug:
        wA = [("ve", rec.v("ve")), ("ac", rec.v("ac")), ("pe", rec.v("pe"))]
        E("gpsimd", lambda g: g.dma_start(out=d_xnT[:], in_=xnT_own[:]), wait=wA, inc=[("so0", 16)])
        E("gpsimd", lambda g: g.dma_start(out=d_KT[:], in_=KT[:]), inc=[("so0", 16)])
        E("gpsimd", lambda g: g.dma_start(out=d_Vd[:], in_=Vd[:]), inc=[("so0", 16)])
        E("gpsimd", lambda g: g.dma_start(out=d_QT[:], in_=QT[:]), inc=[("so0", 16)])
        E("gpsimd", lambda g: g.dma_start(out=d_comb[:], in_=comb[:]), inc=[("so0", 16)])
        E("gpsimd", lambda g: g.dma_start(out=d_r[:], in_=r_all[:]), inc=[("so0", 16)])
        E("gpsimd", lambda g: g.dma_start(out=d_ctxT[:], in_=ctxT[:]), inc=[("so0", 16)])
        E("gpsimd", None, wait=[("so0", rec.v("so0"))])


    ve_A0 = rec.v("ve")

    ve_A = rec.v("ve")
    ac_A = rec.v("ac")
    pe_A = rec.v("pe")

    replay(nc, rec, sems)
    esA.close()

    rec = Rec(rec.cnt)
    E = rec.E
    barA = [("ve", ve_A), ("ac", ac_A), ("pe", pe_A)]

    NWW, CAPW, SPE, NSEC = 2, 384, 768, 6
    W1CH = 16
    W2CH = 16

    esB = ExitStack()
    xn_sel = esB.enter_context(nc.sbuf_tensor("xn_sel", [128, NC8 * SPE], BF16))
    hT = esB.enter_context(nc.sbuf_tensor("hT", [128, NH8 * SPE], BF16))
    y_half = esB.enter_context(nc.sbuf_tensor("y_half", [128, NG * C], BF16))
    y_selE = [esB.enter_context(nc.sbuf_tensor(f"y_selE{i}", [128, NSEC * C], BF16))
              for i in range(2)]
    w1sb = [esB.enter_context(nc.sbuf_tensor(f"w1sb{i}", [128, NC8 * 256], BF16)) for i in range(3)]
    w2sb = [esB.enter_context(nc.sbuf_tensor(f"w2sb{i}", [128, 2 * 512], BF16)) for i in range(4)]
    w2f32 = [esB.enter_context(nc.sbuf_tensor(f"w2f32{i}", [128, 2 * 512], F32)) for i in range(3)]
    iota_t = esB.enter_context(nc.sbuf_tensor("iota_t", [128, 384], F32))
    rel_sb = esB.enter_context(nc.sbuf_tensor("rel_sb", [128, 128], F32))
    onesB = esB.enter_context(nc.sbuf_tensor("onesB", [128, 1], BF16))
    ones1b = esB.enter_context(nc.sbuf_tensor("ones1b", [1, 128], BF16))
    st_ring = esB.enter_context(nc.sbuf_tensor("st_ring", [128, 6 * 128], BF16))
    tfeat = esB.enter_context(nc.sbuf_tensor("tfeat", [128, 2 * NG], BF16))
    tls_sb = [esB.enter_context(nc.sbuf_tensor(f"tls_sb{i}", [128, 12], F32)) for i in range(2)]
    eq_tl = esB.enter_context(nc.sbuf_tensor("eq_tl", [128, 6 * 128], BF16))
    m_col = esB.enter_context(nc.sbuf_tensor("m_col", [128, 1], F32))
    tls_snap = esB.enter_context(nc.sbuf_tensor("tls_snap", [128, 12], F32))
    yps = [esB.enter_context(nc.psum_tensor(f"ypsB{i}", [128, 512], F32)) for i in range(8)]
    es_g = ExitStack()
    xn_tok = es_g.enter_context(nc.sbuf_tensor("xn_tok", [128, NG * C], BF16))
    S8 = es_g.enter_context(nc.sbuf_tensor("S8", [128, 8 * 384], BF16))
    es_r = ExitStack()
    ltri_sb = es_r.enter_context(nc.sbuf_tensor("ltri_sb", [128, 128], BF16))
    gw_sb = es_r.enter_context(nc.sbuf_tensor("gw_sb", [128, 128], BF16))
    mask_bf = es_r.enter_context(nc.sbuf_tensor("mask_bf", [128, 128], BF16))
    offs_sb = es_r.enter_context(nc.sbuf_tensor("offs_sb", [1, 128], F32))
    tot_bfc = es_r.enter_context(nc.sbuf_tensor("tot_bfc", [128, 1], BF16))
    offs_hi = es_r.enter_context(nc.sbuf_tensor("offs_hi", [1, 128], BF16))
    offs_lo = es_r.enter_context(nc.sbuf_tensor("offs_lo", [1, 128], BF16))
    offs_hf = es_r.enter_context(nc.sbuf_tensor("offs_hf", [1, 128], F32))
    if debug:
        d_rel = nc.dram_tensor("d_rel", [128, 128], F32, kind="ExternalOutput")
        d_yh = nc.dram_tensor("d_yh", [128, NG * C], F32, kind="ExternalOutput")
        d_xsel = nc.dram_tensor("d_xsel", [128, NC8 * SPE], F32, kind="ExternalOutput")

    poolB = [list(barA) for _ in range(8)]

    E("gpsimd", lambda g: g.dma_start(out=ltri_sb[:], in_=ltri_in[:]), inc=[("ss", 16)])
    E("gpsimd", lambda g: g.dma_start(out=gw_sb[:], in_=gw_in[:]), inc=[("ss", 16)])
    E("gpsimd", lambda g: g.dma_start(out=iota_t[:], in_=iota_in[:]), inc=[("ss", 16)])
    E("gpsimd", lambda g: g.dma_start(out=tfeat[:], in_=tfeat_in[:]), inc=[("ss", 16)])
    SS_B = ("ss", rec.v("ss"))
    E("sync", lambda e: e.dma_start(out=xn_tok[:, 0:8 * C], in_=xnt_dram[:, 0:8 * C]),
      wait=[("sxt", 16 * NG)], inc=[("swh", 16)])
    E("sync", lambda e: e.dma_start(out=xn_tok[:, 8 * C:NG * C], in_=xnt_dram[:, 8 * C:NG * C]),
      inc=[("swh", 16)])
    SI_XT = ("swh", rec.v("swh"))

    E("vector", lambda e: e.memset(onesB[:], 1.0), wait=barA, inc=[("ve", 1)])
    E("vector", lambda e: e.memset(ones1b[:], 1.0), inc=[("ve", 1)])
    ve_onesB = rec.v("ve")

    E("vector", lambda e: e.tensor_copy(mask_bf[:], mask_f32[:]), inc=[("ve", 1)])
    ve_mbf = rec.v("ve")
    E("tensor", lambda t: t.matmul(yps[0][:, 0:128], ltri_sb[:], mask_bf[:],
                                   start=True, stop=True),
      wait=poolB[0] + [("ve", ve_mbf), SS_B], inc=[("pe", 1)])
    pe_pp = rec.v("pe")
    E("vector", lambda e: e.tensor_copy(rel_sb[:], yps[0][:, 0:128]),
      wait=[("pe", pe_pp)], inc=[("ve", 1)])
    ve_pp = rec.v("ve")
    poolB[0] = [("ve", ve_pp)]
    E("tensor", lambda t: t.matmul(yps[1][:, 0:1], mask_bf[:], onesB[:],
                                   start=True, stop=True),
      wait=poolB[1] + [("ve", ve_onesB), ("ve", ve_mbf)], inc=[("pe", 1)])
    pe_tot = rec.v("pe")
    E("scalar", lambda s: s.copy(tot_bfc[:], yps[1][:, 0:1]),
      wait=[("pe", pe_tot)], inc=[("ac", 1)])
    ac_tot = rec.v("ac")
    poolB[1] = [("ac", ac_tot)]
    E("tensor", lambda t: t.matmul(yps[2][0:1, 0:128], tot_bfc[:], gw_sb[:],
                                   start=True, stop=True),
      wait=poolB[2] + [("ac", ac_tot)], inc=[("pe", 1)])
    pe_of = rec.v("pe")
    E("vector", lambda e: e.tensor_copy(offs_sb[:], yps[2][0:1, 0:128]),
      wait=[("pe", pe_of)], inc=[("ve", 1)])
    ve_of = rec.v("ve")
    poolB[2] = [("ve", ve_of)]
    E("scalar", lambda s: s.copy(offs_hi[:], offs_sb[:]),
      wait=[("ve", ve_of)], inc=[("ac", 1)])
    ac_oh = rec.v("ac")
    E("scalar", lambda s: s.copy(offs_hf[:], offs_hi[:]), wait=[("ac", ac_oh)], inc=[("ac", 1)])
    ac_ohf = rec.v("ac")
    E("vector", lambda v: v.tensor_tensor(offs_sb[:], offs_sb[:], offs_hf[:], ALU.subtract),
      wait=[("ac", ac_ohf)], inc=[("ve", 1)])
    v_lf = rec.v("ve")
    VF()
    E("vector", lambda v: v.tensor_copy(offs_lo[:], offs_sb[:]), wait=[("ve", v_lf)],
      inc=[("ve", 1)])
    ve_ol = rec.v("ve")
    for part, wsem in ((offs_hi, ("ac", ac_oh)), (offs_lo, ("ve", ve_ol))):
        E("tensor", (lambda part: lambda t: t.matmul(
            yps[3][:, 0:128], ones1b[:], part[:],
            start=(part is offs_hi), stop=(part is offs_lo)))(part),
          wait=(poolB[3] + [wsem, ("ve", ve_onesB)] if part is offs_hi else [wsem]),
          inc=([("pe", 1)] if part is offs_lo else []))
    pe_ob = rec.v("pe")
    VF()
    E("vector", lambda e: e.tensor_tensor(rel_sb[:], rel_sb[:], yps[3][:, 0:128], ALU.add),
      wait=[("pe", pe_ob), ("ve", ve_pp)], inc=[("ve", 1)])
    v_r1 = rec.v("ve")
    VF()
    E("vector", lambda e: e.tensor_tensor(rel_sb[:], rel_sb[:], mask_f32[:], ALU.mult),
      wait=[("ve", v_r1)], inc=[("ve", 1)])
    v_r2 = rec.v("ve")
    VF()
    E("vector", lambda e: e.tensor_scalar(rel_sb[:], rel_sb[:], 1.0, None, ALU.subtract),
      wait=[("ve", v_r2)], inc=[("ve", 1)])
    ve_rel = rec.v("ve")
    poolB[3] = [("ve", rec.v("ve"))]
    if debug:
        E("gpsimd", lambda g: g.dma_start(out=d_rel[:], in_=rel_sb[:]),
          wait=[("ve", ve_rel)], inc=[("so0", 16)])

    es_r.close()

    w1_loaded = {}
    w1_free = {}
    w1_ptr = [0]
    W1SEM = ("swa", "swb", "swc")

    def issue_w1():
        idx = w1_ptr[0]
        if idx >= NE * W1CH:
            return
        w1_ptr[0] += 1
        e_, ch = idx // W1CH, idx % W1CH
        buf = idx % 3
        waits = [w1_free[idx - 3]] if idx >= 3 else []
        E("gpsimd", (lambda e_, ch, buf: lambda g: g.dma_start(
            out=w1sb[buf].rearrange("p (c h) -> p c h", c=NC8),
            in_=w1[e_, :, 256 * ch:256 * (ch + 1)].rearrange("(c p) h -> p c h", p=128)))(e_, ch, buf),
            wait=waits, inc=[(W1SEM[buf], 16)])
        w1_loaded[idx] = (W1SEM[buf], rec.v(W1SEM[buf]))

    w2_loaded = {}
    w2_free = {}
    w2sb_free = {}
    w2_cast = {}
    w2_ptr = [0]
    w2_cast_ptr = [0]
    W2SEM = ("swe", "swf", "swg", "swh")

    def issue_w2():
        idx = w2_ptr[0]
        if idx >= NE * 2 * W2CH:
            return
        w2_ptr[0] += 1
        e_ = idx // (2 * W2CH)
        cc = (idx // W2CH) % 2
        ch = idx % W2CH
        buf = idx % 3
        q = ("sync", "scalar", "sync")[buf]
        sem = ("swe", "swf", "swg")[buf]
        waits = [w2_free[idx - 3]] if idx >= 3 else []
        E(q, (lambda e_, cc, ch, buf: lambda g: g.dma_start(
            out=w2f32[buf].rearrange("p (h c) -> p h c", h=2),
            in_=w2[e_, 256 * ch:256 * (ch + 1), 512 * cc:512 * (cc + 1)]
                .rearrange("(h p) c -> p h c", p=128)))(e_, cc, ch, buf),
            wait=waits, inc=[(sem, 16)])
        w2_loaded[idx] = (sem, rec.v(sem))

    def cast_w2_upto(idx):
        while w2_cast_ptr[0] <= idx:
            k = w2_cast_ptr[0]
            w2_cast_ptr[0] += 1
            fb, bb = k % 3, k % 4
            waits = [w2_loaded[k]]
            if k >= 4:
                waits.append(w2sb_free[k - 4])
            if k % 2 == 0:
                E("scalar", (lambda fb, bb: lambda s: s.copy(w2sb[bb][:], w2f32[fb][:]))(fb, bb),
                  wait=waits, inc=[("ac", 1)])
                w2_cast[k] = ("ac", rec.v("ac"))
            else:
                E("vector", (lambda fb, bb: lambda v: v.tensor_copy(w2sb[bb][:], w2f32[fb][:]))(fb, bb),
                  wait=waits, inc=[("ve", 1)])
                w2_cast[k] = ("ve", rec.v("ve"))
            w2_free[k] = w2_cast[k]

    for _ in range(3):
        issue_w1()
        issue_w2()

    s8_use_pe = [0]
    gather_done_ve = {}

    def emit_s8_builds(e_, w):
        for k in range(8):
            g_ = w * 8 + k
            wts = [("ve", ve_rel)] if (k == 0) else []
            if k == 0 and s8_use_pe[0]:
                wts.append(("pe", s8_use_pe[0]))
            E("vector", (lambda k, g_, e_: lambda v: v.tensor_scalar(
                S8[:, 384 * k:384 * (k + 1)], iota_t[:],
                rel_sb[:, 8 * g_ + e_:8 * g_ + e_ + 1], None, ALU.is_equal))(k, g_, e_),
              wait=wts, inc=[("ve", 1)])
        return rec.v("ve")

    def emit_gather_tile(e_, idx, bank, ve_s8):
        w, c = idx // 8, idx % 8
        for k in range(8):
            wts = []
            if k == 0:
                wts = poolB[bank] + [("ve", ve_s8), SI_XT]
            E("tensor", (lambda w, c, k, bank: lambda t: t.matmul(
                yps[bank][:, 0:384],
                xn_tok[:, (w * 8 + k) * C + 128 * c:(w * 8 + k) * C + 128 * (c + 1)],
                S8[:, 384 * k:384 * (k + 1)],
                start=(k == 0), stop=(k == 7), skip_group_check=True))(w, c, k, bank),
              wait=wts, inc=([("pe", 1)] if k == 7 else []))
        pe_g = rec.v("pe")
        s8_use_pe[0] = pe_g
        E("vector", (lambda w, c, bank: lambda v: v.tensor_copy(
            xn_sel[:, c * SPE + w * CAPW:c * SPE + w * CAPW + 384], yps[bank][:, 0:384]))(w, c, bank),
          wait=[("pe", pe_g)], inc=[("ve", 1)])
        poolB[bank] = [("ve", rec.v("ve"))]

    def emit_gather_tls(e_, w, bank):
        buf = e_ % 2
        for sec3 in range(3):
            for k in range(8):
                wts = []
                if k == 0:
                    wts = list(poolB[bank]) + [SS_B]
                E("tensor", (lambda w, k, sec3, bank: lambda t: t.matmul(
                    yps[bank][:, 0:2],
                    S8[:, 384 * k + 128 * sec3:384 * k + 128 * sec3 + 128],
                    tfeat[:, 2 * (w * 8 + k):2 * (w * 8 + k) + 2],
                    start=(k == 0), stop=(k == 7), skip_group_check=True))(w, k, sec3, bank),
                  wait=wts, inc=([("pe", 1)] if k == 7 else []))
            pe_t = rec.v("pe")
            s8_use_pe[0] = pe_t
            E("vector", (lambda w, sec3, bank, buf: lambda v: v.tensor_copy(
                tls_sb[buf][:, (w * 3 + sec3) * 2:(w * 3 + sec3) * 2 + 2],
                yps[bank][:, 0:2]))(w, sec3, bank, buf),
              wait=[("pe", pe_t)], inc=[("ve", 1)])
            poolB[bank] = [("ve", rec.v("ve"))]

    gelu_chunk = {}
    hb_rr = [0]

    def emit_h_chunk(e_, ch):
        w1idx = e_ * W1CH + ch
        for hp2 in range(2):
            for sg in range(2):
                hb = 6 + (hb_rr[0] % 2)
                hb_rr[0] += 1
                for c in range(NC8):
                    wts = []
                    if c == 0:
                        wts = poolB[hb] + [w1_loaded[w1idx]]
                        if ch == 0 and hp2 == 0 and sg == 0:
                            wts.append(("ve", gather_done_ve[e_]))
                    E("tensor", (lambda ch, hp2, sg, c, hb, buf: lambda t: t.matmul(
                        yps[hb][:, 0:384],
                        w1sb[buf][:, 256 * c + 128 * hp2:256 * c + 128 * (hp2 + 1)],
                        xn_sel[:, c * SPE + sg * 384:c * SPE + (sg + 1) * 384],
                        start=(c == 0), stop=(c == NC8 - 1),
                        skip_group_check=True))(ch, hp2, sg, c, hb, w1idx % 3),
                      wait=wts, inc=([("pe", 1)] if c == NC8 - 1 else []))
                pe_h = rec.v("pe")
                H8 = 2 * ch + hp2
                E("scalar", (lambda H8, sg, hb: lambda s: s.activation(
                    hT[:, H8 * SPE + sg * 384:H8 * SPE + (sg + 1) * 384],
                    yps[hb][:, 0:384], AF.Gelu))(H8, sg, hb),
                  wait=[("pe", pe_h)], inc=[("ac", 1)])
                poolB[hb] = [("ac", rec.v("ac"))]
        w1_free[w1idx] = ("pe", rec.v("pe"))
        issue_w1()
        gelu_chunk[(e_, ch)] = rec.v("ac")

    yevac_ve = {}
    yev_sect = {}

    def emit_y_half(e_, cc, ch, shalf):
        w2idx = e_ * 2 * W2CH + cc * W2CH + ch
        if shalf == 0:
            cast_w2_upto(min(w2idx + 1, NE * 2 * W2CH - 1))
        for s6 in range(3 * shalf, 3 * shalf + 3):
            for h8x in range(2):
                H8 = 2 * ch + h8x
                first = (ch == 0 and h8x == 0)
                last = (ch == W2CH - 1 and h8x == 1)
                wts = []
                if h8x == 0 and s6 == 3 * shalf:
                    wts = [w2_cast[w2idx], ("ac", gelu_chunk[(e_, ch)])]
                if first:
                    wts = wts + poolB[s6]
                E("tensor", (lambda s6, H8, h8x, first, last, buf: lambda t: t.matmul(
                    yps[s6][:],
                    hT[:, H8 * SPE + s6 * 128:H8 * SPE + (s6 + 1) * 128],
                    w2sb[buf][:, h8x * 512:(h8x + 1) * 512],
                    start=first, stop=last,
                    skip_group_check=True))(s6, H8, h8x, first, last, w2idx % 4),
                  wait=wts,
                  inc=([("pe", 1)] if (last or (h8x == 1 and s6 == 3 * shalf + 2)) else []))
        if shalf == 1:
            w2sb_free[w2idx] = ("pe", rec.v("pe"))
            issue_w2()

    def emit_y_evacs(e_, cc):
        pe_y = rec.v("pe")
        buf = e_ % 2
        for s6 in range(NSEC):
            E("vector", (lambda s6, cc, buf: lambda v: v.tensor_copy(
                y_selE[buf][:, s6 * C + cc * 512:s6 * C + cc * 512 + 512], yps[s6][:]))(s6, cc, buf),
              wait=[("pe", pe_y)], inc=[("ve", 1)])
            poolB[s6] = [("ve", rec.v("ve"))]
            yev_sect[(e_, cc, s6)] = rec.v("ve")
        if cc == 1:
            yevac_ve[e_] = rec.v("ve")

    ring_free = [None] * 6
    ring_ve = [0] * 6
    eq_state = {"e": None, "ve": 0}
    mcol_reader = [0]

    def emit_eq_tl(se):
        buf = se % 2
        E("vector", (lambda buf: lambda v: v.tensor_copy(tls_snap[:], tls_sb[buf][:]))(buf),
          wait=[("ve", mcol_reader[0])], inc=[("ve", 1)])
        for s6 in range(NSEC):
            E("vector", (lambda s6, buf: lambda v: v.tensor_scalar(
                eq_tl[:, s6 * 128:(s6 + 1) * 128], iota_t[:, 0:128],
                tls_sb[buf][:, s6 * 2:s6 * 2 + 1], None, ALU.is_equal))(s6, buf),
              inc=[("ve", 1)])
        eq_state["e"] = se
        eq_state["ve"] = rec.v("ve")

    def emit_st_one(se, g_, sloc):
        buf = se % 2
        s6 = (g_ // 8) * 3 + sloc
        E("vector", (lambda g_, s6: lambda v: v.tensor_scalar(
            m_col[:], tls_snap[:, s6 * 2 + 1:s6 * 2 + 2], float(g_), None,
            ALU.is_equal))(g_, s6),
          wait=[("ve", max(mcol_reader[0], eq_state["ve"]))], inc=[("ve", 1)])
        VF()
        rslot = (g_ * 3 + sloc) % 6
        wst = [("ve", rec.v("ve"))]
        if ring_free[rslot] is not None:
            wst.append(ring_free[rslot])
        E("vector", (lambda s6, rslot: lambda v: v.tensor_scalar(
            st_ring[:, rslot * 128:(rslot + 1) * 128], eq_tl[:, s6 * 128:(s6 + 1) * 128],
            m_col[:], None, ALU.mult))(s6, rslot),
          wait=wst, inc=[("ve", 1)])
        ring_ve[rslot] = rec.v("ve")
        mcol_reader[0] = rec.v("ve")

    def emit_partial(se, g_, cc, first_of_se, via_scalar=False):
        w = g_ // 8
        buf = se % 2
        for sloc in range(3):
            rslot = (g_ * 3 + sloc) % 6
            wts = []
            if sloc == 0:
                wts = poolB[7] + [("ve", max(ring_ve[(g_ * 3 + s) % 6] for s in range(3)))]
                if first_of_se:
                    wts.append(("ve", yevac_ve[se]))
            E("tensor", (lambda sloc, rslot, w, cc, buf: lambda t: t.matmul(
                yps[7][:],
                st_ring[:, rslot * 128:(rslot + 1) * 128],
                y_selE[buf][:, (w * 3 + sloc) * C + cc * 512:(w * 3 + sloc) * C + cc * 512 + 512],
                start=(sloc == 0), stop=(sloc == 2),
                skip_group_check=True))(sloc, rslot, w, cc, buf),
              wait=wts, inc=([("pe", 1)] if sloc == 2 else []))
        pe_p = rec.v("pe")
        if cc == 1:
            for sloc in range(3):
                ring_free[(g_ * 3 + sloc) % 6] = ("pe", pe_p)
        cmb = comb[:, NE * g_ + se:NE * g_ + se + 1]
        dst = y_half[:, g_ * C + cc * 512:g_ * C + cc * 512 + 512]
        if se == 0:
            E("vector", (lambda dst, cmb: lambda v: v.tensor_scalar(
                dst, yps[7][:], cmb, None, ALU.mult))(dst, cmb),
              wait=[("pe", pe_p)], inc=[("ve", 1)])
        elif via_scalar:
            pb = ptmp[cc]
            E("scalar", (lambda pb, cmb: lambda s: s.mul(pb[:], yps[7][:], cmb))(pb, cmb),
              wait=[("pe", pe_p)] + ptmp_free[cc], inc=[("ac", 1)])
            ac_pm = rec.v("ac")
            E("vector", (lambda dst, pb: lambda v: v.tensor_tensor(
                dst, pb[:], dst, ALU.add))(dst, pb),
              wait=[("ac", ac_pm)], inc=[("ve", 1)])
            ptmp_free[cc] = [("ve", rec.v("ve"))]
            poolB[7] = [("ac", ac_pm)]
            return rec.v("ve")
        else:
            E("vector", (lambda dst, cmb: lambda v: v.scalar_tensor_tensor(
                dst, yps[7][:], cmb, dst, ALU.mult, ALU.add))(dst, cmb),
              wait=[("pe", pe_p)], inc=[("ve", 1)])
        poolB[7] = [("ve", rec.v("ve"))]
        return rec.v("ve")

    stt_done = {}

    xres_loaded = {}
    xres_free = [[], []]
    out_free = [("so0", rec.v("so0")), ("so1", rec.v("so1"))]
    gtmp_free = [[], []]
    stmp_free = [[], []]

    def load_xres(g_):
        buf = g_ % 2
        sem = "sxr0" if buf == 0 else "sxr1"
        E("sync", (lambda g_, buf: lambda e: e.dma_start(
            out=xres[buf][:], in_=x[512 * g_:512 * g_ + 128, :]))(g_, buf),
            wait=xres_free[buf], inc=[(sem, 16)])
        xres_loaded[g_] = (sem, rec.v(sem))

    def emit_combine(g_):
        gtmp = gtmp2[g_ % 2]
        stmp = stmp2[g_ % 2]
        pe_sv = [0, 0]
        for cc in range(2):
            bank = (g_ % 2) * 2 + cc
            for dh in range(2):
                wts = []
                if dh == 0:
                    wts = poolB[bank]
                E("tensor", (lambda g_, cc, dh, bank: lambda t: t.matmul(
                    yps[bank][:],
                    ctxT[:, M * dh + 128 * g_:M * dh + 128 * (g_ + 1)],
                    wvup_sb[:, C * dh + 512 * cc:C * dh + 512 * (cc + 1)],
                    start=(dh == 0), stop=(dh == 1),
                    skip_group_check=True))(g_, cc, dh, bank),
                  wait=wts, inc=([("pe", 1)] if dh == 1 else []))
            pe_sv[cc] = rec.v("pe")
        for cc in range(2):
            E("scalar", (lambda g_, cc, gtmp: lambda s: s.activation(
                gtmp[:, 512 * cc:512 * (cc + 1)],
                y_half[:, g_ * C + cc * 512:g_ * C + cc * 512 + 512], AF.Silu))(g_, cc, gtmp),
              wait=[("ve", stt_done[g_])] + (gtmp_free[g_ % 2] if cc == 0 else []), inc=[("ac", 1)])
        ac_g = rec.v("ac")
        for cc in range(2):
            bank = (g_ % 2) * 2 + cc
            E("vector", (lambda g_, cc, bank, stmp, gtmp: lambda v: v.scalar_tensor_tensor(
                stmp[:, 512 * cc:512 * (cc + 1)], yps[bank][:],
                rinv_all[:, g_:g_ + 1], gtmp[:, 512 * cc:512 * (cc + 1)],
                ALU.mult, ALU.mult))(g_, cc, bank, stmp, gtmp),
              wait=[("pe", pe_sv[cc]), ("ve", rinv_ve[g_]), ("ac", ac_g)]
                   + (stmp_free[g_ % 2] if cc == 0 else []),
              inc=[("ve", 1)])
            poolB[bank] = [("ve", rec.v("ve"))]
        v_t1 = rec.v("ve")
        VF()
        for cc in range(2):
            E("vector", (lambda g_, cc, stmp, gtmp: lambda v: v.tensor_add(
                gtmp[:, 512 * cc:512 * (cc + 1)], stmp[:, 512 * cc:512 * (cc + 1)],
                xres[g_ % 2][:, 512 * cc:512 * (cc + 1)]))(g_, cc, stmp, gtmp),
              wait=[("ve", v_t1), xres_loaded[g_]] if cc == 0 else [],
              inc=[("ve", 1)])
        stmp_free[g_ % 2] = [("ve", rec.v("ve"))]
        ve_out = rec.v("ve")
        xres_free[g_ % 2] = [("ve", ve_out)]
        if g_ + 2 < NG:
            load_xres(g_ + 2)
        sem = "so0" if g_ % 2 == 0 else "so1"
        E("sync", (lambda g_, gtmp: lambda e: e.dma_start(
            out=out[128 * g_:128 * (g_ + 1), :], in_=gtmp[:]))(g_, gtmp),
          wait=[("ve", ve_out)], inc=[(sem, 16)])
        out_free[g_ % 2] = (sem, rec.v(sem))
        gtmp_free[g_ % 2] = [(sem, rec.v(sem))]

    ve_s8 = emit_s8_builds(0, 0)
    for idx in range(16):
        if idx == 8:
            emit_gather_tls(0, 0, 6)
            ve_s8 = emit_s8_builds(0, 1)
        emit_gather_tile(0, idx, 6 + idx % 2, ve_s8)
    emit_gather_tls(0, 1, 6)
    gather_done_ve[0] = rec.v("ve")

    for e_ in range(NE):
        for ch in range(W1CH):
            emit_h_chunk(e_, ch)
            if ch >= 1:
                emit_y_half(e_, 0, ch - 1, 0)
                emit_y_half(e_, 0, ch - 1, 1)
        emit_y_half(e_, 0, W1CH - 1, 0)
        emit_y_half(e_, 0, W1CH - 1, 1)
        emit_y_evacs(e_, 0)
        se = e_ - 1
        ve_s8 = None
        if se >= 0:
            emit_eq_tl(se)
        for j in range(W2CH):
            if e_ + 1 < NE and j in (0, 8):
                ve_s8 = emit_s8_builds(e_ + 1, j // 8)
            if se >= 0:
                emit_st_one(se, j, 0)
                emit_st_one(se, j, 1)
                emit_st_one(se, j, 2)
            emit_y_half(e_, 1, j, 0)
            if se >= 0 and j >= 1:
                emit_partial(se, j - 1, 0, first_of_se=(j == 1))
            emit_y_half(e_, 1, j, 1)
            if e_ + 1 < NE:
                emit_gather_tile(e_ + 1, j, 6, ve_s8)
                if j in (7, 15):
                    emit_gather_tls(e_ + 1, j // 8, 6)
            if se >= 0 and j >= 1:
                emit_partial(se, j - 1, 1, False)
        if e_ + 1 < NE:
            gather_done_ve[e_ + 1] = rec.v("ve")
        emit_y_evacs(e_, 1)
        if se >= 0:
            emit_partial(se, W2CH - 1, 0, first_of_se=False)
            emit_partial(se, W2CH - 1, 1, False)

    es_g.close()
    es_t = ExitStack()
    gtmp2 = [es_t.enter_context(nc.sbuf_tensor(f"gtmpB{i}", [128, C], F32)) for i in range(2)]
    stmp2 = [es_t.enter_context(nc.sbuf_tensor(f"stmpB{i}", [128, C], F32)) for i in range(2)]
    xres = [es_t.enter_context(nc.sbuf_tensor(f"xresB{i}", [128, C], F32)) for i in range(2)]
    ptmp = [es_t.enter_context(nc.sbuf_tensor(f"ptmpB{i}", [128, 512], BF16)) for i in range(2)]
    ptmp_free = [[], []]
    load_xres(0)
    load_xres(1)
    se = NE - 1
    emit_eq_tl(se)
    for sl in range(3):
        emit_st_one(se, 0, sl)
    for g_ in range(NG):
        if g_ + 1 < NG:
            for sl in range(3):
                emit_st_one(se, g_ + 1, sl)
        emit_partial(se, g_, 0, first_of_se=(g_ == 0), via_scalar=True)
        if g_ >= 1:
            emit_combine(g_ - 1)
        stt_done[g_] = emit_partial(se, g_, 1, False, via_scalar=True)
    emit_combine(NG - 1)

    if debug:
        E("gpsimd", lambda g: g.dma_start(out=d_yh[:], in_=y_half[:]),
          wait=[("ve", rec.v("ve"))], inc=[("so0", 16)])
        E("gpsimd", lambda g: g.dma_start(out=d_xsel[:], in_=xn_sel[:]), inc=[("so0", 16)])
    E("sync", None, wait=[("so0", rec.v("so0")), ("so1", rec.v("so1"))])
    replay(nc, rec, sems)
    es_t.close()
    esB.close()
    es0.close()
    return nc


def _perm_and_mask(cm):
    own = [r for r in range(512) if r % 4 == cm]
    rest = [r for r in range(512) if r % 4 != cm]
    perm = np.array(own + rest, dtype=np.int64)
    p = np.arange(128)[:, None]
    orig = perm[None, :]
    dmask = np.where(orig <= 4 * p + cm, 0.0, NEG).astype(np.float32)
    dm = np.zeros((128, 2048), dtype=np.float32)
    for j in range(4):
        dT = dmask[:, 128 * j:128 * (j + 1)].T
        dm[:, 256 * j:256 * j + 128] = dT
        dm[:, 256 * (4 + j) + 128:256 * (4 + j) + 256] = dT
        dm[:, 256 * (4 + j):256 * (4 + j) + 128] = NEG
    return perm, dm


def kernel(L0_attn_inputs, L0_attn_Wq, L0_attn_Wk, L0_attn_Wvdown, L0_attn_Wvup,
           L0_ffn_router_w, L0_ffn_w1, L0_ffn_w2):
    x = np.asarray(L0_attn_inputs, dtype=np.float32)
    wq = np.ascontiguousarray(np.asarray(L0_attn_Wq, dtype=np.float32))
    wk = np.ascontiguousarray(np.asarray(L0_attn_Wk, dtype=np.float32))
    wvd = np.ascontiguousarray(np.asarray(L0_attn_Wvdown, dtype=np.float32))
    wvup = np.ascontiguousarray(np.asarray(L0_attn_Wvup, dtype=np.float32))
    rwT = np.ascontiguousarray(np.asarray(L0_ffn_router_w, dtype=np.float32).T)
    w1 = np.ascontiguousarray(np.asarray(L0_ffn_w1, dtype=np.float32))
    w2 = np.ascontiguousarray(np.asarray(L0_ffn_w2, dtype=np.float32))
    ident = np.eye(128, dtype=np.float32)
    ltri = np.triu(np.ones((128, 128), dtype=np.float32))
    ii = np.arange(128)
    gi, ei = ii // 8, ii % 8
    gw_m = ((ei[:, None] == ei[None, :]) & ((gi[:, None] // 8) == (gi[None, :] // 8))
            & (gi[:, None] < gi[None, :])).astype(np.float32)
    iota = np.tile(np.arange(384, dtype=np.float32), (128, 1))
    tfeat = np.zeros((128, 32), dtype=np.float32)
    tfeat[:, 0::2] = np.arange(128, dtype=np.float32)[:, None]
    tfeat[:, 1::2] = np.arange(16, dtype=np.float32)[None, :]

    nc = build_nc(NG=16, NE=8, H=4096)
    in_maps = []
    for c in range(8):
        b, cm = c // 4, c % 4
        perm, dmask = _perm_and_mask(cm)
        xb = np.ascontiguousarray(
            x[b].reshape(16, 512, 1024)[:, perm, :].reshape(8192, 1024))
        in_maps.append({"x": xb, "wq": wq, "wk": wk, "wvd": wvd, "wvup": wvup,
                        "rwT": rwT, "w1": w1, "w2": w2,
                        "ident_in": ident, "dmask_in": dmask,
                        "ltri_in": ltri, "gw_in": gw_m, "iota_in": iota,
                        "tfeat_in": tfeat})
    res = run_bass_kernel_spmd(nc, in_maps, core_ids=list(range(8)))
    out = np.empty((2, 8192, 1024), dtype=np.float32)
    for c in range(8):
        b, cm = c // 4, c % 4
        out[b, cm::4, :] = res.results[c]["out"]
    return out
